# version 03
# baseline (speedup 1.0000x reference)
import contextlib
import numpy as np
import ml_dtypes

import concourse.bass as bass
import concourse.mybir as mybir
from concourse.bass_utils import run_bass_kernel_spmd

F32 = mybir.dt.float32
BF16 = mybir.dt.bfloat16
U8 = mybir.dt.uint8
AF = mybir.ActivationFunctionType
ALU = mybir.AluOpType
AX = mybir.AxisListType

T = 2048
TPAD = T + 4
NCH = 8
NEG_BIG = -3.4028235e38
RMS_EPS = 1.1920929e-07

ENGINES = ("tensor", "vector", "scalar", "gpsimd", "sync")
DEBUG = False



class Stage:
    __slots__ = ("idx", "deps", "ops", "milestones", "name")

    def __init__(self, idx, deps, name):
        self.idx = idx
        self.deps = deps
        self.name = name
        self.ops = {}
        self.milestones = {}

    def op(self, eng, fn):
        self.ops.setdefault(eng, []).append(("c", fn))
        return self

    def dma(self, eng, fn):
        self.ops.setdefault(eng, []).append(("d", fn))
        return self


class Pipe:
    def __init__(self, nc, es):
        self.nc = nc
        self.sems = {e: es.enter_context(nc.semaphore(f"sm_{e}")) for e in ENGINES}
        self.stages = []

    def stage(self, name="", deps=None):
        st = Stage(len(self.stages), deps, name)
        self.stages.append(st)
        return st

    def finalize(self):
        nc = self.nc
        counts = {e: 0 for e in ENGINES}
        order = []
        last = None
        for st in self.stages:
            if not st.ops:
                continue
            if st.deps is None:
                st.deps = [last] if last is not None else []
            else:
                st.deps = [d for d in st.deps if d is not None]
            for eng, ops in st.ops.items():
                inc = 16 * sum(1 for k, _ in ops if k == "d")
                if any(k == "c" for k, _ in ops):
                    inc += 1
                counts[eng] += inc
                st.milestones[eng] = counts[eng]
            order.append(st)
            last = st
        scripts = {e: [] for e in ENGINES}
        floors = {e: {f: 0 for f in ENGINES} for e in ENGINES}
        for st in order:
            need = {}
            for d in st.deps:
                for eng, v in d.milestones.items():
                    need[eng] = max(need.get(eng, 0), v)
            for eng, ops in st.ops.items():
                for f, v in sorted(need.items()):
                    if floors[eng][f] >= v:
                        continue
                    scripts[eng].append(("w", f, v))
                    floors[eng][f] = v
                lastc = max((i for i, (k, _) in enumerate(ops) if k == "c"),
                            default=-1)
                for i, (kind, fn) in enumerate(ops):
                    inc = 16 if kind == "d" else (1 if i == lastc else 0)
                    scripts[eng].append(("x", fn, inc))
        used = [e for e in ENGINES if scripts[e]]
        sems = self.sems
        with nc.Block() as blk:
            for eng_name in used:
                script = scripts[eng_name]

                def mk(eng_name=eng_name, script=script):
                    def body(eng):
                        mysem = sems[eng_name]
                        for item in script:
                            if item[0] == "w":
                                eng.wait_ge(sems[item[1]], item[2])
                            else:
                                inst = item[1](eng)
                                if item[2]:
                                    inst.then_inc(mysem, item[2])
                    return body

                getattr(blk, eng_name)(mk())
        return counts




def _xpos_tables():
    dh = 64
    half = np.arange(0, dh, 2, dtype=np.float64)
    inv_freq = 1.0 / 10000.0 ** (half / dh)
    t = np.arange(T, dtype=np.float64)
    freqs = t[:, None] * inv_freq[None, :]
    freqs = np.concatenate([freqs, freqs], -1)
    base = (half + 0.4 * dh) / (1.4 * dh)
    power = (t - T // 2) / (256 // 2)
    scale = base[None, :] ** power[:, None]
    scale = np.concatenate([scale, scale], -1)
    cos, sin = np.cos(freqs), np.sin(freqs)
    tabs = []
    for tab in (cos * scale, sin * scale, cos / scale, sin / scale):
        tabs.append(np.tile(np.ascontiguousarray(tab.T), (2, 1)).astype(np.float32))
    return tabs


def _ropeI_tables():
    dh = 64
    inv_freq = 1.0 / 10000.0 ** (np.arange(0, dh, 2, dtype=np.float64) / dh)
    t = np.arange(T, dtype=np.float64)
    freqs = inv_freq[:, None] * t[None, :]
    cos = np.repeat(np.cos(freqs), 2, axis=0)
    sin = np.repeat(np.sin(freqs), 2, axis=0)
    return (np.tile(cos, (2, 1)).astype(np.float32),
            np.tile(sin, (2, 1)).astype(np.float32))


def _perm_interleaved():
    R = np.zeros((128, 128), np.float32)
    for b in range(2):
        for i in range(32):
            e = b * 64 + 2 * i
            R[e, e + 1] = -1.0
            R[e + 1, e] = 1.0
    return np.ascontiguousarray(R.T)


def _perm_half():
    R = np.zeros((128, 128), np.float32)
    for b in range(2):
        o = b * 64
        for i in range(32):
            R[o + i, o + 32 + i] = -1.0
            R[o + 32 + i, o + i] = 1.0
    return np.ascontiguousarray(R.T)


class WImg:
    def __init__(self):
        self.blobs = []
        self.meta = {}
        self.n = 0

    def add(self, name, arr):
        arr = np.asarray(arr, np.float32)
        K, F = arr.shape
        nk = (K + 127) // 128
        off0 = self.n
        fws = []
        blocks = []
        for fb0 in range(0, F, 128):
            fw = min(128, F - fb0)
            blk = np.zeros((128, nk * fw), np.float32)
            for k in range(nk):
                r0, r1 = k * 128, min((k + 1) * 128, K)
                blk[0:r1 - r0, k * fw:k * fw + fw] = arr[r0:r1, fb0:fb0 + fw]
            blocks.append(blk.reshape(-1))
            fws.append(fw)
            self.n += blk.size
        self.blobs.append(np.concatenate(blocks))
        self.meta[name] = (off0, nk, fws, F)

    def cat(self):
        flat = np.concatenate(self.blobs)
        pad = (-flat.size) % 512
        if pad:
            flat = np.concatenate([flat, np.zeros(pad, np.float32)])
        return flat


def host_prep(inputs):
    gi = lambda k: np.asarray(inputs[k], dtype=np.float32)
    w = WImg()
    w.add("l0a_qkv", gi("L0_attn_Wqkv").T)
    w.add("l0a_out", gi("L0_attn_Wout").T)
    W1 = gi("L0_ffn_W1")
    W2 = gi("L0_ffn_W2")
    W1p = np.zeros((6, 1024, 5632), np.float32)
    W1p[:, :, 0:2730] = W1[:, :, 0:2730]
    W1p[:, :, 2816:2816 + 2730] = W1[:, :, 2730:5460]
    W2p = np.zeros((6, 2816, 1024), np.float32)
    W2p[:, 0:2730, :] = W2
    for i in range(6):
        w.add(f"l0f_qkv{i}", gi("L0_ffn_Wqkv")[i])
        w.add(f"l0f_wo{i}", gi("L0_ffn_Wo")[i])
        w.add(f"l0f_w1{i}", W1p[i])
        w.add(f"l0f_w2{i}", W2p[i])
    w.add("l1a_in", gi("L1_attn_W_in"))
    w.add("l1a_B", gi("L1_attn_W_B"))
    w.add("l1a_C", gi("L1_attn_W_C"))
    w.add("l1a_delta", gi("L1_attn_W_delta"))
    w.add("l1a_tau", gi("L1_attn_W_tau"))
    w.add("l1a_out", gi("L1_attn_W_out"))
    for e in range(8):
        w.add(f"l1f_w1_{e}", gi("L1_ffn_W1")[e].T)
        w.add(f"l1f_w2_{e}", gi("L1_ffn_W2")[e].T)
    wsrc = w.cat()

    c1, s1, c2, s2 = _xpos_tables()
    cosI, sinI = _ropeI_tables()
    r = np.arange(128)
    MASKV = -1e30
    ut = np.where(r[None, :] >= r[:, None], 0.0, MASKV).astype(np.float32)
    lt = np.where(r[None, :] <= r[:, None], 0.0, MASKV).astype(np.float32)

    cstf = np.zeros((128, 648), np.float32)
    cstf[:, 0:128] = _perm_interleaved()
    cstf[:, 128:256] = _perm_half()
    cstf[:, 256:384] = ut
    cstf[:, 384:512] = lt
    cstf[:, 512:520] = np.tile(np.arange(8, dtype=np.float32), (128, 1))
    cstf[:, 520:648] = np.eye(128)
    cstb = np.zeros((128, 392), ml_dtypes.bfloat16)
    cstb[:, 0:128] = 1.0
    cstb[:, 128:256] = np.eye(128)
    cstb[:, 256] = 1.0 / 1024.0
    cstb[:, 257] = 1.0
    cstb[:, 258] = -1e9
    cstb[0:64, 259] = 1.0
    cstb[64:128, 260] = 1.0
    cstb[0, 264:264 + 64] = 1.0
    cstb[1, 264 + 64:264 + 128] = 1.0
    vec = {}
    vec["bqkv"] = np.ascontiguousarray(gi("L0_attn_bqkv").reshape(24, 128).T)
    vec["bout"] = np.ascontiguousarray(gi("L0_attn_bout").reshape(8, 128).T)
    vec["gqk_col"] = np.ascontiguousarray(
        np.concatenate([gi("L0_attn_gq").reshape(8, 128).T,
                        gi("L0_attn_gk").reshape(8, 128).T], axis=1))
    vec["gout_row"] = gi("L0_attn_gout")[None, :].copy()
    vec["l0f_qs"] = np.ascontiguousarray(np.tile(gi("L0_ffn_q_scale"), (1, 2)).T)
    vec["l0f_ks"] = np.ascontiguousarray(np.tile(gi("L0_ffn_k_scale"), (1, 2)).T)
    vec["l0f_g"] = gi("L0_ffn_ln_g").copy()
    vec["l0f_b"] = gi("L0_ffn_ln_b").copy()
    b1 = gi("L0_ffn_b1")
    b1p = np.zeros((6, 5632), np.float32)
    b1p[:, 0:2730] = b1[:, 0:2730]
    b1p[:, 2816:2816 + 2730] = b1[:, 2730:5460]
    vec["l0f_b1"] = np.ascontiguousarray(b1p.reshape(6, 44, 128).transpose(0, 2, 1))
    vec["l0f_b2"] = np.ascontiguousarray(gi("L0_ffn_b2").reshape(6, 8, 128).transpose(0, 2, 1))
    vec["l1a_conv"] = np.ascontiguousarray(
        gi("L1_attn_conv_w").reshape(8, 128, 3).transpose(1, 0, 2).reshape(128, 24))
    vec["l1a_db"] = np.ascontiguousarray(gi("L1_attn_delta_base").reshape(8, 128).T)
    vec["l1a_alog"] = np.ascontiguousarray(
        gi("L1_attn_A_log").reshape(8, 128, 16).transpose(1, 0, 2).reshape(128, 128))
    vec["l1f_wr"] = np.ascontiguousarray(gi("L1_ffn_Wr").T.reshape(8, 128, 8).transpose(1, 0, 2).reshape(128, 64))
    vec["l1f_br"] = gi("L1_ffn_br").reshape(8, 1).copy()
    vec["l1f_b1"] = np.ascontiguousarray(gi("L1_ffn_b1").reshape(8, 32, 128).transpose(0, 2, 1))
    vec["l1f_b2"] = gi("L1_ffn_b2").copy()

    cstk = np.zeros((128, 8), np.float32)
    cstk[:, 0] = 1e-5
    cstk[:, 1] = RMS_EPS
    cstk[:, 2] = 1.0
    cstk[:, 3] = -1e-6
    consts = {
        "cstk": cstk,
        "tabs_xpos": np.stack([c1, s1, c2, s2]),
        "tabs_ropeI": np.stack([cosI, sinI]),
        "cstf": cstf,
        "cstb": cstb,
    }
    return w, wsrc, vec, consts


def x_to_fm(xseq):
    out = np.zeros((NCH, 128, TPAD), np.float32)
    out[:, :, 1:T + 1] = np.ascontiguousarray(xseq.T).reshape(NCH, 128, T)
    return out




class Ctx:
    def __init__(self, nc, es, pipe, wmeta):
        self.nc = nc
        self.es = es
        self.pipe = pipe
        self.wmeta = wmeta
        self.d = {}
        self.sb = {}
        self.ps = {}

    def dram(self, name, shape, dtype, kind="Internal"):
        t = self.nc.dram_tensor(name, shape, dtype, kind=kind)
        self.d[name] = t
        return t

    _uid = 0

    def sbuf(self, name, shape, dtype, scope=None):
        holder = scope if scope is not None else self.es
        Ctx._uid += 1
        t = holder.enter_context(
            self.nc.sbuf_tensor(f"sb{Ctx._uid}_{name}", shape, dtype))
        self.sb[name] = t
        return t

    def psum(self, name, shape, dtype):
        t = self.es.enter_context(self.nc.psum_tensor(name, shape, dtype))
        self.ps[name] = t
        return t

    def wimg_ap(self, boff, n):
        if boff < self.wsplit:
            assert boff + n <= self.wsplit, (boff, n)
            return self.d["wimgA"][boff:boff + n]
        return self.d["wimgB"][boff - self.wsplit:boff - self.wsplit + n]

    def load_wblock(self, st, name, fbk, buf):
        off0, nk, fws, F = self.wmeta[name]
        fw = fws[fbk]
        boff = off0 + sum(128 * nk * f for f in fws[:fbk])
        n = 128 * nk * fw
        ap = self.wimg_ap(boff, n).rearrange("(p x) -> p x", p=128)
        st.dma("sync", lambda e, buf=buf, ap=ap, n=n: e.dma_start(
            out=buf[:, 0:n // 128], in_=ap))
        return nk, fw

    _dbgn = 0

    def dbg(self, name, ap_fn, enable=True):
        if not DEBUG or not enable:
            return
        Ctx._dbgn += 1
        ap0 = ap_fn()
        shape = list(ap0.shape)
        t = self.dram(f"dbg_{name}", shape, ap0.dtype, kind="ExternalOutput")
        st = self.pipe.stage(f"dbg_{name}")
        st.dma("sync", lambda e: e.dma_start(out=t[:], in_=ap_fn()))
        st2 = self.pipe.stage(f"dbgw_{name}", deps=[st])
        st2.op("sync", lambda e: e.nop())

    def x_ap(self, seq, c, t0, n, shift=0):
        return self.d["xst"][seq, c, :, t0 + 1 - shift: t0 + 1 - shift + n]

    def x_strip(self, seq, tc, shift=False):
        xst = self.d["xst"]
        if not shift:
            return xst[seq, :, :, 1 + tc * 512: 1 + (tc + 1) * 512].rearrange(
                "c p t -> p c t")
        return None

    def xf(self, c, w=512):
        return self.sb["XF"][:, c * 512: c * 512 + w]




def emit_xf_load(cx, seq, tc, shift=False, name="", xsrc="xst"):
    st = cx.pipe.stage(f"xfL{name}_{tc}")
    x = cx.d[xsrc][seq] if xsrc == "xst" else cx.d[xsrc]
    if not shift:
        st.dma("sync", lambda e: e.dma_start(
            out=cx.sb["XF"][:].rearrange("p (c t) -> p c t", t=512),
            in_=x[:, :, 1 + tc * 512: 1 + (tc + 1) * 512].rearrange("c p t -> p c t")))
    else:
        st.dma("sync", lambda e: e.dma_start(
            out=cx.sb["XF"][:].rearrange("p (c t) -> p c t", t=512)[:, 0:4, :],
            in_=x[0:4, :, 1 + tc * 512: 1 + (tc + 1) * 512].rearrange("c p t -> p c t")))
        st.dma("sync", lambda e: e.dma_start(
            out=cx.sb["XF"][:].rearrange("p (c t) -> p c t", t=512)[:, 4:8, :],
            in_=x[4:8, :, 0 + tc * 512: 0 + (tc + 1) * 512].rearrange("c p t -> p c t")))
    return st




def mm_floop(cx, prefix, wname, fbs, nk, rhs_fn, evac_fn, tcs, kmax=None):
    pipe = cx.pipe
    P = cx.ps
    WS = (cx.sb["WS0"], cx.sb["WS1"])
    nkk = kmax if kmax is not None else nk
    mm_st = {}
    rd_st = {}
    ld_st = {}
    for i, fb in enumerate(fbs):
        buf = WS[i % 2]
        ld = pipe.stage(f"{prefix}L{i}", deps=[mm_st.get(i - 1)] if i >= 1 else None)
        cx.load_wblock(ld, wname, fb, buf)
        ld_st[i] = ld
        deps = [ld]
        if i >= 2 and rd_st.get(i - 2) is not None:
            deps.append(rd_st[i - 2])
        if i == 0:
            deps = None
        mm = pipe.stage(f"{prefix}M{i}", deps=deps)
        if i == 0:
            mm.deps = None
        g = i % 2
        banks = {tc: P[f"P{g * 2 + j}"] for j, tc in enumerate(tcs)}
        for k in range(nkk):
            for j, tc in enumerate(tcs):
                mm.op("tensor", lambda e, k=k, tc=tc, buf=buf, b=banks[tc], nkk=nkk: e.matmul(
                    b[:, 0:512], buf[:, k * 128:(k + 1) * 128],
                    rhs_fn(k, tc), start=(k == 0), stop=(k == nkk - 1)))
        mm_st[i] = mm
        rd_st[i] = evac_fn(i, fb, banks,
                           lambda nm, deps=None: pipe.stage(f"{prefix}{nm}{i}", deps=deps))
    return mm_st[len(fbs) - 1], rd_st[len(fbs) - 1]


def emit_vec_loads(cx):
    pipe = cx.pipe
    st = pipe.stage("vecs")
    for name, sbname in [("cstf", "cstf"), ("cstb", "cstb"), ("cstk", "cstk"),
                         ("bqkv", "bqkv"),
                         ("bout", "bout"), ("gqk_col", "gqk"), ("l0f_qs", "qsf"), ("l0f_ks", "ksf"),
                         ("l1a_conv", "taps"), ("l1a_db", "dbase"),
                         ("l1a_alog", "alog"), ("l1f_wr", "wrf"),
                         ("l1f_br", "brf")]:
        st.dma("sync", lambda e, name=name, sbname=sbname: e.dma_start(
            out=cx.sb[sbname][:], in_=cx.d[name][:]))
    st2p = pipe.stage("vecs2p")
    st2p.dma("sync", lambda e: e.dma_start(out=cx.sb["XF"][0:1, 0:1024],
                                           in_=cx.d["gout_row"][0:1, :]))
    st2 = pipe.stage("vecs2")
    st2.op("vector", lambda e: e.tensor_copy(cx.sb["goutb"][0:1, :],
                                             cx.sb["XF"][0:1, 0:1024]))
    st2.op("gpsimd", lambda e: e.memset(
        cx.sb["XB"][:].rearrange("p (c t) -> p c t", t=2049)[:, :, 0:1], 0))
    st3 = pipe.stage("vecs3")
    st3.op("vector", lambda e: e.memset(cx.sb["bMR2"][0:2, 0:T], -1.0))


def emit_state_init(cx):
    st = cx.pipe.stage("xinit")
    st.dma("gpsimd", lambda e: e.dma_start(out=cx.d["xst"][:], in_=cx.d["xin"][:]))


def emit_prepass(cx, nelem):
    pipe = cx.pipe
    CH = 128 * 4096
    fb = [cx.sb["ppf0"], cx.sb["ppf1"]]
    bb = [cx.sb["WS0"], cx.sb["WS1"]]
    prev = [None, None]
    wsrc = cx.d["wsrc"]
    i = 0
    for seg, seglen in (("wimgA", cx.wsplit), ("wimgB", nelem - cx.wsplit)):
        base = 0 if seg == "wimgA" else cx.wsplit
        wseg = cx.d[seg]
        nsl = (seglen + CH - 1) // CH
        for j in range(nsl):
            o = j * CH
            n = min(CH, seglen - o)
            npart = n // 128
            assert n % 128 == 0
            b = i % 2
            ld = pipe.stage(f"ppL{i}", deps=[prev[b]] if prev[b] else None)
            ld.dma("sync", lambda e, b=b, o=o, n=n, npart=npart, base=base: e.dma_start(
                out=fb[b][:, 0:npart],
                in_=wsrc[base + o: base + o + n].rearrange("(p x) -> p x", p=128)))
            cv = pipe.stage(f"ppC{i}", deps=[ld])
            cv.op("vector", lambda e, b=b, npart=npart: e.tensor_copy(
                bb[b][:, 0:npart], fb[b][:, 0:npart]))
            stg = pipe.stage(f"ppS{i}", deps=[cv])
            stg.dma("gpsimd", lambda e, b=b, o=o, n=n, npart=npart, wseg=wseg: e.dma_start(
                out=wseg[o:o + n].rearrange("(p x) -> p x", p=128),
                in_=bb[b][:, 0:npart]))
            prev[b] = stg
            i += 1


def emit_ln(cx, seq, label, dst_fn, shift=False, gb_idx=None, out_f32_fn=None,
            post_f32=None, xsrc="xst"):
    pipe = cx.pipe
    P4, P5 = cx.ps["P4"], cx.ps["P5"]
    xbt, sqt = cx.sb["xbt"], cx.sb["sqt"]
    scA = cx.sb["scA"]
    cb = cx.sb["cstb"]
    invD = cb[:, 256:257]
    onesb = cb[:, 0:128]
    sMU, sMSQ, sVS, sMR = (cx.sb["sMU"], cx.sb["sMSQ"], cx.sb["sVS"], cx.sb["sMR"])
    bRS, bMR2 = cx.sb["bRS"], cx.sb["bMR2"]

    if gb_idx is not None:
        g0 = pipe.stage(f"{label}gb")
        g0.dma("sync", lambda e: e.dma_start(
            out=cx.sb["XF"][0:1, 0:1024], in_=cx.d["l0f_g"][gb_idx:gb_idx + 1, :]))
        g0.dma("sync", lambda e: e.dma_start(
            out=cx.sb["XF"][1:2, 0:1024], in_=cx.d["l0f_b"][gb_idx:gb_idx + 1, :]))
        g1 = pipe.stage(f"{label}gb2")
        g1.op("vector", lambda e: e.tensor_copy(cx.sb["gb2b"][0:2, :],
                                                cx.sb["XF"][0:2, 0:1024]))

    for tc in range(4):
        ldx = emit_xf_load(cx, seq, tc, shift=shift, name=label, xsrc=xsrc)
        for c in range(NCH):
            s1 = pipe.stage(f"{label}a{tc}_{c}")
            s1.op("vector", lambda e, c=c: e.tensor_copy(xbt[:, 0:512], cx.xf(c)))
            s1.op("scalar", lambda e, c=c: e.activation(sqt[:, 0:512], cx.xf(c), AF.Square))
            s2 = pipe.stage(f"{label}b{tc}_{c}")
            s2.op("tensor", lambda e, c=c: e.matmul(
                P4[0:1, 0:512], invD, xbt[:, 0:512], start=(c == 0), stop=(c == 7)))
            s2.op("tensor", lambda e, c=c: e.matmul(
                P5[0:1, 0:512], invD, sqt[:, 0:512], start=(c == 0), stop=(c == 7)))
        s3 = pipe.stage(f"{label}c{tc}")
        s3.op("vector", lambda e: e.tensor_copy(sMU[0:1, 0:512], P4[0:1, 0:512]))
        s3.op("scalar", lambda e: e.copy(sMSQ[0:1, 0:512], P5[0:1, 0:512]))
        st = pipe.stage(f"{label}v1{tc}")
        st.op("vector", lambda e: e.scalar_tensor_tensor(
            sVS[0:1, 0:512], sMU[0:1, 0:512], -1.0, sMU[0:1, 0:512], ALU.mult, ALU.mult))
        st = pipe.stage(f"{label}v2{tc}")
        st.op("vector", lambda e: e.tensor_tensor(
            sVS[0:1, 0:512], sVS[0:1, 0:512], sMSQ[0:1, 0:512], ALU.add))
        st = pipe.stage(f"{label}v3{tc}")
        st.op("scalar", lambda e: e.activation(sVS[0:1, 0:512], sVS[0:1, 0:512], AF.Ln, bias=1e-5))
        st = pipe.stage(f"{label}v4{tc}")
        st.op("scalar", lambda e: e.activation(sVS[0:1, 0:512], sVS[0:1, 0:512], AF.Exp, scale=-0.5))
        st = pipe.stage(f"{label}v5{tc}")
        st.op("vector", lambda e: e.tensor_tensor(
            sMR[0:1, 0:512], sMU[0:1, 0:512], sVS[0:1, 0:512], ALU.mult))
        st = pipe.stage(f"{label}v6{tc}")
        st.op("vector", lambda e, tc=tc: e.tensor_copy(
            bRS[0:1, tc * 512:(tc + 1) * 512], sVS[0:1, 0:512]))
        st.op("scalar", lambda e, tc=tc: e.copy(
            bMR2[0:1, tc * 512:(tc + 1) * 512], sMR[0:1, 0:512]))

    for tc in range(4):
        ldx = emit_xf_load(cx, seq, tc, shift=shift, name=label + "n", xsrc=xsrc)
        if gb_idx is None:
            bc = pipe.stage(f"{label}d{tc}")
            bc.op("tensor", lambda e, tc=tc: e.matmul(
                P4[:, 0:512], onesb[0:1, :], bRS[0:1, tc * 512:(tc + 1) * 512],
                start=True, stop=True))
            bc.op("tensor", lambda e, tc=tc: e.matmul(
                P5[:, 0:512], onesb[0:1, :], bMR2[0:1, tc * 512:(tc + 1) * 512],
                start=True, stop=True))
            for c in range(NCH):
                n1 = pipe.stage(f"{label}e{tc}_{c}")
                n1.op("vector", lambda e, c=c: e.tensor_tensor(
                    scA[:, 0:512], cx.xf(c), P4[:, 0:512], ALU.mult))
                n2 = pipe.stage(f"{label}f{tc}_{c}")
                n2.op("vector", lambda e, c=c, tc=tc: e.tensor_tensor(
                    dst_fn(c, tc), scA[:, 0:512], P5[:, 0:512], ALU.subtract))
                if out_f32_fn is not None:
                    n3 = pipe.stage(f"{label}g{tc}_{c}")
                    n3.op("vector", lambda e, c=c, tc=tc: e.tensor_tensor(
                        out_f32_fn(c, tc), scA[:, 0:512], P5[:, 0:512], ALU.subtract))
                    if post_f32 is not None:
                        post_f32(c, tc)
        else:
            for c in range(NCH):
                bc = pipe.stage(f"{label}d{tc}_{c}")
                bc.op("tensor", lambda e, c=c, tc=tc: e.matmul(
                    P4[:, 0:512], cx.sb["gb2b"][0:1, c * 128:(c + 1) * 128],
                    bRS[0:1, tc * 512:(tc + 1) * 512], start=True, stop=True))
                bc.op("tensor", lambda e, c=c, tc=tc: e.matmul(
                    P5[:, 0:512], cx.sb["gb2b"][0:2, c * 128:(c + 1) * 128],
                    bMR2[0:2, tc * 512:(tc + 1) * 512], start=True, stop=True))
                n1 = pipe.stage(f"{label}e{tc}_{c}")
                n1.op("vector", lambda e, c=c: e.tensor_tensor(
                    scA[:, 0:512], cx.xf(c), P4[:, 0:512], ALU.mult))
                n2 = pipe.stage(f"{label}f{tc}_{c}")
                n2.op("vector", lambda e, c=c, tc=tc: e.tensor_tensor(
                    dst_fn(c, tc), scA[:, 0:512], P5[:, 0:512], ALU.subtract))


def emit_l0a(cx, seq):
    pipe = cx.pipe
    P = cx.ps
    XB = cx.sb["XB"]
    cb = cx.sb["cstb"]
    cstf = cx.sb["cstf"]
    onesb = cb[:, 0:128]
    idb = cb[:, 128:256]
    invD = cb[:, 256:257]
    WS0, WS1 = cx.sb["WS0"], cx.sb["WS1"]
    scA, scB = cx.sb["scA"], cx.sb["scB"]
    sqt = cx.sb["sqt"]
    sVS, bRS = cx.sb["sVS"], cx.sb["bRS"]

    def xbsl(c, tc, w=512):
        return XB[:, c * 2049 + 1 + tc * 512: c * 2049 + 1 + tc * 512 + w]

    with contextlib.ExitStack() as ph:
        QB = cx.sbuf("QB", [128, NCH * T], BF16, ph)
        KB = cx.sbuf("KB", [128, NCH * T], BF16, ph)
        VB = cx.sbuf("VB", [128, NCH * T], BF16, ph)
        TB2 = cx.sbuf("TB2", [128, 2 * T], BF16, ph)
        vkT = cx.sbuf("vkT", [128, 8 * 66], BF16, ph)
        ttile = cx.sbuf("ttile", [128, 16 * 65 + NCH * 128], BF16, ph)
        vkb = cx.sbuf("vkb", [128, 64], BF16, ph)
        den = scA
        rb16 = cx.sb["xbt"]
        bcb = cx.sb["sqt"]
        PTg = ttile[:, 0:NCH * 128]
        KOFF = 16 * 65

        st = pipe.stage("l0a_tbl")
        st.dma("sync", lambda e: e.dma_start(
            out=cx.sb["XF"][:, 0:T], in_=cx.d["ropeI"][0]))
        st.dma("sync", lambda e: e.dma_start(
            out=cx.sb["XF"][:, T:2 * T], in_=cx.d["ropeI"][1]))
        st2 = pipe.stage("l0a_tbl2")
        st2.op("vector", lambda e: e.tensor_copy(TB2[:, 0:2 * T], cx.sb["XF"][:, 0:4096]))

        emit_ln(cx, seq, f"A{seq}ln", xbsl)

        cx.dbg(f"ln{seq}", lambda: XB[:, 0:NCH * 2049], seq == 0)
        if DEBUG and seq == 0:
            for _fb in (0, 4, 5):
                cx.dbg(f"wimgq{_fb}", (lambda _fb=_fb: cx.wimg_ap(
                    cx.wmeta["l0a_qkv"][0] + _fb * 131072, 131072).rearrange(
                        "(p x) -> p x", p=128)))
        pmm = {}
        prd = {}
        idx = 0
        for half in range(2):
            for fbk in range(24):
                buf = WS0 if idx % 2 == 0 else WS1
                ld = pipe.stage(f"A{seq}qL{half}_{fbk}",
                                deps=[pmm.get(idx - 1)] if idx >= 1 else None)
                cx.load_wblock(ld, "l0a_qkv", fbk, buf)
                mmd = None if idx == 0 else [ld] + ([prd[idx - 2]] if idx >= 2 else [])
                mm = pipe.stage(f"A{seq}qM{half}_{fbk}", deps=mmd)
                g = idx % 2
                for k in range(8):
                    for j in range(2):
                        tc = half * 2 + j
                        mm.op("tensor", lambda e, k=k, j=j, tc=tc, buf=buf, g=g: e.matmul(
                            P[f"P{g * 2 + j}"][:, 0:512], buf[:, k * 128:(k + 1) * 128],
                            xbsl(k, tc), start=(k == 0), stop=(k == 7)))
                ev = pipe.stage(f"A{seq}qE{half}_{fbk}",
                                deps=[mm, prd.get(idx - 1)])
                sect, cc = divmod(fbk, 8)
                dst = (QB, KB, VB)[sect]
                for j in range(2):
                    tc = half * 2 + j
                    ev.op("scalar", lambda e, j=j, tc=tc, cc=cc, dst=dst, fbk=fbk, g=g: e.activation(
                        dst[:, cc * T + tc * 512: cc * T + (tc + 1) * 512],
                        P[f"P{g * 2 + j}"][:, 0:512], AF.Identity,
                        bias=cx.sb["bqkv"][:, fbk:fbk + 1]))
                pmm[idx] = mm
                prd[idx] = ev
                idx += 1

        cx.dbg(f"qraw{seq}", lambda: QB[:, :], seq == 0)
        cx.dbg(f"kraw{seq}", lambda: KB[:, :], seq == 0)
        cx.dbg(f"vraw{seq}", lambda: VB[:, :], seq == 0)
        for idx, SRC in ((0, QB), (1, KB)):
            for tc in range(4):
                for c in range(NCH):
                    s1 = pipe.stage(f"A{seq}s{idx}a{tc}_{c}")
                    s1.op("scalar", lambda e, SRC=SRC, c=c, tc=tc: e.activation(
                        sqt[:, 0:512],
                        SRC[:, c * T + tc * 512: c * T + (tc + 1) * 512], AF.Square))
                    s2 = pipe.stage(f"A{seq}s{idx}b{tc}_{c}")
                    s2.op("tensor", lambda e, c=c: e.matmul(
                        P["P4"][0:1, 0:512], invD, sqt[:, 0:512],
                        start=(c == 0), stop=(c == 7)))
                s3 = pipe.stage(f"A{seq}s{idx}c{tc}")
                s3.op("scalar", lambda e: e.activation(
                    sVS[0:1, 0:512], P["P4"][0:1, 0:512], AF.Ln, bias=RMS_EPS))
                s4 = pipe.stage(f"A{seq}s{idx}d{tc}")
                s4.op("scalar", lambda e: e.activation(
                    sVS[0:1, 0:512], sVS[0:1, 0:512], AF.Exp, scale=-0.5))
                s5 = pipe.stage(f"A{seq}s{idx}e{tc}")
                s5.op("vector", lambda e, tc=tc: e.tensor_copy(
                    bRS[0:1, tc * 512:(tc + 1) * 512], sVS[0:1, 0:512]))
            for c in range(NCH):
                stp = pipe.stage(f"A{seq}s{idx}g{c}")
                stp.op("vector", lambda e, idx=idx, c=c: e.tensor_scalar(
                    PTg[:, c * 128:(c + 1) * 128],
                    cstf[:, 0:128],
                    cx.sb["gqk"][:, idx * NCH + c:idx * NCH + c + 1], None, ALU.mult))
            for c in range(NCH):
                for tc in range(4):
                    sl = slice(c * T + tc * 512, c * T + (tc + 1) * 512)
                    m1 = pipe.stage(f"A{seq}r{idx}a{c}_{tc}")
                    m1.op("tensor", lambda e, SRC=SRC, sl=sl, c=c: e.matmul(
                        P["P4"][:, 0:512],
                        PTg[:, c * 128:(c + 1) * 128],
                        SRC[:, sl], start=True, stop=True))
                    m1.op("vector", lambda e, SRC=SRC, sl=sl, idx=idx, c=c, tc=tc: e.scalar_tensor_tensor(
                        scA[:, 0:512], SRC[:, sl],
                        cx.sb["gqk"][:, idx * NCH + c:idx * NCH + c + 1],
                        TB2[:, tc * 512:(tc + 1) * 512], ALU.mult, ALU.mult))
                    m2 = pipe.stage(f"A{seq}r{idx}b{c}_{tc}")
                    m2.op("vector", lambda e, tc=tc: e.tensor_tensor(
                        scB[:, 0:512], P["P4"][:, 0:512],
                        TB2[:, T + tc * 512: T + (tc + 1) * 512], ALU.mult))
                    m3 = pipe.stage(f"A{seq}r{idx}c{c}_{tc}")
                    m3.op("vector", lambda e: e.tensor_tensor(
                        scA[:, 0:512], scA[:, 0:512], scB[:, 0:512], ALU.add))
                    m4 = pipe.stage(f"A{seq}r{idx}d{c}_{tc}")
                    m4.op("tensor", lambda e, tc=tc: e.matmul(
                        P["P4"][:, 0:512], onesb[0:1, :],
                        bRS[0:1, tc * 512:(tc + 1) * 512], start=True, stop=True))
                    m5 = pipe.stage(f"A{seq}r{idx}e{c}_{tc}")
                    m5.op("vector", lambda e: e.tensor_tensor(
                        scA[:, 0:512], scA[:, 0:512], P["P4"][:, 0:512], ALU.mult))
                    m6 = pipe.stage(f"A{seq}r{idx}f{c}_{tc}")
                    m6.op("vector", lambda e, SRC=SRC, sl=sl: e.tensor_relu(
                        SRC[:, sl], scA[:, 0:512]))

        cx.dbg(f"qf{seq}", lambda: QB[:, :], seq == 0)
        cx.dbg(f"kf{seq}", lambda: KB[:, :], seq == 0)
        stm = pipe.stage(f"A{seq}ones")
        stm.op("gpsimd", lambda e: e.memset(
            ttile[:, 0:16 * 65].rearrange("p (h w) -> p h w", w=65)[:, :, 64:65],
            1.0))
        for tt in range(16):
            trs = pipe.stage(f"A{seq}vT{tt}")
            for c in range(NCH):
                trs.op("tensor", lambda e, c=c, tt=tt: e.transpose(
                    P["PT"][:, c * 128:(c + 1) * 128],
                    KB[:, c * T + tt * 128: c * T + (tt + 1) * 128], idb))
            tre = pipe.stage(f"A{seq}vU{tt}")
            tre.op("vector", lambda e: e.tensor_copy(
                ttile[:, KOFF:KOFF + NCH * 128], P["PT"][:, 0:NCH * 128]))
            trs2 = pipe.stage(f"A{seq}vV{tt}")
            for c in range(NCH):
                trs2.op("tensor", lambda e, c=c, tt=tt: e.transpose(
                    P["PT"][:, c * 128:(c + 1) * 128],
                    VB[:, c * T + tt * 128: c * T + (tt + 1) * 128], idb))
            tre2 = pipe.stage(f"A{seq}vW{tt}")
            for h in range(16):
                hc, hp = divmod(h, 2)
                tre2.op("vector", lambda e, hc=hc, hp=hp, h=h: e.tensor_copy(
                    ttile[:, h * 65:h * 65 + 64],
                    P["PT"][:, hc * 128 + hp * 64: hc * 128 + hp * 64 + 64]))
            mm = pipe.stage(f"A{seq}vM{tt}")
            for h in range(16):
                bank = P["P4"] if h < 8 else P["P5"]
                hh = h % 8
                mm.op("tensor", lambda e, h=h, hh=hh, bank=bank, tt=tt: e.matmul(
                    bank[0:65, hh * 64:(hh + 1) * 64],
                    ttile[:, h * 65:h * 65 + 65],
                    ttile[:, KOFF + h * 64: KOFF + (h + 1) * 64],
                    start=(tt == 0 and hh == 0), stop=(tt == 15 and hh == 7)))

        for h in range(16):
            bank = P["P4"] if h < 8 else P["P5"]
            hh = h % 8
            base = 64 * (h % 2)
            c = h // 2
            ev = pipe.stage(f"A{seq}kE{h}")
            ev.op("scalar", lambda e, bank=bank, hh=hh: e.activation(
                vkb[0:65, 0:64], bank[0:65, hh * 64:(hh + 1) * 64], AF.Identity))
            tr = pipe.stage(f"A{seq}kF{h}")
            tr.op("tensor", lambda e, base=base: e.transpose(
                P["PT"][base:base + 64, 0:65], vkb[0:65, 0:64], idb[0:65, 0:65]))
            ev2 = pipe.stage(f"A{seq}kG{h}")
            ev2.op("vector", lambda e, base=base, c=c: e.tensor_copy(
                vkT[base:base + 64, c * 66:c * 66 + 65],
                P["PT"][base:base + 64, 0:65]))

        cx.dbg(f"vkT{seq}", lambda: vkT[:, :], seq == 0)
        for tc in range(4):
            for h in range(16):
                base = 64 * (h % 2)
                c = h // 2
                qsl = slice(c * T + tc * 512, c * T + (tc + 1) * 512)
                mm = pipe.stage(f"A{seq}xM{tc}_{h}")
                if base == 0:
                    mm.op("tensor", lambda e, c=c, qsl=qsl: e.matmul(
                        P["P6"][0:65, 0:512], vkT[0:64, c * 66:c * 66 + 65],
                        QB[0:64, qsl], start=True, stop=True))
                else:
                    mm.op("tensor", lambda e, c=c, qsl=qsl: e.matmul(
                        P["P6"][64:128, 0:512], vkT[64:128, c * 66:c * 66 + 64],
                        QB[64:128, qsl], start=True, stop=True))
                    mm.op("tensor", lambda e, c=c, qsl=qsl: e.matmul(
                        P["P0"][0:1, 0:512], vkT[64:128, c * 66 + 64:c * 66 + 65],
                        QB[64:128, qsl], start=True, stop=True))
                dsrc = 64 if base == 0 else 0
                r1 = pipe.stage(f"A{seq}xR{tc}_{h}")
                if base == 0:
                    r1.op("vector", lambda e: e.tensor_scalar(
                        den[64:65, 0:512], P["P6"][64:65, 0:512], 1e-6, None, ALU.add))
                else:
                    r1.op("vector", lambda e: e.tensor_scalar(
                        den[0:1, 0:512], P["P0"][0:1, 0:512], 1e-6, None, ALU.add))
                r2 = pipe.stage(f"A{seq}xS{tc}_{h}")
                r2.op("vector", lambda e, dsrc=dsrc: e.reciprocal(
                    den[dsrc:dsrc + 1, 0:512], den[dsrc:dsrc + 1, 0:512]))
                r3 = pipe.stage(f"A{seq}xT{tc}_{h}")
                r3.op("vector", lambda e, dsrc=dsrc: e.tensor_copy(
                    rb16[dsrc:dsrc + 1, 0:512], den[dsrc:dsrc + 1, 0:512]))
                r4 = pipe.stage(f"A{seq}xU{tc}_{h}")
                r4.op("tensor", lambda e, base=base, dsrc=dsrc: e.matmul(
                    P["P5"][base:base + 64, 0:512], onesb[dsrc:dsrc + 1, 0:64],
                    rb16[dsrc:dsrc + 1, 0:512], start=True, stop=True))
                r5 = pipe.stage(f"A{seq}xV{tc}_{h}")
                r5.op("scalar", lambda e, base=base: e.activation(
                    bcb[base:base + 64, 0:512], P["P5"][base:base + 64, 0:512],
                    AF.Identity))
                r6 = pipe.stage(f"A{seq}xW{tc}_{h}")
                nsrc = 0 if base == 0 else 64
                r6.op("vector", lambda e, base=base, c=c, tc=tc, nsrc=nsrc: e.tensor_tensor(
                    XB[base:base + 64,
                       c * 2049 + 1 + tc * 512: c * 2049 + 1 + (tc + 1) * 512],
                    P["P6"][nsrc:nsrc + 64, 0:512], bcb[base:base + 64, 0:512],
                    ALU.mult))

        cx.dbg(f"attn{seq}", lambda: XB[:, 0:NCH * 2049], seq == 0)
        for tc in range(4):
            for fbk in range(8):
                buf = WS0 if fbk % 2 == 0 else WS1
                ld = pipe.stage(f"A{seq}wL{tc}_{fbk}")
                cx.load_wblock(ld, "l0a_out", fbk, buf)
                mm = pipe.stage(f"A{seq}wM{tc}_{fbk}")
                for k in range(8):
                    mm.op("tensor", lambda e, k=k, tc=tc, buf=buf: e.matmul(
                        P["P0"][:, 0:512], buf[:, k * 128:(k + 1) * 128],
                        XB[:, k * 2049 + 1 + tc * 512: k * 2049 + 1 + (tc + 1) * 512],
                        start=(k == 0), stop=(k == 7)))
                ev = pipe.stage(f"A{seq}wE{tc}_{fbk}")
                ev.op("scalar", lambda e, fbk=fbk: e.activation(
                    VB[:, fbk * 512:(fbk + 1) * 512], P["P0"][:, 0:512],
                    AF.Identity, bias=cx.sb["bout"][:, fbk:fbk + 1]))
                ev2 = pipe.stage(f"A{seq}wF{tc}_{fbk}")
                ev2.op("scalar", lambda e, fbk=fbk: e.activation(
                    sqt[:, 0:512], P["P0"][:, 0:512], AF.Square,
                    bias=cx.sb["bout"][:, fbk:fbk + 1]))
                mm2 = pipe.stage(f"A{seq}wS{tc}_{fbk}")
                mm2.op("tensor", lambda e, fbk=fbk: e.matmul(
                    P["P4"][0:1, 0:512], invD, sqt[:, 0:512],
                    start=(fbk == 0), stop=(fbk == 7)))
            s1 = pipe.stage(f"A{seq}o1{tc}")
            s1.op("scalar", lambda e: e.activation(
                sVS[0:1, 0:512], P["P4"][0:1, 0:512], AF.Ln, bias=RMS_EPS))
            s2 = pipe.stage(f"A{seq}o2{tc}")
            s2.op("scalar", lambda e: e.activation(
                sVS[0:1, 0:512], sVS[0:1, 0:512], AF.Exp, scale=-0.5))
            s3 = pipe.stage(f"A{seq}o3{tc}")
            s3.op("vector", lambda e: e.tensor_copy(bRS[0:1, 0:512], sVS[0:1, 0:512]))
            ldx = emit_xf_load(cx, seq, tc, name=f"A{seq}res")
            for c in range(NCH):
                b1 = pipe.stage(f"A{seq}p{tc}_{c}")
                b1.op("tensor", lambda e, c=c: e.matmul(
                    P["P4"][:, 0:512], cx.sb["goutb"][0:1, c * 128:(c + 1) * 128],
                    bRS[0:1, 0:512], start=True, stop=True))
                b2 = pipe.stage(f"A{seq}q{tc}_{c}")
                b2.op("vector", lambda e, c=c: e.tensor_tensor(
                    scA[:, 0:512], VB[:, c * 512:(c + 1) * 512], P["P4"][:, 0:512],
                    ALU.mult))
                b3 = pipe.stage(f"A{seq}r{tc}_{c}")
                b3.op("vector", lambda e, c=c: e.tensor_tensor(
                    scB[:, 0:512], scA[:, 0:512], cx.xf(c), ALU.add))
                b4 = pipe.stage(f"A{seq}s{tc}_{c}")
                b4.dma("sync", lambda e, c=c, tc=tc: e.dma_start(
                    out=cx.x_ap(seq, c, tc * 512, 512), in_=scB[:, 0:512]))




PARTS_ALL = ("l0a", "l0f", "l1a", "l1f")


def build_program(wmeta, n_wsrc, parts=PARTS_ALL, prepass=True):
    nc = bass.Bass("TRN2", target_bir_lowering=False)
    es = contextlib.ExitStack()
    pipe = Pipe(nc, es)
    cx = Ctx(nc, es, pipe, wmeta)

    cx.dram("wsrc", [n_wsrc], F32, kind="ExternalInput")
    wsplit = wmeta["l1a_in"][0]
    assert wsplit % 512 == 0, wsplit
    cx.wsplit = wsplit
    cx.dram("wimgA", [wsplit], BF16, kind="Internal")
    cx.dram("wimgB", [n_wsrc - wsplit], BF16, kind="Internal")
    cx.dram("xin", [2, NCH, 128, TPAD], F32, kind="ExternalInput")
    cx.dram("xst", [2, NCH, 128, TPAD], F32, kind="Internal")
    cx.dram("xst2", [NCH, 128, TPAD], F32, kind="Internal")
    cx.dram("gdram", [NCH, 128, T], BF16, kind="Internal")
    cx.dram("ddram", [NCH, 128, T], BF16, kind="Internal")
    cx.dram("ygdram", [NCH, 128, T], BF16, kind="Internal")
    cx.dram("bcrow", [2, 16, T], BF16, kind="Internal")
    cx.dram("mdram", [8, T], BF16, kind="Internal")
    cx.dram("xout", [2, NCH, 128, T], F32, kind="ExternalOutput")
    cx.dram("ropeI", [2, 128, T], F32, kind="ExternalInput")
    cx.dram("xpos", [4, 128, T], F32, kind="ExternalInput")
    for name, shape in [("cstf", [128, 648]), ("bqkv", [128, 24]),
                        ("bout", [128, 8]), ("gqk_col", [128, 16]),
                        ("gout_row", [1, 1024]),
                        ("l0f_qs", [128, 6]), ("l0f_ks", [128, 6]),
                        ("l0f_g", [6, 1024]), ("l0f_b", [6, 1024]),
                        ("l0f_b1", [6, 128, 44]), ("l0f_b2", [6, 128, 8]),
                        ("l1a_conv", [128, 24]), ("l1a_db", [128, 8]),
                        ("l1a_alog", [128, 128]), ("l1f_wr", [128, 64]),
                        ("l1f_br", [8, 1]), ("l1f_b1", [8, 128, 32]),
                        ("l1f_b2", [8, 1024])]:
        cx.dram(name, shape, F32, kind="ExternalInput")
    cx.dram("cstb", [128, 392], BF16, kind="ExternalInput")
    cx.dram("cstk", [128, 8], F32, kind="ExternalInput")

    cx.sbuf("XB", [128, NCH * 2049], BF16)
    cx.sbuf("XF", [128, 4096], F32)
    cx.sbuf("WS0", [128, 4096], BF16)
    cx.sbuf("WS1", [128, 4096], BF16)
    cx.sbuf("cstf", [128, 648], F32)
    cx.sbuf("cstk", [128, 8], F32)
    cx.sbuf("cstb", [128, 392], BF16)
    cx.sbuf("scA", [128, 512], F32)
    cx.sbuf("scB", [128, 512], F32)
    cx.sbuf("xbt", [128, 512], BF16)
    cx.sbuf("sqt", [128, 512], BF16)
    cx.sbuf("sMU", [1, 512], F32)
    cx.sbuf("sMSQ", [1, 512], F32)
    cx.sbuf("sVS", [1, 512], F32)
    cx.sbuf("sMR", [1, 512], F32)
    cx.sbuf("bRS", [1, T], BF16)
    cx.sbuf("bMR2", [2, T], BF16)
    cx.sbuf("gb2b", [2, 1024], BF16)
    cx.sbuf("bqkv", [128, 24], F32)
    cx.sbuf("bout", [128, 8], F32)
    cx.sbuf("gqk", [128, 16], F32)
    cx.sbuf("goutb", [1, 1024], BF16)
    cx.sbuf("qsf", [128, 6], F32)
    cx.sbuf("ksf", [128, 6], F32)
    cx.sbuf("taps", [128, 24], F32)
    cx.sbuf("dbase", [128, 8], F32)
    cx.sbuf("alog", [128, 128], F32)
    cx.sbuf("wrf", [128, 64], F32)
    cx.sbuf("brf", [8, 1], F32)
    for i in range(7):
        cx.psum(f"P{i}", [128, 512], F32)
    cx.psum("PT", [128, 1024], BF16)
    for j, v in ((0, 1e-5), (1, RMS_EPS), (2, 1.0), (3, -1e-6)):
        nc.const_aps.register(F32, v, cx.sb["cstk"][:, j:j + 1])

    emit_vec_loads(cx)
    emit_state_init(cx)
    if prepass:
        with contextlib.ExitStack() as pps:
            cx.sbuf("ppf0", [128, 4096], F32, pps)
            cx.sbuf("ppf1", [128, 4096], F32, pps)
            emit_prepass(cx, n_wsrc)

    for seq in range(2):
        if "l0a" in parts:
            emit_l0a(cx, seq)
        if "l0f" in parts:
            emit_l0f(cx, seq)
        if "l1a" in parts:
            emit_l1a(cx, seq)
        if "l1f" in parts:
            emit_l1f(cx, seq)

    fin = pipe.stage("xoutc")
    fin.dma("gpsimd", lambda e: e.dma_start(
        out=cx.d["xout"][:], in_=cx.d["xst"][:, :, :, 1:T + 1]))
    done = pipe.stage("done", deps=[fin])
    done.op("sync", lambda e: e.nop())

    pipe.finalize()
    es.close()
    return nc


_UNUSED = 0



def emit_l0f(cx, seq):
    pipe = cx.pipe
    P = cx.ps
    XB = cx.sb["XB"]
    cb = cx.sb["cstb"]
    cstf = cx.sb["cstf"]
    idb = cb[:, 128:256]
    EH = cb[:, 259:261]
    ETsel = cb[0:2, 264:392]
    WS0, WS1 = cx.sb["WS0"], cx.sb["WS1"]
    scA, scB = cx.sb["scA"], cx.sb["scB"]
    sqt = cx.sb["sqt"]
    XF = cx.sb["XF"]
    maskUT = cstf[:, 256:384]
    maskLT = cstf[:, 384:512]

    def xbsl(c, tc, w=512):
        return XB[:, c * 2049 + 1 + tc * 512: c * 2049 + 1 + tc * 512 + w]

    with contextlib.ExitStack() as ph:
        QB = cx.sbuf("QB", [128, NCH * T], BF16, ph)
        KB = cx.sbuf("KB", [128, NCH * T], BF16, ph)
        VB = cx.sbuf("VB", [128, NCH * T], BF16, ph)
        tblC = cx.sbuf("tblC", [128, T], BF16, ph)
        tblS = cx.sbuf("tblS", [128, T], BF16, ph)
        vtm = cx.sbuf("vtm", [128, 3 * 1024], BF16, ph)
        pnT = cx.sbuf("pnT", [128, 384], BF16, ph)
        vb1 = cx.sbuf("vb1", [128, 44], F32, ph)
        vb2 = cx.sbuf("vb2", [128, 8], F32, ph)
        xft = cx.sbuf("xft", [128, 512], F32, ph)
        scS = scA[:, 0:384]
        pn = sqt[:, 0:384]
        mxc = scB[:, 0:2]
        rinvb = cx.sb["xbt"][0:2, 0:512]
        PTg = vtm[:, 0:NCH * 128]
        atm = cx.sb["xbt"][:, 0:512]
        gtm = sqt[:, 0:512]
        gtm2 = scB[:, 0:256].bitcast(BF16)

        def out_dma(c, tc):
            st = pipe.stage(f"G{seq}od{c}_{tc}")
            st.dma("sync", lambda e, c=c, tc=tc: e.dma_start(
                out=cx.d["xst2"][c, :, 1 + tc * 512: 1 + (tc + 1) * 512],
                in_=scB[:, 0:512]))

        emit_ln(cx, seq, f"G{seq}ln", xbsl, out_f32_fn=lambda c, tc: scB[:, 0:512],
                post_f32=out_dma)
        stz = pipe.stage(f"G{seq}zp")
        stz.op("vector", lambda e: e.memset(scA[:, 0:8], 0.0))
        stz2 = pipe.stage(f"G{seq}zp2")
        for c in range(NCH):
            stz2.dma("sync", lambda e, c=c: e.dma_start(
                out=cx.d["xst2"][c, :, 0:1], in_=scA[:, c:c + 1],
                allow_slow_non_contiguous=True))

        cx.dbg(f"x2init{seq}", lambda: cx.d["xst2"][:, :, :], seq == 0)
        for li in range(6):
            for tc in range(4):
                emit_xf_load(cx, seq, tc, shift=True, name=f"G{seq}L{li}c{tc}",
                             xsrc="xst2")
                cp = pipe.stage(f"G{seq}L{li}cc{tc}")
                for c in range(NCH):
                    cp.op("vector", lambda e, c=c, tc=tc: e.tensor_copy(
                        xbsl(c, tc), cx.xf(c)))
            cx.dbg(f"xsh{li}", lambda: XB[:, 0:NCH * 2049], seq == 0 and li == 0)
            pmm = {}
            prd = {}
            idx = 0
            for half in range(2):
                for fbk in range(24):
                    buf = WS0 if idx % 2 == 0 else WS1
                    ld = pipe.stage(f"G{seq}L{li}qL{half}_{fbk}",
                                    deps=[pmm.get(idx - 1)] if idx >= 1 else None)
                    cx.load_wblock(ld, f"l0f_qkv{li}", fbk, buf)
                    mmd = None if idx == 0 else [ld] + ([prd[idx - 2]] if idx >= 2 else [])
                    mm = pipe.stage(f"G{seq}L{li}qM{half}_{fbk}", deps=mmd)
                    g = idx % 2
                    for k in range(8):
                        for j in range(2):
                            tc = half * 2 + j
                            mm.op("tensor", lambda e, k=k, j=j, tc=tc, buf=buf, g=g: e.matmul(
                                P[f"P{g * 2 + j}"][:, 0:512], buf[:, k * 128:(k + 1) * 128],
                                xbsl(k, tc), start=(k == 0), stop=(k == 7)))
                    ev = pipe.stage(f"G{seq}L{li}qE{half}_{fbk}",
                                    deps=[mm, prd.get(idx - 1)])
                    sect, cc = divmod(fbk, 8)
                    dst = (QB, KB, VB)[sect]
                    for j in range(2):
                        tc = half * 2 + j
                        ev.op("scalar", lambda e, j=j, tc=tc, cc=cc, dst=dst, g=g: e.activation(
                            dst[:, cc * T + tc * 512: cc * T + (tc + 1) * 512],
                            P[f"P{g * 2 + j}"][:, 0:512], AF.Identity))
                    pmm[idx] = mm
                    prd[idx] = ev
                    idx += 1

            cx.dbg(f"qr{li}", lambda: QB[:, :], seq == 0 and li == 0)
            cx.dbg(f"vr{li}", lambda: VB[:, :], seq == 0 and li == 0)
            for idx, (SRC, scol) in enumerate(((QB, "qsf"), (KB, "ksf"))):
                tl = pipe.stage(f"G{seq}L{li}t{idx}")
                tl.dma("sync", lambda e, idx=idx: e.dma_start(
                    out=XF[:, 0:T], in_=cx.d["xpos"][2 * idx]))
                tl.dma("sync", lambda e, idx=idx: e.dma_start(
                    out=XF[:, T:2 * T], in_=cx.d["xpos"][2 * idx + 1]))
                tl2 = pipe.stage(f"G{seq}L{li}u{idx}")
                tl2.op("vector", lambda e: e.tensor_copy(tblC[:, 0:T], XF[:, 0:T]))
                tl2.op("scalar", lambda e: e.copy(tblS[:, 0:T], XF[:, T:2 * T]))
                pg = pipe.stage(f"G{seq}L{li}pg{idx}")
                for c in range(NCH):
                    pg.op("vector", lambda e, c=c, scol=scol: e.tensor_scalar(
                        PTg[:, c * 128:(c + 1) * 128], cstf[:, 128:256],
                        cx.sb[scol][:, li:li + 1], None, ALU.mult))
                for c in range(NCH):
                    for tc in range(4):
                        sl = slice(c * T + tc * 512, c * T + (tc + 1) * 512)
                        s1 = pipe.stage(f"G{seq}L{li}n{idx}a{c}_{tc}")
                        s1.op("scalar", lambda e, SRC=SRC, sl=sl: e.activation(
                            sqt[:, 0:512], SRC[:, sl], AF.Square))
                        s2 = pipe.stage(f"G{seq}L{li}n{idx}b{c}_{tc}")
                        s2.op("tensor", lambda e: e.matmul(
                            P["P4"][0:2, 0:512], EH, sqt[:, 0:512],
                            start=True, stop=True))
                        s3 = pipe.stage(f"G{seq}L{li}n{idx}c{c}_{tc}")
                        s3.op("vector", lambda e: e.tensor_scalar(
                            scA[0:2, 0:512], P["P4"][0:2, 0:512], 1e-24, None,
                            ALU.max))
                        s4 = pipe.stage(f"G{seq}L{li}n{idx}d{c}_{tc}")
                        s4.op("scalar", lambda e: e.activation(
                            scA[0:2, 0:512], scA[0:2, 0:512], AF.Ln))
                        s5 = pipe.stage(f"G{seq}L{li}n{idx}e{c}_{tc}")
                        s5.op("scalar", lambda e: e.activation(
                            scA[0:2, 0:512], scA[0:2, 0:512], AF.Exp, scale=-0.5))
                        s6 = pipe.stage(f"G{seq}L{li}n{idx}f{c}_{tc}")
                        s6.op("vector", lambda e: e.tensor_copy(
                            rinvb[0:2, 0:512], scA[0:2, 0:512]))
                        m1 = pipe.stage(f"G{seq}L{li}r{idx}a{c}_{tc}")
                        m1.op("tensor", lambda e, c=c, SRC=SRC, sl=sl: e.matmul(
                            P["P4"][:, 0:512], PTg[:, c * 128:(c + 1) * 128],
                            SRC[:, sl], start=True, stop=True))
                        m1.op("vector", lambda e, SRC=SRC, sl=sl, scol=scol, tc=tc: e.scalar_tensor_tensor(
                            scA[:, 0:512], SRC[:, sl],
                            cx.sb[scol][:, li:li + 1],
                            tblC[:, tc * 512:(tc + 1) * 512], ALU.mult, ALU.mult))
                        m2 = pipe.stage(f"G{seq}L{li}r{idx}b{c}_{tc}")
                        m2.op("vector", lambda e, tc=tc: e.tensor_tensor(
                            scB[:, 0:512], P["P4"][:, 0:512],
                            tblS[:, tc * 512:(tc + 1) * 512], ALU.mult))
                        m3 = pipe.stage(f"G{seq}L{li}r{idx}c{c}_{tc}")
                        m3.op("vector", lambda e: e.tensor_tensor(
                            scA[:, 0:512], scA[:, 0:512], scB[:, 0:512], ALU.add))
                        m4 = pipe.stage(f"G{seq}L{li}r{idx}d{c}_{tc}")
                        m4.op("tensor", lambda e: e.matmul(
                            P["P4"][:, 0:512], ETsel, rinvb[0:2, 0:512],
                            start=True, stop=True))
                        m5 = pipe.stage(f"G{seq}L{li}r{idx}e{c}_{tc}")
                        m5.op("vector", lambda e: e.tensor_tensor(
                            scA[:, 0:512], scA[:, 0:512], P["P4"][:, 0:512], ALU.mult))
                        m6 = pipe.stage(f"G{seq}L{li}r{idx}f{c}_{tc}")
                        m6.op("vector", lambda e, SRC=SRC, sl=sl: e.tensor_copy(
                            SRC[:, sl], scA[:, 0:512]))

            cx.dbg(f"qh{li}", lambda: QB[:, :], seq == 0 and li == 0)
            cx.dbg(f"kh{li}", lambda: KB[:, :], seq == 0 and li == 0)
            for i0 in range(16):
                slot = i0 % 3
                tv = pipe.stage(f"G{seq}L{li}aV{i0}")
                for c in range(NCH):
                    tv.op("tensor", lambda e, c=c, i0=i0: e.transpose(
                        P["PT"][:, c * 128:(c + 1) * 128],
                        VB[:, c * T + i0 * 128: c * T + (i0 + 1) * 128], idb))
                tv2 = pipe.stage(f"G{seq}L{li}aW{i0}")
                tv2.op("vector", lambda e, slot=slot: e.tensor_copy(
                    vtm[:, slot * 1024:(slot + 1) * 1024], P["PT"][:, 0:1024]))
                ncnk = min(3, i0 + 1)
                jlo = i0 - ncnk + 1
                jw = ncnk * 128
                for h in range(16):
                    base = 64 * (h % 2)
                    c = h // 2
                    mm = pipe.stage(f"G{seq}L{li}aM{i0}_{h}")
                    mm.op("tensor", lambda e, c=c, base=base, i0=i0, jlo=jlo, jw=jw: e.matmul(
                        P["P6"][:, 0:jw],
                        QB[base:base + 64, c * T + i0 * 128: c * T + (i0 + 1) * 128],
                        KB[base:base + 64, c * T + jlo * 128: c * T + jlo * 128 + jw],
                        start=True, stop=True))
                    ms = pipe.stage(f"G{seq}L{li}aS{i0}_{h}")
                    for jc in range(ncnk):
                        if jc == 0 and ncnk == 3:
                            msk = maskUT
                        elif jc == ncnk - 1:
                            msk = maskLT
                        else:
                            msk = None
                        if msk is None:
                            ms.op("vector", lambda e, jc=jc: e.tensor_copy(
                                scS[:, jc * 128:(jc + 1) * 128],
                                P["P6"][:, jc * 128:(jc + 1) * 128]))
                        else:
                            ms.op("vector", lambda e, jc=jc, msk=msk: e.tensor_tensor(
                                scS[:, jc * 128:(jc + 1) * 128],
                                P["P6"][:, jc * 128:(jc + 1) * 128], msk, ALU.add))
                    cx.dbg(f"scS{i0}_{h}", lambda: scA[:, 0:384],
                           seq == 0 and li == 0 and i0 <= 1 and h == 0)
                    mr = pipe.stage(f"G{seq}L{li}aR{i0}_{h}")
                    mr.op("vector", lambda e, jw=jw: e.reduce_max(
                        mxc[:, 0:1], scS[:, 0:jw], axis=AX.X, negate=True))
                    mr2 = pipe.stage(f"G{seq}L{li}aR2{i0}_{h}")
                    mr2.op("vector", lambda e: e.tensor_scalar(
                        mxc[:, 1:2], mxc[:, 0:1], 8.0, None, ALU.mult))
                    mex = pipe.stage(f"G{seq}L{li}aE{i0}_{h}")
                    mex.op("scalar", lambda e, jw=jw: e.activation(
                        scS[:, 0:jw], scS[:, 0:jw], AF.Exp, scale=8.0,
                        bias=mxc[:, 1:2], accum_out=mxc[:, 0:1]))
                    cx.dbg(f"pexp{i0}_{h}", lambda: scA[:, 0:384],
                           seq == 0 and li == 0 and i0 <= 1 and h == 0)
                    cx.dbg(f"mxc{i0}_{h}", lambda: scB[:, 0:2],
                           seq == 0 and li == 0 and i0 <= 1 and h == 0)
                    mrc = pipe.stage(f"G{seq}L{li}aC{i0}_{h}")
                    mrc.op("vector", lambda e: e.reciprocal(mxc[:, 0:1], mxc[:, 0:1]))
                    mpn = pipe.stage(f"G{seq}L{li}aP{i0}_{h}")
                    mpn.op("vector", lambda e, jw=jw: e.tensor_scalar(
                        pn[:, 0:jw], scS[:, 0:jw], mxc[:, 0:1], None, ALU.mult))
                    cx.dbg(f"pn{i0}_{h}", lambda: cx.sb["sqt"][:, 0:384],
                           seq == 0 and li == 0 and i0 <= 1 and h == 0)
                    mtr = pipe.stage(f"G{seq}L{li}aT{i0}_{h}")
                    for jc in range(ncnk):
                        mtr.op("tensor", lambda e, jc=jc: e.transpose(
                            P["PT"][:, jc * 128:(jc + 1) * 128],
                            pn[:, jc * 128:(jc + 1) * 128], idb))
                    mte = pipe.stage(f"G{seq}L{li}aU{i0}_{h}")
                    mte.op("vector", lambda e, jw=jw: e.tensor_copy(
                        pnT[:, 0:jw], P["PT"][:, 0:jw]))
                    mav = pipe.stage(f"G{seq}L{li}aX{i0}_{h}")
                    for jc in range(ncnk):
                        jt = jlo + jc
                        mav.op("tensor", lambda e, jc=jc, jt=jt, h=h, base=base, ncnk=ncnk: e.matmul(
                            P["P5"][base:base + 64, 0:128],
                            vtm[:, (jt % 3) * 1024 + h * 64: (jt % 3) * 1024 + h * 64 + 64],
                            pnT[:, jc * 128:(jc + 1) * 128],
                            start=(jc == 0), stop=(jc == ncnk - 1)))
                    mae = pipe.stage(f"G{seq}L{li}aY{i0}_{h}")
                    mae.op("scalar", lambda e, base=base, c=c, i0=i0: e.activation(
                        XB[base:base + 64,
                           c * 2049 + 1 + i0 * 128: c * 2049 + 1 + (i0 + 1) * 128],
                        P["P5"][base:base + 64, 0:128], AF.Identity))

            cx.dbg(f"ao{li}", lambda: XB[:, 0:NCH * 2049], seq == 0 and li == 0)
            pmm = {}
            prd = {}
            idx = 0
            for tc in range(4):
                emit_xf_load(cx, seq, tc, shift=False, name=f"G{seq}L{li}w{tc}",
                             xsrc="xst2")
                xfst = pipe.stages[-1]
                for fbk in range(8):
                    buf = WS0 if idx % 2 == 0 else WS1
                    ld = pipe.stage(f"G{seq}L{li}wL{tc}_{fbk}",
                                    deps=[pmm.get(idx - 1)] if idx >= 1 else None)
                    cx.load_wblock(ld, f"l0f_wo{li}", fbk, buf)
                    mmd = None if idx == 0 else [ld] + ([prd[idx - 2]] if idx >= 2 else [])
                    mm = pipe.stage(f"G{seq}L{li}wM{tc}_{fbk}", deps=mmd)
                    g = idx % 2
                    for k in range(8):
                        mm.op("tensor", lambda e, k=k, tc=tc, buf=buf, g=g: e.matmul(
                            P[f"P{g * 2}"][:, 0:512], buf[:, k * 128:(k + 1) * 128],
                            xbsl(k, tc), start=(k == 0), stop=(k == 7)))
                    ev = pipe.stage(f"G{seq}L{li}wE{tc}_{fbk}",
                                    deps=[mm, prd.get(idx - 1), xfst])
                    ev.op("vector", lambda e, fbk=fbk, g=g: e.tensor_tensor(
                        scB[:, 0:512], P[f"P{g * 2}"][:, 0:512], cx.xf(fbk), ALU.add))
                    ev2 = pipe.stage(f"G{seq}L{li}wF{tc}_{fbk}")
                    ev2.dma("sync", lambda e, fbk=fbk, tc=tc: e.dma_start(
                        out=cx.d["xst2"][fbk, :, 1 + tc * 512: 1 + (tc + 1) * 512],
                        in_=scB[:, 0:512]))
                    pmm[idx] = mm
                    prd[idx] = ev2
                    idx += 1

            cx.dbg(f"x2a{li}", lambda: cx.d["xst2"][:, :, :], seq == 0 and li < 2)
            emit_ln(cx, seq, f"G{seq}L{li}f", xbsl, shift=True, gb_idx=li,
                    xsrc="xst2")
            ldb = pipe.stage(f"G{seq}L{li}b1")
            ldb.dma("sync", lambda e: e.dma_start(out=vb1[:], in_=cx.d["l0f_b1"][li]))
            ldb.dma("sync", lambda e: e.dma_start(out=vb2[:], in_=cx.d["l0f_b2"][li]))
            for half in range(2):
                pmm = {}
                pevg = {}
                peva = {}
                plast = {}
                for j in range(22):
                    lda = pipe.stage(f"G{seq}L{li}1L{half}_{j}",
                                     deps=[pmm.get(j - 1)] if j >= 1 else None)
                    cx.load_wblock(lda, f"l0f_w1{li}", j, WS0)
                    cx.load_wblock(lda, f"l0f_w1{li}", j + 22, WS1)
                    ab = 0 if j % 2 == 0 else 4
                    mmd = None if j == 0 else ([lda] +
                                               ([pevg[j - 1]] if j >= 1 else []) +
                                               ([peva[j - 2]] if j >= 2 else []))
                    mm = pipe.stage(f"G{seq}L{li}1M{half}_{j}", deps=mmd)
                    for k in range(8):
                        for jj in range(2):
                            tc = half * 2 + jj
                            mm.op("tensor", lambda e, k=k, tc=tc, jj=jj, ab=ab: e.matmul(
                                P[f"P{ab + jj}"][:, 0:512], WS0[:, k * 128:(k + 1) * 128],
                                xbsl(k, tc), start=(k == 0), stop=(k == 7)))
                            mm.op("tensor", lambda e, k=k, tc=tc, jj=jj: e.matmul(
                                P[f"P{2 + jj}"][:, 0:512], WS1[:, k * 128:(k + 1) * 128],
                                xbsl(k, tc), start=(k == 0), stop=(k == 7)))
                    evg0 = pipe.stage(f"G{seq}L{li}1G{half}_{j}_0",
                                      deps=[mm, plast.get(j - 1)])
                    evg0.op("scalar", lambda e, j=j: e.activation(
                        gtm[:, 0:512], P["P2"][:, 0:512], AF.Gelu,
                        bias=vb1[:, j + 22:j + 23]))
                    evg1 = pipe.stage(f"G{seq}L{li}1G{half}_{j}_1")
                    evg1.op("scalar", lambda e, j=j: e.activation(
                        gtm2[:, 0:512], P["P3"][:, 0:512], AF.Gelu,
                        bias=vb1[:, j + 22:j + 23]))
                    eva0 = pipe.stage(f"G{seq}L{li}1E{half}_{j}_0")
                    eva0.op("scalar", lambda e, j=j, ab=ab: e.activation(
                        atm[:, 0:512], P[f"P{ab}"][:, 0:512], AF.Identity,
                        bias=vb1[:, j:j + 1]))
                    evm0 = pipe.stage(f"G{seq}L{li}1H{half}_{j}_0")
                    evm0.op("vector", lambda e, j=j: e.tensor_tensor(
                        QB[:, j * 512:(j + 1) * 512], atm[:, 0:512],
                        gtm[:, 0:512], ALU.mult))
                    eva1 = pipe.stage(f"G{seq}L{li}1E{half}_{j}_1")
                    eva1.op("scalar", lambda e, j=j, ab=ab: e.activation(
                        atm[:, 0:512], P[f"P{ab + 1}"][:, 0:512], AF.Identity,
                        bias=vb1[:, j:j + 1]))
                    evm1 = pipe.stage(f"G{seq}L{li}1H{half}_{j}_1")
                    evm1.op("vector", lambda e, j=j: e.tensor_tensor(
                        KB[:, j * 512:(j + 1) * 512], atm[:, 0:512],
                        gtm2[:, 0:512], ALU.mult))
                    pmm[j] = mm
                    pevg[j] = evg1
                    peva[j] = eva1
                    plast[j] = evm1
                cx.dbg(f"g{li}_{half}", lambda: QB[:, 0:22 * 512],
                       seq == 0 and li == 0 and half == 0)
                pmm = {}
                prd = {}
                for fbk in range(8):
                    buf = WS0 if fbk % 2 == 0 else WS1
                    ld = pipe.stage(f"G{seq}L{li}2L{half}_{fbk}",
                                    deps=[pmm.get(fbk - 1)] if fbk >= 1 else None)
                    cx.load_wblock(ld, f"l0f_w2{li}", fbk, buf)
                    mmd = None if fbk == 0 else [ld] + ([prd[fbk - 2]] if fbk >= 2 else [])
                    mm = pipe.stage(f"G{seq}L{li}2M{half}_{fbk}", deps=mmd)
                    g = fbk % 2
                    for k in range(22):
                        for jj, GD in ((0, QB), (1, KB)):
                            mm.op("tensor", lambda e, k=k, jj=jj, GD=GD, buf=buf, g=g: e.matmul(
                                P[f"P{g * 2 + jj}"][:, 0:512], buf[:, k * 128:(k + 1) * 128],
                                GD[:, k * 512:(k + 1) * 512],
                                start=(k == 0), stop=(k == 21)))
                    last = None
                    for jj in range(2):
                        tc = half * 2 + jj
                        lx = pipe.stage(f"G{seq}L{li}2X{half}_{fbk}_{jj}",
                                        deps=[mm, prd.get(fbk - 1)] if jj == 0 else None)
                        lx.dma("sync", lambda e, fbk=fbk, tc=tc: e.dma_start(
                            out=xft[:, 0:512],
                            in_=cx.d["xst2"][fbk, :, 1 + tc * 512: 1 + (tc + 1) * 512]))
                        ev = pipe.stage(f"G{seq}L{li}2E{half}_{fbk}_{jj}")
                        ev.op("scalar", lambda e, fbk=fbk, jj=jj, g=g: e.activation(
                            scA[:, 0:512], P[f"P{g * 2 + jj}"][:, 0:512], AF.Identity,
                            bias=vb2[:, fbk:fbk + 1]))
                        ev2 = pipe.stage(f"G{seq}L{li}2F{half}_{fbk}_{jj}")
                        ev2.op("vector", lambda e: e.tensor_tensor(
                            scB[:, 0:512], scA[:, 0:512], xft[:, 0:512], ALU.add))
                        ev3 = pipe.stage(f"G{seq}L{li}2G{half}_{fbk}_{jj}")
                        ev3.dma("sync", lambda e, fbk=fbk, tc=tc: e.dma_start(
                            out=cx.d["xst2"][fbk, :, 1 + tc * 512: 1 + (tc + 1) * 512],
                            in_=scB[:, 0:512]))
                        last = ev3
                    pmm[fbk] = mm
                    prd[fbk] = last

        cx.dbg(f"x2f{seq}", lambda: cx.d["xst2"][:, :, :], seq == 0)
        for tc in range(4):
            emit_xf_load(cx, seq, tc, shift=False, name=f"G{seq}fr{tc}", xsrc="xst")
            for c in range(NCH):
                l1 = pipe.stage(f"G{seq}fa{tc}_{c}")
                l1.dma("sync", lambda e, c=c, tc=tc: e.dma_start(
                    out=xft[:, 0:512],
                    in_=cx.d["xst2"][c, :, 1 + tc * 512: 1 + (tc + 1) * 512]))
                l2 = pipe.stage(f"G{seq}fb{tc}_{c}")
                l2.op("vector", lambda e, c=c: e.tensor_tensor(
                    scB[:, 0:512], cx.xf(c), xft[:, 0:512], ALU.add))
                l3 = pipe.stage(f"G{seq}fc{tc}_{c}")
                l3.dma("sync", lambda e, c=c, tc=tc: e.dma_start(
                    out=cx.x_ap(seq, c, tc * 512, 512), in_=scB[:, 0:512]))



_UNUSED2 = 0



def emit_l1a(cx, seq):
    pipe = cx.pipe
    P = cx.ps
    XB = cx.sb["XB"]
    cb = cx.sb["cstb"]
    onesb = cb[:, 0:128]
    WS0, WS1 = cx.sb["WS0"], cx.sb["WS1"]
    scA, scB = cx.sb["scA"], cx.sb["scB"]
    xbt, sqt = cx.sb["xbt"], cx.sb["sqt"]
    XF = cx.sb["XF"]

    def xbsl(c, tc, w=512):
        return XB[:, c * 2049 + 1 + tc * 512: c * 2049 + 1 + tc * 512 + w]

    def dlt(c, tc, w=512):
        return XB[:, c * 2049 + 1 + tc * 512: c * 2049 + 1 + tc * 512 + w]

    with contextlib.ExitStack() as ph:
        VR = cx.sbuf("VR", [128, NCH * 2050], BF16, ph)
        BCB = cx.sbuf("BCB", [128, 2 * 16 * 512], BF16, ph)
        YAC = cx.sbuf("YAC", [128, 512], F32, ph)
        dat = cx.sbuf("dat", [128, 4 * 512], F32, ph)
        rt = cx.sbuf("rt", [128, 4 * 512], F32, ph)
        wk = cx.sbuf("wk", [128, 4 * 512], F32, ph)
        et = cx.sbuf("et", [128, 4 * 512], BF16, ph)
        bu = cx.sbuf("bu", [128, 4 * 512], BF16, ph)
        u2 = cx.sbuf("u2", [128, 512], F32, ph)
        hst = cx.sbuf("hst", [128, 8 * 16], F32, ph)
        AT = cx.sbuf("AT", [128, 8 * 16], F32, ph)
        brow = cx.sbuf("brow", [16, 512], BF16, ph)
        gtb = cx.sbuf("gtb", [128, 512], BF16, ph)
        bRS2 = cx.sbuf("bRS2", [1, T], BF16, ph)
        bMR22 = cx.sbuf("bMR22", [1, T], BF16, ph)

        def vr(c, tc, off=0, w=512):
            return VR[:, c * 2050 + 1 + off + tc * 512: c * 2050 + 1 + off + tc * 512 + w]

        emit_ln(cx, seq, f"M{seq}ln", xbsl)
        sv = pipe.stage(f"M{seq}sv")
        sv.op("vector", lambda e: e.tensor_copy(bRS2[0:1, 0:T], cx.sb["bRS"][0:1, 0:T]))
        sv.op("scalar", lambda e: e.copy(bMR22[0:1, 0:T], cx.sb["bMR2"][0:1, 0:T]))
        zp = pipe.stage(f"M{seq}zp")
        zp.op("gpsimd", lambda e: e.memset(
            VR[:].rearrange("p (c t) -> p c t", t=2050)[:, :, 0:1], 0))
        zp.op("vector", lambda e: e.memset(
            VR[:].rearrange("p (c t) -> p c t", t=2050)[:, :, 2049:2050], 0))

        pmm = {}
        prd = {}
        idx = 0
        for half in range(2):
            for fbk in range(16):
                buf = WS0 if idx % 2 == 0 else WS1
                ld = pipe.stage(f"M{seq}iL{half}_{fbk}",
                                deps=[pmm.get(idx - 1)] if idx >= 1 else None)
                cx.load_wblock(ld, "l1a_in", fbk, buf)
                mmd = None if idx == 0 else [ld] + ([prd[idx - 2]] if idx >= 2 else [])
                mm = pipe.stage(f"M{seq}iM{half}_{fbk}", deps=mmd)
                g = idx % 2
                for k in range(8):
                    for j in range(2):
                        tc = half * 2 + j
                        mm.op("tensor", lambda e, k=k, j=j, tc=tc, buf=buf, g=g: e.matmul(
                            P[f"P{g * 2 + j}"][:, 0:512], buf[:, k * 128:(k + 1) * 128],
                            xbsl(k, tc), start=(k == 0), stop=(k == 7)))
                lastst = None
                for j in range(2):
                    tc = half * 2 + j
                    dd = [mm, prd.get(idx - 1)] if j == 0 else None
                    if fbk < 8:
                        ev = pipe.stage(f"M{seq}iE{half}_{fbk}_{j}", deps=dd)
                        ev.op("scalar", lambda e, j=j, fbk=fbk, tc=tc, g=g: e.activation(
                            vr(fbk, tc), P[f"P{g * 2 + j}"][:, 0:512], AF.Identity))
                        lastst = ev
                    else:
                        ev = pipe.stage(f"M{seq}iE{half}_{fbk}_{j}", deps=dd)
                        ev.op("scalar", lambda e, j=j, g=g: e.activation(
                            gtb[:, 0:512], P[f"P{g * 2 + j}"][:, 0:512], AF.Identity))
                        ev2 = pipe.stage(f"M{seq}iF{half}_{fbk}_{j}")
                        ev2.dma("sync", lambda e, fbk=fbk, tc=tc: e.dma_start(
                            out=cx.d["gdram"][fbk - 8, :, tc * 512:(tc + 1) * 512],
                            in_=gtb[:, 0:512]))
                        lastst = ev2
                pmm[idx] = mm
                prd[idx] = lastst
                idx += 1

        for c in range(NCH):
            for tc in range(4):
                c1 = pipe.stage(f"M{seq}ca{c}_{tc}")
                c1.op("vector", lambda e, c=c, tc=tc: e.tensor_scalar(
                    YAC[:, 0:512], vr(c, tc, off=-1), cx.sb["taps"][:, c * 3:c * 3 + 1],
                    None, ALU.mult))
                c2 = pipe.stage(f"M{seq}cb{c}_{tc}")
                c2.op("vector", lambda e, c=c, tc=tc: e.scalar_tensor_tensor(
                    YAC[:, 0:512], vr(c, tc), cx.sb["taps"][:, c * 3 + 1:c * 3 + 2],
                    YAC[:, 0:512], ALU.mult, ALU.add))
                c3 = pipe.stage(f"M{seq}cc{c}_{tc}")
                c3.op("vector", lambda e, c=c, tc=tc: e.scalar_tensor_tensor(
                    YAC[:, 0:512], vr(c, tc, off=1), cx.sb["taps"][:, c * 3 + 2:c * 3 + 3],
                    YAC[:, 0:512], ALU.mult, ALU.add))
                c4 = pipe.stage(f"M{seq}cd{c}_{tc}")
                c4.op("vector", lambda e, c=c, tc=tc: e.tensor_copy(
                    scB[:, 0:512], YAC[:, 0:512]))
                c5 = pipe.stage(f"M{seq}ce{c}_{tc}")
                c5.op("scalar", lambda e, c=c, tc=tc: e.activation(
                    vr(c, tc), scB[:, 0:512], AF.Silu))

        ldb = pipe.stage(f"M{seq}bL")
        cx.load_wblock(ldb, "l1a_B", 0, WS0)
        cx.load_wblock(ldb, "l1a_C", 0, WS1)
        for tc in range(4):
            mm = pipe.stage(f"M{seq}bM{tc}")
            for k in range(8):
                mm.op("tensor", lambda e, k=k, tc=tc: e.matmul(
                    P["P4"][0:16, 0:512], WS0[:, k * 16:(k + 1) * 16],
                    vr(k, tc), start=(k == 0), stop=(k == 7)))
                mm.op("tensor", lambda e, k=k, tc=tc: e.matmul(
                    P["P5"][0:16, 0:512], WS1[:, k * 16:(k + 1) * 16],
                    vr(k, tc), start=(k == 0), stop=(k == 7)))
            ev = pipe.stage(f"M{seq}bE{tc}")
            ev.op("scalar", lambda e: e.activation(
                brow[0:16, 0:512], P["P4"][0:16, 0:512], AF.Identity))
            ev1 = pipe.stage(f"M{seq}bF{tc}")
            ev1.dma("sync", lambda e, tc=tc: e.dma_start(
                out=cx.d["bcrow"][0, :, tc * 512:(tc + 1) * 512], in_=brow[0:16, 0:512]))
            ev2 = pipe.stage(f"M{seq}bG{tc}")
            ev2.op("scalar", lambda e: e.activation(
                brow[0:16, 0:512], P["P5"][0:16, 0:512], AF.Identity))
            ev3 = pipe.stage(f"M{seq}bH{tc}")
            ev3.dma("sync", lambda e, tc=tc: e.dma_start(
                out=cx.d["bcrow"][1, :, tc * 512:(tc + 1) * 512], in_=brow[0:16, 0:512]))

        pmm = {}
        prd = {}
        idx = 0
        for half in range(2):
            for fbk in range(8):
                buf = WS0 if idx % 2 == 0 else WS1
                ld = pipe.stage(f"M{seq}dL{half}_{fbk}",
                                deps=[pmm.get(idx - 1)] if idx >= 1 else None)
                cx.load_wblock(ld, "l1a_delta", fbk, buf)
                mmd = None if idx == 0 else [ld] + ([prd[idx - 2]] if idx >= 2 else [])
                mm = pipe.stage(f"M{seq}dM{half}_{fbk}", deps=mmd)
                g = idx % 2
                for k in range(8):
                    for j in range(2):
                        tc = half * 2 + j
                        mm.op("tensor", lambda e, k=k, j=j, tc=tc, buf=buf, g=g: e.matmul(
                            P[f"P{g * 2 + j}"][:, 0:512], buf[:, k * 128:(k + 1) * 128],
                            vr(k, tc), start=(k == 0), stop=(k == 7)))
                ev = pipe.stage(f"M{seq}dE{half}_{fbk}", deps=[mm, prd.get(idx - 1)])
                for j in range(2):
                    tc = half * 2 + j
                    ev.op("scalar", lambda e, j=j, fbk=fbk, tc=tc, g=g: e.activation(
                        xbsl(fbk, tc), P[f"P{g * 2 + j}"][:, 0:512], AF.Identity,
                        bias=cx.sb["dbase"][:, fbk:fbk + 1]))
                pmm[idx] = mm
                prd[idx] = ev
                idx += 1

        at1 = pipe.stage(f"M{seq}at1")
        at1.op("scalar", lambda e: e.activation(AT[:, 0:128], cx.sb["alog"][:, 0:128],
                                                AF.Exp))
        at2 = pipe.stage(f"M{seq}at2")
        at2.op("vector", lambda e: e.tensor_scalar(AT[:, 0:128], AT[:, 0:128], -1.0,
                                                   None, ALU.mult))

        pmm = {}
        prd = {}
        idx = 0
        for half in range(2):
            for fbk in range(8):
                buf = WS0 if idx % 2 == 0 else WS1
                ld = pipe.stage(f"M{seq}tL{half}_{fbk}",
                                deps=[pmm.get(idx - 1)] if idx >= 1 else None)
                cx.load_wblock(ld, "l1a_tau", fbk, buf)
                mmd = None if idx == 0 else [ld] + ([prd[idx - 2]] if idx >= 2 else [])
                mm = pipe.stage(f"M{seq}tM{half}_{fbk}", deps=mmd)
                g = idx % 2
                for k in range(8):
                    for j in range(2):
                        tc = half * 2 + j
                        mm.op("tensor", lambda e, k=k, j=j, tc=tc, buf=buf, g=g: e.matmul(
                            P[f"P{g * 2 + j}"][:, 0:512], buf[:, k * 128:(k + 1) * 128],
                            xbsl(k, tc), start=(k == 0), stop=(k == 7)))
                for j in range(2):
                    tc = half * 2 + j
                    bank = P[f"P{g * 2 + j}"]
                    s1 = pipe.stage(f"M{seq}tA{half}_{fbk}_{j}",
                                    deps=[mm, prd.get(idx - 1)] if j == 0 else None)
                    s1.op("vector", lambda e, bank=bank: e.tensor_scalar(
                        scA[:, 0:512], bank[:, 0:512], -1.0, None, ALU.mult))
                    s2 = pipe.stage(f"M{seq}tB{half}_{fbk}_{j}")
                    s2.op("vector", lambda e, bank=bank: e.tensor_tensor(
                        scA[:, 0:512], scA[:, 0:512], bank[:, 0:512], ALU.max))
                    s3 = pipe.stage(f"M{seq}tC{half}_{fbk}_{j}")
                    s3.op("scalar", lambda e: e.activation(
                        scB[:, 0:512], scA[:, 0:512], AF.Exp, scale=-1.0))
                    s4 = pipe.stage(f"M{seq}tD{half}_{fbk}_{j}")
                    s4.op("scalar", lambda e: e.activation(
                        scB[:, 0:512], scB[:, 0:512], AF.Ln, bias=1.0))
                    s5 = pipe.stage(f"M{seq}tE{half}_{fbk}_{j}")
                    s5.op("vector", lambda e, bank=bank: e.tensor_scalar(
                        scA[:, 0:512], bank[:, 0:512], 0.0, None, ALU.max))
                    s6 = pipe.stage(f"M{seq}tF{half}_{fbk}_{j}")
                    s6.op("vector", lambda e: e.tensor_tensor(
                        gtb[:, 0:512], scA[:, 0:512], scB[:, 0:512], ALU.add))
                    s7 = pipe.stage(f"M{seq}tG{half}_{fbk}_{j}")
                    s7.dma("sync", lambda e, fbk=fbk, tc=tc: e.dma_start(
                        out=cx.d["ddram"][fbk, :, tc * 512:(tc + 1) * 512],
                        in_=gtb[:, 0:512]))
                    if j == 1:
                        pmm[idx] = mm
                        prd[idx] = s7
                        idx += 1

        for c in range(NCH):
            ldd = pipe.stage(f"M{seq}dl{c}")
            ldd.dma("sync", lambda e, c=c: e.dma_start(
                out=XB[:, c * 2049 + 1: c * 2049 + 1 + T], in_=cx.d["ddram"][c]))

        for tt in range(4):
            bb = pipe.stage(f"M{seq}sB{tt}")
            for s in range(16):
                for bc in range(2):
                    bb.dma("gpsimd", lambda e, s=s, bc=bc, tt=tt: e.dma_start(
                        out=BCB[:, (bc * 16 + s) * 512:(bc * 16 + s + 1) * 512],
                        in_=cx.d["bcrow"][bc, s, tt * 512:(tt + 1) * 512]
                            .unsqueeze(0).broadcast_to((128, 512))))
            for c in range(NCH):
                uu = pipe.stage(f"M{seq}sU{tt}_{c}")
                uu.op("vector", lambda e, c=c, tt=tt: e.tensor_tensor(
                    u2[:, 0:512], dlt(c, tt), vr(c, tt), ALU.mult))
                ym = pipe.stage(f"M{seq}sY{tt}_{c}")
                ym.op("vector", lambda e: e.memset(YAC[:, 0:512], 0.0))
                for g in range(4):
                    for si in range(4):
                        s = g * 4 + si
                        d1 = pipe.stage(f"M{seq}sa{tt}_{c}_{g}_{si}")
                        d1.op("vector", lambda e, c=c, tt=tt, s=s, si=si: e.tensor_scalar(
                            dat[:, si * 512:(si + 1) * 512], dlt(c, tt),
                            AT[:, c * 16 + s:c * 16 + s + 1], -2e-6,
                            ALU.mult, ALU.min))
                    d2 = pipe.stage(f"M{seq}sb{tt}_{c}_{g}")
                    d2.op("scalar", lambda e: e.activation(
                        et[:, 0:2048], dat[:, 0:2048], AF.Exp))
                    d3 = pipe.stage(f"M{seq}sc{tt}_{c}_{g}")
                    d3.op("scalar", lambda e: e.activation(
                        rt[:, 0:2048], dat[:, 0:2048], AF.Ln, scale=-1.0, bias=-1e-6))
                    d4 = pipe.stage(f"M{seq}sd{tt}_{c}_{g}")
                    d4.op("scalar", lambda e: e.activation(
                        rt[:, 0:2048], rt[:, 0:2048], AF.Exp, scale=-1.0))
                    d5 = pipe.stage(f"M{seq}se{tt}_{c}_{g}")
                    d5.op("vector", lambda e: e.tensor_scalar(
                        wk[:, 0:2048], et[:, 0:2048], -1.0, 1.0, ALU.mult, ALU.add))
                    d6 = pipe.stage(f"M{seq}sf{tt}_{c}_{g}")
                    d6.op("vector", lambda e: e.tensor_tensor(
                        wk[:, 0:2048], wk[:, 0:2048], rt[:, 0:2048], ALU.mult))
                    d7 = pipe.stage(f"M{seq}sg{tt}_{c}_{g}")
                    d7.op("vector", lambda e, g=g: e.tensor_tensor(
                        dat[:, 0:2048].rearrange("p (s t) -> p s t", t=512),
                        u2[:, 0:512].unsqueeze(1).broadcast_to((128, 4, 512)),
                        BCB[:, g * 4 * 512:(g + 1) * 4 * 512].rearrange(
                            "p (s t) -> p s t", t=512),
                        ALU.mult))
                    d8 = pipe.stage(f"M{seq}sh{tt}_{c}_{g}")
                    d8.op("vector", lambda e: e.tensor_tensor(
                        bu[:, 0:2048], wk[:, 0:2048], dat[:, 0:2048], ALU.mult))
                    for si in range(4):
                        s = g * 4 + si
                        d9 = pipe.stage(f"M{seq}si{tt}_{c}_{g}_{si}")
                        if tt == 0:
                            d9.op("vector", lambda e, si=si: e.tensor_tensor_scan(
                                rt[:, si * 512:(si + 1) * 512],
                                et[:, si * 512:(si + 1) * 512],
                                bu[:, si * 512:(si + 1) * 512],
                                0.0, ALU.mult, ALU.add))
                        else:
                            d9.op("vector", lambda e, si=si, c=c, s=s: e.tensor_tensor_scan(
                                rt[:, si * 512:(si + 1) * 512],
                                et[:, si * 512:(si + 1) * 512],
                                bu[:, si * 512:(si + 1) * 512],
                                hst[:, c * 16 + s:c * 16 + s + 1], ALU.mult, ALU.add))
                        da = pipe.stage(f"M{seq}sj{tt}_{c}_{g}_{si}")
                        da.op("vector", lambda e, si=si, c=c, s=s: e.tensor_copy(
                            hst[:, c * 16 + s:c * 16 + s + 1],
                            rt[:, si * 512 + 511: si * 512 + 512]))
                    db = pipe.stage(f"M{seq}sk{tt}_{c}_{g}")
                    db.op("vector", lambda e, g=g: e.tensor_tensor(
                        rt[:, 0:2048], rt[:, 0:2048],
                        BCB[:, (16 + g * 4) * 512:(16 + (g + 1) * 4) * 512],
                        ALU.mult))
                    for si in range(4):
                        dc = pipe.stage(f"M{seq}sl{tt}_{c}_{g}_{si}")
                        dc.op("vector", lambda e, si=si: e.tensor_tensor(
                            YAC[:, 0:512], YAC[:, 0:512],
                            rt[:, si * 512:(si + 1) * 512], ALU.add))
                g1 = pipe.stage(f"M{seq}sg1{tt}_{c}")
                g1.dma("sync", lambda e, c=c, tt=tt: e.dma_start(
                    out=xbt[:, 0:512],
                    in_=cx.d["gdram"][c, :, tt * 512:(tt + 1) * 512]))
                g2 = pipe.stage(f"M{seq}sg2{tt}_{c}")
                g2.op("scalar", lambda e: e.activation(
                    scA[:, 0:512], xbt[:, 0:512], AF.Silu))
                g3 = pipe.stage(f"M{seq}sg3{tt}_{c}")
                g3.op("vector", lambda e: e.tensor_tensor(
                    gtb[:, 0:512], YAC[:, 0:512], scA[:, 0:512], ALU.mult))
                g4s = pipe.stage(f"M{seq}sg4{tt}_{c}")
                g4s.dma("sync", lambda e, c=c, tt=tt: e.dma_start(
                    out=cx.d["ygdram"][c, :, tt * 512:(tt + 1) * 512],
                    in_=gtb[:, 0:512]))

        for c in range(NCH):
            ldy = pipe.stage(f"M{seq}oy{c}")
            ldy.dma("sync", lambda e, c=c: e.dma_start(
                out=VR[:, c * 2050 + 1: c * 2050 + 1 + T], in_=cx.d["ygdram"][c]))
        pmm = {}
        prd = {}
        idx = 0
        for tc in range(4):
            emit_xf_load(cx, seq, tc, shift=False, name=f"M{seq}or{tc}", xsrc="xst")
            xfst = pipe.stages[-1]
            for fbk in range(8):
                buf = WS0 if idx % 2 == 0 else WS1
                ld = pipe.stage(f"M{seq}oL{tc}_{fbk}",
                                deps=[pmm.get(idx - 1)] if idx >= 1 else None)
                cx.load_wblock(ld, "l1a_out", fbk, buf)
                mmd = None if idx == 0 else [ld] + ([prd[idx - 2]] if idx >= 2 else [])
                mm = pipe.stage(f"M{seq}oM{tc}_{fbk}", deps=mmd)
                g = idx % 2
                for k in range(8):
                    mm.op("tensor", lambda e, k=k, tc=tc, buf=buf, g=g: e.matmul(
                        P[f"P{g * 2}"][:, 0:512], buf[:, k * 128:(k + 1) * 128],
                        vr(k, tc), start=(k == 0), stop=(k == 7)))
                b1 = pipe.stage(f"M{seq}oB{tc}_{fbk}",
                                deps=[mm, prd.get(idx - 1), xfst])
                b1.op("tensor", lambda e, tc=tc: e.matmul(
                    P["P4"][:, 0:512], onesb[0:1, :],
                    bRS2[0:1, tc * 512:(tc + 1) * 512], start=True, stop=True))
                b1.op("tensor", lambda e, tc=tc: e.matmul(
                    P["P5"][:, 0:512], onesb[0:1, :],
                    bMR22[0:1, tc * 512:(tc + 1) * 512], start=True, stop=True))
                e1 = pipe.stage(f"M{seq}oE{tc}_{fbk}")
                e1.op("vector", lambda e, fbk=fbk: e.tensor_tensor(
                    scA[:, 0:512], cx.xf(fbk), P["P4"][:, 0:512], ALU.mult))
                e2 = pipe.stage(f"M{seq}oF{tc}_{fbk}")
                e2.op("vector", lambda e: e.tensor_tensor(
                    scA[:, 0:512], scA[:, 0:512], P["P5"][:, 0:512], ALU.subtract))
                e3 = pipe.stage(f"M{seq}oG{tc}_{fbk}")
                e3.op("vector", lambda e, fbk=fbk: e.tensor_tensor(
                    scA[:, 0:512], scA[:, 0:512], cx.xf(fbk), ALU.add))
                e4 = pipe.stage(f"M{seq}oH{tc}_{fbk}")
                e4.op("vector", lambda e, g=g: e.tensor_tensor(
                    scB[:, 0:512], scA[:, 0:512], P[f"P{g * 2}"][:, 0:512], ALU.add))
                e5 = pipe.stage(f"M{seq}oI{tc}_{fbk}")
                e5.dma("sync", lambda e, fbk=fbk, tc=tc: e.dma_start(
                    out=cx.x_ap(seq, fbk, tc * 512, 512), in_=scB[:, 0:512]))
                pmm[idx] = mm
                prd[idx] = e5
                idx += 1



_UNUSED3 = 0



def emit_l1f(cx, seq):
    pipe = cx.pipe
    P = cx.ps
    XB = cx.sb["XB"]
    cb = cx.sb["cstb"]
    cstf = cx.sb["cstf"]
    onesb = cb[:, 0:128]
    idf = cstf[:, 520:648]
    WS0, WS1 = cx.sb["WS0"], cx.sb["WS1"]
    scA, scB = cx.sb["scA"], cx.sb["scB"]
    xbt = cx.sb["xbt"]
    XF = cx.sb["XF"]

    def xbsl(c, tc, w=512):
        return XB[:, c * 2049 + 1 + tc * 512: c * 2049 + 1 + tc * 512 + w]

    with contextlib.ExitStack() as ph:
        mrowb = cx.sbuf("mrowb", [8, T], BF16, ph)
        lgt = cx.sbuf("lgt", [8, 512], F32, ph)
        whib = cx.sbuf("whib", [128, 64], BF16, ph)
        wlob = cx.sbuf("wlob", [128, 64], BF16, ph)
        b2b = cx.sbuf("b2b", [8, 1024], BF16, ph)
        b2f = cx.sbuf("b2f", [8, 1024], F32, ph)
        vb1e = cx.sbuf("vb1e", [128, 32], F32, ph)
        mb2 = cx.sbuf("mb2", [128, 1024], BF16, ph)
        yacc = cx.sbuf("yacc", [128, 8 * 1024], F32, ph)
        htmp = cx.sbuf("htmp", [128, 512], BF16, ph)

        r0 = pipe.stage(f"R{seq}w0")
        r0.op("vector", lambda e: e.tensor_copy(whib[:, 0:64], cx.sb["wrf"][:, 0:64]))
        r1 = pipe.stage(f"R{seq}w1")
        r1.op("vector", lambda e: e.tensor_tensor(
            wlob[:, 0:64], cx.sb["wrf"][:, 0:64], whib[:, 0:64], ALU.subtract))
        r2 = pipe.stage(f"R{seq}w2")
        r2.dma("sync", lambda e: e.dma_start(out=b2f[:], in_=cx.d["l1f_b2"][:]))
        r3 = pipe.stage(f"R{seq}w3")
        r3.op("vector", lambda e: e.tensor_copy(b2b[:], b2f[:]))

        with contextlib.ExitStack() as ph2:
            LO = cx.sbuf("LO", [128, NCH * T], BF16, ph2)

            def lo_sl(c, tc):
                return LO[:, c * T + tc * 512: c * T + (tc + 1) * 512]

            def f32out(c, tc):
                return scB[:, 0:512]

            def post(c, tc):
                st = pipe.stage(f"R{seq}lo{c}_{tc}")
                st.op("vector", lambda e, c=c, tc=tc: e.tensor_tensor(
                    lo_sl(c, tc), scB[:, 0:512], xbsl(c, tc), ALU.subtract))

            emit_ln(cx, seq, f"R{seq}ln", xbsl, out_f32_fn=f32out, post_f32=post)

            for tc in range(4):
                mm = pipe.stage(f"R{seq}rM{tc}")
                for k in range(8):
                    mm.op("tensor", lambda e, k=k, tc=tc: e.matmul(
                        P["P4"][0:8, 0:512], whib[:, k * 8:(k + 1) * 8],
                        xbsl(k, tc), start=(k == 0), stop=False))
                for k in range(8):
                    mm.op("tensor", lambda e, k=k, tc=tc: e.matmul(
                        P["P4"][0:8, 0:512], wlob[:, k * 8:(k + 1) * 8],
                        xbsl(k, tc), start=False, stop=False))
                for k in range(8):
                    mm.op("tensor", lambda e, k=k, tc=tc: e.matmul(
                        P["P4"][0:8, 0:512], whib[:, k * 8:(k + 1) * 8],
                        lo_sl(k, tc), start=False, stop=(k == 7)))
                ev = pipe.stage(f"R{seq}rE{tc}")
                ev.op("scalar", lambda e: e.activation(
                    lgt[0:8, 0:512], P["P4"][0:8, 0:512], AF.Identity,
                    bias=cx.sb["brf"][0:8, 0:1]))
                tr = pipe.stage(f"R{seq}rT{tc}")
                for st4 in range(4):
                    tr.op("tensor", lambda e, st4=st4: e.transpose(
                        P["P5"][:, st4 * 8:(st4 + 1) * 8],
                        lgt[0:8, st4 * 128:(st4 + 1) * 128], idf[0:8, 0:8]))
                mx = pipe.stage(f"R{seq}rX{tc}")
                mx.op("vector", lambda e: e.reduce_max(
                    scB[:, 0:4], P["P5"][:, 0:32].rearrange("p (a b) -> p a b", b=8),
                    axis=AX.X))
                eq = pipe.stage(f"R{seq}rQ{tc}")
                eq.op("vector", lambda e: e.tensor_tensor(
                    scA[:, 0:32].rearrange("p (a b) -> p a b", b=8),
                    P["P5"][:, 0:32].rearrange("p (a b) -> p a b", b=8),
                    scB[:, 0:4].unsqueeze(2).broadcast_to((128, 4, 8)),
                    ALU.is_equal))
                tb = pipe.stage(f"R{seq}rB{tc}")
                for st4 in range(4):
                    tb.op("tensor", lambda e, st4=st4: e.transpose(
                        P["P6"][0:8, st4 * 128:(st4 + 1) * 128],
                        scA[:, st4 * 8:(st4 + 1) * 8], idf[:, 0:128]))
                me = pipe.stage(f"R{seq}rC{tc}")
                me.op("scalar", lambda e, tc=tc: e.activation(
                    mrowb[0:8, tc * 512:(tc + 1) * 512], P["P6"][0:8, 0:512],
                    AF.Identity))
                md = pipe.stage(f"R{seq}rD{tc}")
                md.dma("sync", lambda e, tc=tc: e.dma_start(
                    out=cx.d["mdram"][:, tc * 512:(tc + 1) * 512],
                    in_=mrowb[0:8, tc * 512:(tc + 1) * 512]))

        with contextlib.ExitStack() as ph3:
            HB0 = cx.sbuf("HB0", [128, 32 * 512], BF16, ph3)
            HB1 = cx.sbuf("HB1", [128, 32 * 512], BF16, ph3)
            for half in range(2):
                for fbk in range(8):
                    ib = pipe.stage(f"R{seq}y0{half}_{fbk}")
                    for jj in range(2):
                        tc = half * 2 + jj
                        ib.op("tensor", lambda e, fbk=fbk, jj=jj, tc=tc: e.matmul(
                            P["P4"][:, 0:512] if jj == 0 else P["P5"][:, 0:512],
                            b2b[0:8, fbk * 128:(fbk + 1) * 128],
                            mrowb[0:8, tc * 512:(tc + 1) * 512],
                            start=True, stop=True))
                    ie = pipe.stage(f"R{seq}y1{half}_{fbk}")
                    ie.op("vector", lambda e, fbk=fbk: e.tensor_copy(
                        yacc[:, fbk * 1024: fbk * 1024 + 512], P["P4"][:, 0:512]))
                    ie.op("scalar", lambda e, fbk=fbk: e.copy(
                        yacc[:, fbk * 1024 + 512: fbk * 1024 + 1024], P["P5"][:, 0:512]))
                for ex in range(8):
                    lb = pipe.stage(f"R{seq}e{half}_{ex}b")
                    lb.dma("sync", lambda e, ex=ex: e.dma_start(
                        out=vb1e[:], in_=cx.d["l1f_b1"][ex]))
                    for jj in range(2):
                        tc = half * 2 + jj
                        lb.dma("gpsimd", lambda e, ex=ex, jj=jj, tc=tc: e.dma_start(
                            out=mb2[:, jj * 512:(jj + 1) * 512],
                            in_=cx.d["mdram"][ex, tc * 512:(tc + 1) * 512]
                                .unsqueeze(0).broadcast_to((128, 512))))
                    pmm = {}
                    prd = {}
                    for fbk in range(32):
                        buf = WS0 if fbk % 2 == 0 else WS1
                        ld = pipe.stage(f"R{seq}e{half}_{ex}L{fbk}",
                                        deps=[pmm.get(fbk - 1)] if fbk >= 1 else None)
                        cx.load_wblock(ld, f"l1f_w1_{ex}", fbk, buf)
                        mmd = None if fbk == 0 else [ld] + ([prd[fbk - 2]] if fbk >= 2 else [])
                        mm = pipe.stage(f"R{seq}e{half}_{ex}M{fbk}", deps=mmd)
                        g = fbk % 2
                        for k in range(8):
                            for jj in range(2):
                                tc = half * 2 + jj
                                mm.op("tensor", lambda e, k=k, jj=jj, tc=tc, buf=buf, g=g: e.matmul(
                                    P[f"P{g * 2 + jj}"][:, 0:512], buf[:, k * 128:(k + 1) * 128],
                                    xbsl(k, tc), start=(k == 0), stop=(k == 7)))
                        last = None
                        for jj, HB in ((0, HB0), (1, HB1)):
                            e1 = pipe.stage(f"R{seq}e{half}_{ex}E{fbk}_{jj}",
                                            deps=[mm, prd.get(fbk - 1)] if jj == 0 else None)
                            e1.op("scalar", lambda e, jj=jj, fbk=fbk, g=g: e.activation(
                                htmp[:, 0:512], P[f"P{g * 2 + jj}"][:, 0:512], AF.Relu,
                                bias=vb1e[:, fbk:fbk + 1]))
                            e2 = pipe.stage(f"R{seq}e{half}_{ex}F{fbk}_{jj}")
                            e2.op("vector", lambda e, jj=jj, fbk=fbk, HB=HB: e.tensor_tensor(
                                HB[:, fbk * 512:(fbk + 1) * 512], htmp[:, 0:512],
                                mb2[:, jj * 512:(jj + 1) * 512], ALU.mult))
                            last = e2
                        pmm[fbk] = mm
                        prd[fbk] = last
                    pmm = {}
                    prd = {}
                    for fbk in range(8):
                        buf = WS0 if fbk % 2 == 0 else WS1
                        ld = pipe.stage(f"R{seq}e{half}_{ex}W{fbk}",
                                        deps=[pmm.get(fbk - 1)] if fbk >= 1 else None)
                        cx.load_wblock(ld, f"l1f_w2_{ex}", fbk, buf)
                        mmd = None if fbk == 0 else [ld] + ([prd[fbk - 2]] if fbk >= 2 else [])
                        mm = pipe.stage(f"R{seq}e{half}_{ex}N{fbk}", deps=mmd)
                        g = fbk % 2
                        for k in range(32):
                            for jj, HB in ((0, HB0), (1, HB1)):
                                mm.op("tensor", lambda e, k=k, jj=jj, HB=HB, buf=buf, g=g: e.matmul(
                                    P[f"P{g * 2 + jj}"][:, 0:512], buf[:, k * 128:(k + 1) * 128],
                                    HB[:, k * 512:(k + 1) * 512],
                                    start=(k == 0), stop=(k == 31)))
                        last = None
                        for jj in range(2):
                            av = pipe.stage(f"R{seq}e{half}_{ex}A{fbk}_{jj}",
                                            deps=[mm, prd.get(fbk - 1)] if jj == 0 else None)
                            av.op("vector", lambda e, jj=jj, fbk=fbk, g=g: e.tensor_tensor(
                                yacc[:, fbk * 1024 + jj * 512: fbk * 1024 + (jj + 1) * 512],
                                yacc[:, fbk * 1024 + jj * 512: fbk * 1024 + (jj + 1) * 512],
                                P[f"P{g * 2 + jj}"][:, 0:512], ALU.add))
                            last = av
                        pmm[fbk] = mm
                        prd[fbk] = last
                for jj in range(2):
                    tc = half * 2 + jj
                    emit_xf_load(cx, seq, tc, shift=False, name=f"R{seq}res{half}_{jj}",
                                 xsrc="xst")
                    for fbk in range(8):
                        w1 = pipe.stage(f"R{seq}z{half}_{jj}_{fbk}")
                        w1.op("vector", lambda e, fbk=fbk, jj=jj: e.tensor_tensor(
                            scB[:, 0:512],
                            yacc[:, fbk * 1024 + jj * 512: fbk * 1024 + (jj + 1) * 512],
                            cx.xf(fbk), ALU.add))
                        w2 = pipe.stage(f"R{seq}zz{half}_{jj}_{fbk}")
                        w2.dma("sync", lambda e, fbk=fbk, tc=tc: e.dma_start(
                            out=cx.x_ap(seq, fbk, tc * 512, 512), in_=scB[:, 0:512]))



_CACHE = {}


def _prep_cache(inputs):
    key = id(inputs.get("L0_attn_Wqkv"))
    if key not in _CACHE:
        _CACHE.clear()
        _CACHE[key] = host_prep(inputs)
    return _CACHE[key]


def make_in_map(inputs, core):
    w, wsrc, vec, consts = _prep_cache(inputs)
    x = np.asarray(inputs["L0_attn_x"], np.float32)
    xin = np.stack([x_to_fm(x[core * 2 + s]) for s in range(2)])
    m = {"wsrc": wsrc, "xin": xin,
         "ropeI": consts["tabs_ropeI"], "xpos": consts["tabs_xpos"],
         "cstf": consts["cstf"], "cstb": consts["cstb"], "cstk": consts["cstk"]}
    for k, v in vec.items():
        m[k.replace("gqk_col", "gqk_col")] = v
    return m


VEC_DRAM_NAMES = ["bqkv", "bout", "gqk_col", "gout_row", "l0f_qs", "l0f_ks",
                  "l0f_g", "l0f_b", "l0f_b1", "l0f_b2", "l1a_conv", "l1a_db",
                  "l1a_alog", "l1f_wr", "l1f_br", "l1f_b1", "l1f_b2"]


def kernel(**inputs):
    w, wsrc, vec, consts = _prep_cache(inputs)
    parts = getattr(kernel, "_parts", PARTS_ALL)
    ncores = getattr(kernel, "_ncores", 8)
    nc = build_program(w.meta, wsrc.size, parts=parts)
    in_maps = [make_in_map(inputs, c) for c in range(ncores)]
    res = run_bass_kernel_spmd(nc, in_maps, core_ids=list(range(ncores)))
    outs = []
    for c in range(ncores):
        fm = np.asarray(res.results[c]["xout"], np.float32)
        for s in range(2):
            outs.append(fm[s].reshape(1024, T).T)
    return np.stack(outs).astype(np.float32)


# version 05
# speedup vs baseline: 1.3686x; 1.3686x over previous
import contextlib
import numpy as np
import ml_dtypes

import concourse.bass as bass
import concourse.mybir as mybir
from concourse.bass_utils import run_bass_kernel_spmd

F32 = mybir.dt.float32
BF16 = mybir.dt.bfloat16
U8 = mybir.dt.uint8
AF = mybir.ActivationFunctionType
ALU = mybir.AluOpType
AX = mybir.AxisListType

T = 2048
TPAD = T + 4
NCH = 8
NEG_BIG = -3.4028235e38
RMS_EPS = 1.1920929e-07

ENGINES = ("tensor", "vector", "scalar", "gpsimd", "sync")
DEBUG = False



class Stage:
    __slots__ = ("idx", "deps", "ops", "milestones", "name")

    def __init__(self, idx, deps, name):
        self.idx = idx
        self.deps = deps
        self.name = name
        self.ops = {}
        self.milestones = {}

    def op(self, eng, fn):
        self.ops.setdefault(eng, []).append(("c", fn))
        return self

    def dma(self, eng, fn):
        self.ops.setdefault(eng, []).append(("d", fn))
        return self


class Pipe:
    def __init__(self, nc, es):
        self.nc = nc
        self.sems = {e: es.enter_context(nc.semaphore(f"sm_{e}")) for e in ENGINES}
        self.stages = []

    def stage(self, name="", deps=None):
        st = Stage(len(self.stages), deps, name)
        self.stages.append(st)
        return st

    def finalize(self):
        nc = self.nc
        counts = {e: 0 for e in ENGINES}
        order = []
        last = None
        for st in self.stages:
            if not st.ops:
                continue
            if st.deps is None:
                st.deps = [last] if last is not None else []
            else:
                st.deps = [d for d in st.deps if d is not None]
            for eng, ops in st.ops.items():
                inc = 16 * sum(1 for k, _ in ops if k == "d")
                if any(k == "c" for k, _ in ops):
                    inc += 1
                counts[eng] += inc
                st.milestones[eng] = counts[eng]
            order.append(st)
            last = st
        scripts = {e: [] for e in ENGINES}
        floors = {e: {f: 0 for f in ENGINES} for e in ENGINES}
        for st in order:
            need = {}
            for d in st.deps:
                for eng, v in d.milestones.items():
                    need[eng] = max(need.get(eng, 0), v)
            for eng, ops in st.ops.items():
                for f, v in sorted(need.items()):
                    if floors[eng][f] >= v:
                        continue
                    scripts[eng].append(("w", f, v))
                    floors[eng][f] = v
                lastc = max((i for i, (k, _) in enumerate(ops) if k == "c"),
                            default=-1)
                for i, (kind, fn) in enumerate(ops):
                    inc = 16 if kind == "d" else (1 if i == lastc else 0)
                    scripts[eng].append(("x", fn, inc))
        used = [e for e in ENGINES if scripts[e]]
        sems = self.sems
        with nc.Block() as blk:
            for eng_name in used:
                script = scripts[eng_name]

                def mk(eng_name=eng_name, script=script):
                    def body(eng):
                        mysem = sems[eng_name]
                        for item in script:
                            if item[0] == "w":
                                eng.wait_ge(sems[item[1]], item[2])
                            else:
                                inst = item[1](eng)
                                if item[2]:
                                    inst.then_inc(mysem, item[2])
                    return body

                getattr(blk, eng_name)(mk())
        return counts




def _xpos_tables():
    dh = 64
    half = np.arange(0, dh, 2, dtype=np.float64)
    inv_freq = 1.0 / 10000.0 ** (half / dh)
    t = np.arange(T, dtype=np.float64)
    freqs = t[:, None] * inv_freq[None, :]
    freqs = np.concatenate([freqs, freqs], -1)
    base = (half + 0.4 * dh) / (1.4 * dh)
    power = (t - T // 2) / (256 // 2)
    scale = base[None, :] ** power[:, None]
    scale = np.concatenate([scale, scale], -1)
    cos, sin = np.cos(freqs), np.sin(freqs)
    tabs = []
    for tab in (cos * scale, sin * scale, cos / scale, sin / scale):
        tabs.append(np.tile(np.ascontiguousarray(tab.T), (2, 1)).astype(np.float32))
    return tabs


def _ropeI_tables():
    dh = 64
    inv_freq = 1.0 / 10000.0 ** (np.arange(0, dh, 2, dtype=np.float64) / dh)
    t = np.arange(T, dtype=np.float64)
    freqs = inv_freq[:, None] * t[None, :]
    cos = np.repeat(np.cos(freqs), 2, axis=0)
    sin = np.repeat(np.sin(freqs), 2, axis=0)
    return (np.tile(cos, (2, 1)).astype(np.float32),
            np.tile(sin, (2, 1)).astype(np.float32))


def _perm_interleaved():
    R = np.zeros((128, 128), np.float32)
    for b in range(2):
        for i in range(32):
            e = b * 64 + 2 * i
            R[e, e + 1] = -1.0
            R[e + 1, e] = 1.0
    return np.ascontiguousarray(R.T)


def _perm_half():
    R = np.zeros((128, 128), np.float32)
    for b in range(2):
        o = b * 64
        for i in range(32):
            R[o + i, o + 32 + i] = -1.0
            R[o + 32 + i, o + i] = 1.0
    return np.ascontiguousarray(R.T)


class WImg:
    def __init__(self):
        self.blobs = []
        self.meta = {}
        self.n = 0

    def add(self, name, arr):
        arr = np.asarray(arr, np.float32)
        K, F = arr.shape
        nk = (K + 127) // 128
        off0 = self.n
        fws = []
        blocks = []
        for fb0 in range(0, F, 128):
            fw = min(128, F - fb0)
            blk = np.zeros((128, nk * fw), np.float32)
            for k in range(nk):
                r0, r1 = k * 128, min((k + 1) * 128, K)
                blk[0:r1 - r0, k * fw:k * fw + fw] = arr[r0:r1, fb0:fb0 + fw]
            blocks.append(blk.reshape(-1))
            fws.append(fw)
            self.n += blk.size
        self.blobs.append(np.concatenate(blocks))
        self.meta[name] = (off0, nk, fws, F)

    def cat(self):
        flat = np.concatenate(self.blobs)
        pad = (-flat.size) % 512
        if pad:
            flat = np.concatenate([flat, np.zeros(pad, np.float32)])
        return flat


def host_prep(inputs):
    gi = lambda k: np.asarray(inputs[k], dtype=np.float32)
    w = WImg()
    w.add("l0a_qkv", gi("L0_attn_Wqkv").T)
    w.add("l0a_out", gi("L0_attn_Wout").T)
    W1 = gi("L0_ffn_W1")
    W2 = gi("L0_ffn_W2")
    W1p = np.zeros((6, 1024, 5632), np.float32)
    W1p[:, :, 0:2730] = W1[:, :, 0:2730]
    W1p[:, :, 2816:2816 + 2730] = W1[:, :, 2730:5460]
    W2p = np.zeros((6, 2816, 1024), np.float32)
    W2p[:, 0:2730, :] = W2
    for i in range(6):
        w.add(f"l0f_qkv{i}", gi("L0_ffn_Wqkv")[i])
        w.add(f"l0f_wo{i}", gi("L0_ffn_Wo")[i])
        w.add(f"l0f_w1{i}", W1p[i])
        w.add(f"l0f_w2{i}", W2p[i])
    w.add("l1a_in", gi("L1_attn_W_in"))
    w.add("l1a_B", gi("L1_attn_W_B"))
    w.add("l1a_C", gi("L1_attn_W_C"))
    w.add("l1a_delta", gi("L1_attn_W_delta"))
    w.add("l1a_tau", gi("L1_attn_W_tau"))
    w.add("l1a_out", gi("L1_attn_W_out"))
    for e in range(8):
        w.add(f"l1f_w1_{e}", gi("L1_ffn_W1")[e].T)
        w.add(f"l1f_w2_{e}", gi("L1_ffn_W2")[e].T)
    wsrc = w.cat()

    c1, s1, c2, s2 = _xpos_tables()
    cosI, sinI = _ropeI_tables()
    r = np.arange(128)
    MASKV = -1e30
    ut = np.where(r[None, :] >= r[:, None], 0.0, MASKV).astype(np.float32)
    lt = np.where(r[None, :] <= r[:, None], 0.0, MASKV).astype(np.float32)

    cstf = np.zeros((128, 648), np.float32)
    cstf[:, 0:128] = _perm_interleaved()
    cstf[:, 128:256] = _perm_half()
    cstf[:, 256:384] = ut
    cstf[:, 384:512] = lt
    cstf[:, 512:520] = np.tile(np.arange(8, dtype=np.float32), (128, 1))
    cstf[:, 520:648] = np.eye(128)
    cstb = np.zeros((128, 392), ml_dtypes.bfloat16)
    cstb[:, 0:128] = 1.0
    cstb[:, 128:256] = np.eye(128)
    cstb[:, 256] = 1.0 / 1024.0
    cstb[:, 257] = 1.0
    cstb[:, 258] = -1e9
    cstb[0:64, 259] = 1.0
    cstb[64:128, 260] = 1.0
    cstb[0, 264:264 + 64] = 1.0
    cstb[1, 264 + 64:264 + 128] = 1.0
    vec = {}
    vec["bqkv"] = np.ascontiguousarray(gi("L0_attn_bqkv").reshape(24, 128).T)
    vec["bout"] = np.ascontiguousarray(gi("L0_attn_bout").reshape(8, 128).T)
    vec["gqk_col"] = np.ascontiguousarray(
        np.concatenate([gi("L0_attn_gq").reshape(8, 128).T,
                        gi("L0_attn_gk").reshape(8, 128).T], axis=1))
    vec["gout_row"] = gi("L0_attn_gout")[None, :].copy()
    vec["l0f_qs"] = np.ascontiguousarray(np.tile(gi("L0_ffn_q_scale"), (1, 2)).T)
    vec["l0f_ks"] = np.ascontiguousarray(np.tile(gi("L0_ffn_k_scale"), (1, 2)).T)
    vec["l0f_g"] = gi("L0_ffn_ln_g").copy()
    vec["l0f_b"] = gi("L0_ffn_ln_b").copy()
    b1 = gi("L0_ffn_b1")
    b1p = np.zeros((6, 5632), np.float32)
    b1p[:, 0:2730] = b1[:, 0:2730]
    b1p[:, 2816:2816 + 2730] = b1[:, 2730:5460]
    vec["l0f_b1"] = np.ascontiguousarray(b1p.reshape(6, 44, 128).transpose(0, 2, 1))
    vec["l0f_b2"] = np.ascontiguousarray(gi("L0_ffn_b2").reshape(6, 8, 128).transpose(0, 2, 1))
    vec["l1a_conv"] = np.ascontiguousarray(
        gi("L1_attn_conv_w").reshape(8, 128, 3).transpose(1, 0, 2).reshape(128, 24))
    vec["l1a_db"] = np.ascontiguousarray(gi("L1_attn_delta_base").reshape(8, 128).T)
    vec["l1a_alog"] = np.ascontiguousarray(
        gi("L1_attn_A_log").reshape(8, 128, 16).transpose(1, 0, 2).reshape(128, 128))
    vec["l1f_wr"] = np.ascontiguousarray(gi("L1_ffn_Wr").T.reshape(8, 128, 8).transpose(1, 0, 2).reshape(128, 64))
    vec["l1f_br"] = gi("L1_ffn_br").reshape(8, 1).copy()
    vec["l1f_b1"] = np.ascontiguousarray(gi("L1_ffn_b1").reshape(8, 32, 128).transpose(0, 2, 1))
    vec["l1f_b2"] = gi("L1_ffn_b2").copy()

    cstk = np.zeros((128, 8), np.float32)
    cstk[:, 0] = 1e-5
    cstk[:, 1] = RMS_EPS
    cstk[:, 2] = 1.0
    cstk[:, 3] = -1e-6
    consts = {
        "cstk": cstk,
        "tabs_xpos": np.stack([c1, s1, c2, s2]),
        "tabs_ropeI": np.stack([cosI, sinI]),
        "cstf": cstf,
        "cstb": cstb,
    }
    return w, wsrc, vec, consts


def x_to_fm(xseq):
    out = np.zeros((NCH, 128, TPAD), np.float32)
    out[:, :, 1:T + 1] = np.ascontiguousarray(xseq.T).reshape(NCH, 128, T)
    return out




class Ctx:
    def __init__(self, nc, es, pipe, wmeta):
        self.nc = nc
        self.es = es
        self.pipe = pipe
        self.wmeta = wmeta
        self.d = {}
        self.sb = {}
        self.ps = {}

    def dram(self, name, shape, dtype, kind="Internal"):
        t = self.nc.dram_tensor(name, shape, dtype, kind=kind)
        self.d[name] = t
        return t

    _uid = 0

    def sbuf(self, name, shape, dtype, scope=None):
        holder = scope if scope is not None else self.es
        Ctx._uid += 1
        t = holder.enter_context(
            self.nc.sbuf_tensor(f"sb{Ctx._uid}_{name}", shape, dtype))
        self.sb[name] = t
        return t

    def psum(self, name, shape, dtype):
        t = self.es.enter_context(self.nc.psum_tensor(name, shape, dtype))
        self.ps[name] = t
        return t

    def wimg_ap(self, boff, n):
        if boff < self.wsplit:
            assert boff + n <= self.wsplit, (boff, n)
            return self.d["wimgA"][boff:boff + n]
        return self.d["wimgB"][boff - self.wsplit:boff - self.wsplit + n]

    def load_wblock(self, st, name, fbk, buf, col=0):
        off0, nk, fws, F = self.wmeta[name]
        fw = fws[fbk]
        boff = off0 + sum(128 * nk * f for f in fws[:fbk])
        n = 128 * nk * fw
        ap = self.wimg_ap(boff, n).rearrange("(p x) -> p x", p=128)
        st.dma("sync", lambda e, buf=buf, ap=ap, n=n, col=col: e.dma_start(
            out=buf[:, col:col + n // 128], in_=ap))
        return nk, fw

    _dbgn = 0

    def dbg(self, name, ap_fn, enable=True):
        if not DEBUG or not enable:
            return
        Ctx._dbgn += 1
        ap0 = ap_fn()
        shape = list(ap0.shape)
        t = self.dram(f"dbg_{name}", shape, ap0.dtype, kind="ExternalOutput")
        st = self.pipe.stage(f"dbg_{name}")
        st.dma("sync", lambda e: e.dma_start(out=t[:], in_=ap_fn()))
        st2 = self.pipe.stage(f"dbgw_{name}", deps=[st])
        st2.op("sync", lambda e: e.nop())

    def x_ap(self, seq, c, t0, n, shift=0):
        return self.d["xst"][seq, c, :, t0 + 1 - shift: t0 + 1 - shift + n]

    def x_strip(self, seq, tc, shift=False):
        xst = self.d["xst"]
        if not shift:
            return xst[seq, :, :, 1 + tc * 512: 1 + (tc + 1) * 512].rearrange(
                "c p t -> p c t")
        return None

    def xf(self, c, w=512):
        return self.sb["XF"][:, c * 512: c * 512 + w]




def emit_xf_load(cx, seq, tc, shift=False, name="", xsrc="xst"):
    st = cx.pipe.stage(f"xfL{name}_{tc}")
    x = cx.d[xsrc][seq] if xsrc == "xst" else cx.d[xsrc]
    if not shift:
        st.dma("sync", lambda e: e.dma_start(
            out=cx.sb["XF"][:].rearrange("p (c t) -> p c t", t=512),
            in_=x[:, :, 1 + tc * 512: 1 + (tc + 1) * 512].rearrange("c p t -> p c t")))
    else:
        st.dma("sync", lambda e: e.dma_start(
            out=cx.sb["XF"][:].rearrange("p (c t) -> p c t", t=512)[:, 0:4, :],
            in_=x[0:4, :, 1 + tc * 512: 1 + (tc + 1) * 512].rearrange("c p t -> p c t")))
        st.dma("sync", lambda e: e.dma_start(
            out=cx.sb["XF"][:].rearrange("p (c t) -> p c t", t=512)[:, 4:8, :],
            in_=x[4:8, :, 0 + tc * 512: 0 + (tc + 1) * 512].rearrange("c p t -> p c t")))
    return st




def mm_floop(cx, prefix, wname, fbs, nk, rhs_fn, evac_fn, tcs, kmax=None):
    pipe = cx.pipe
    P = cx.ps
    WS = (cx.sb["WS0"], cx.sb["WS1"])
    nkk = kmax if kmax is not None else nk
    mm_st = {}
    rd_st = {}
    ld_st = {}
    for i, fb in enumerate(fbs):
        buf = WS[i % 2]
        ld = pipe.stage(f"{prefix}L{i}", deps=[mm_st.get(i - 1)] if i >= 1 else None)
        cx.load_wblock(ld, wname, fb, buf)
        ld_st[i] = ld
        deps = [ld]
        if i >= 2 and rd_st.get(i - 2) is not None:
            deps.append(rd_st[i - 2])
        if i == 0:
            deps = None
        mm = pipe.stage(f"{prefix}M{i}", deps=deps)
        if i == 0:
            mm.deps = None
        g = i % 2
        banks = {tc: P[f"P{g * 2 + j}"] for j, tc in enumerate(tcs)}
        for k in range(nkk):
            for j, tc in enumerate(tcs):
                mm.op("tensor", lambda e, k=k, tc=tc, buf=buf, b=banks[tc], nkk=nkk: e.matmul(
                    b[:, 0:512], buf[:, k * 128:(k + 1) * 128],
                    rhs_fn(k, tc), start=(k == 0), stop=(k == nkk - 1)))
        mm_st[i] = mm
        rd_st[i] = evac_fn(i, fb, banks,
                           lambda nm, deps=None: pipe.stage(f"{prefix}{nm}{i}", deps=deps))
    return mm_st[len(fbs) - 1], rd_st[len(fbs) - 1]


def emit_vec_loads(cx):
    pipe = cx.pipe
    st = pipe.stage("vecs")
    for name, sbname in [("cstf", "cstf"), ("cstb", "cstb"), ("cstk", "cstk"),
                         ("bqkv", "bqkv"),
                         ("bout", "bout"), ("gqk_col", "gqk"), ("l0f_qs", "qsf"), ("l0f_ks", "ksf"),
                         ("l1a_conv", "taps"), ("l1a_db", "dbase"),
                         ("l1a_alog", "alog"), ("l1f_wr", "wrf"),
                         ("l1f_br", "brf")]:
        st.dma("sync", lambda e, name=name, sbname=sbname: e.dma_start(
            out=cx.sb[sbname][:], in_=cx.d[name][:]))
    st2p = pipe.stage("vecs2p")
    st2p.dma("sync", lambda e: e.dma_start(out=cx.sb["XF"][0:1, 0:1024],
                                           in_=cx.d["gout_row"][0:1, :]))
    st2 = pipe.stage("vecs2")
    st2.op("vector", lambda e: e.tensor_copy(cx.sb["goutb"][0:1, :],
                                             cx.sb["XF"][0:1, 0:1024]))
    st2.op("gpsimd", lambda e: e.memset(
        cx.sb["XB"][:].rearrange("p (c t) -> p c t", t=2049)[:, :, 0:1], 0))
    st3 = pipe.stage("vecs3")
    st3.op("vector", lambda e: e.memset(cx.sb["bMR2"][0:2, 0:T], -1.0))


def emit_state_init(cx):
    st = cx.pipe.stage("xinit")
    st.dma("gpsimd", lambda e: e.dma_start(out=cx.d["xst"][:], in_=cx.d["xin"][:]))


def emit_prepass(cx, nelem):
    pipe = cx.pipe
    CH = 128 * 4096
    fb = [cx.sb["ppf0"], cx.sb["ppf1"]]
    bb = [cx.sb["WS0"], cx.sb["WS1"]]
    prev = [None, None]
    wsrc = cx.d["wsrc"]
    i = 0
    for seg, seglen in (("wimgA", cx.wsplit), ("wimgB", nelem - cx.wsplit)):
        base = 0 if seg == "wimgA" else cx.wsplit
        wseg = cx.d[seg]
        nsl = (seglen + CH - 1) // CH
        for j in range(nsl):
            o = j * CH
            n = min(CH, seglen - o)
            npart = n // 128
            assert n % 128 == 0
            b = i % 2
            ld = pipe.stage(f"ppL{i}", deps=[prev[b]] if prev[b] else None)
            ld.dma("sync", lambda e, b=b, o=o, n=n, npart=npart, base=base: e.dma_start(
                out=fb[b][:, 0:npart],
                in_=wsrc[base + o: base + o + n].rearrange("(p x) -> p x", p=128)))
            cv = pipe.stage(f"ppC{i}", deps=[ld])
            cv.op("vector", lambda e, b=b, npart=npart: e.tensor_copy(
                bb[b][:, 0:npart], fb[b][:, 0:npart]))
            stg = pipe.stage(f"ppS{i}", deps=[cv])
            stg.dma("gpsimd", lambda e, b=b, o=o, n=n, npart=npart, wseg=wseg: e.dma_start(
                out=wseg[o:o + n].rearrange("(p x) -> p x", p=128),
                in_=bb[b][:, 0:npart]))
            prev[b] = stg
            i += 1


def emit_ln(cx, seq, label, dst_fn, shift=False, gb_idx=None, out_f32_fn=None,
            post_f32=None, xsrc="xst"):
    pipe = cx.pipe
    P4, P5 = cx.ps["P4"], cx.ps["P5"]
    xbt, sqt = cx.sb["xbt"], cx.sb["sqt"]
    scA = cx.sb["scA"]
    cb = cx.sb["cstb"]
    invD = cb[:, 256:257]
    onesb = cb[:, 0:128]
    sMU, sMSQ, sVS, sMR = (cx.sb["sMU"], cx.sb["sMSQ"], cx.sb["sVS"], cx.sb["sMR"])
    bRS, bMR2 = cx.sb["bRS"], cx.sb["bMR2"]

    if gb_idx is not None:
        g0 = pipe.stage(f"{label}gb")
        g0.dma("sync", lambda e: e.dma_start(
            out=cx.sb["XF"][0:1, 0:1024], in_=cx.d["l0f_g"][gb_idx:gb_idx + 1, :]))
        g0.dma("sync", lambda e: e.dma_start(
            out=cx.sb["XF"][1:2, 0:1024], in_=cx.d["l0f_b"][gb_idx:gb_idx + 1, :]))
        g1 = pipe.stage(f"{label}gb2")
        g1.op("vector", lambda e: e.tensor_copy(cx.sb["gb2b"][0:2, :],
                                                cx.sb["XF"][0:2, 0:1024]))

    for tc in range(4):
        ldx = emit_xf_load(cx, seq, tc, shift=shift, name=label, xsrc=xsrc)
        for c in range(NCH):
            s1 = pipe.stage(f"{label}a{tc}_{c}")
            s1.op("vector", lambda e, c=c: e.tensor_copy(xbt[:, 0:512], cx.xf(c)))
            s1.op("scalar", lambda e, c=c: e.activation(sqt[:, 0:512], cx.xf(c), AF.Square))
            s2 = pipe.stage(f"{label}b{tc}_{c}")
            s2.op("tensor", lambda e, c=c: e.matmul(
                P4[0:1, 0:512], invD, xbt[:, 0:512], start=(c == 0), stop=(c == 7)))
            s2.op("tensor", lambda e, c=c: e.matmul(
                P5[0:1, 0:512], invD, sqt[:, 0:512], start=(c == 0), stop=(c == 7)))
        s3 = pipe.stage(f"{label}c{tc}")
        s3.op("vector", lambda e: e.tensor_copy(sMU[0:1, 0:512], P4[0:1, 0:512]))
        s3.op("scalar", lambda e: e.copy(sMSQ[0:1, 0:512], P5[0:1, 0:512]))
        st = pipe.stage(f"{label}v1{tc}")
        st.op("vector", lambda e: e.scalar_tensor_tensor(
            sVS[0:1, 0:512], sMU[0:1, 0:512], -1.0, sMU[0:1, 0:512], ALU.mult, ALU.mult))
        st = pipe.stage(f"{label}v2{tc}")
        st.op("vector", lambda e: e.tensor_tensor(
            sVS[0:1, 0:512], sVS[0:1, 0:512], sMSQ[0:1, 0:512], ALU.add))
        st = pipe.stage(f"{label}v3{tc}")
        st.op("scalar", lambda e: e.activation(sVS[0:1, 0:512], sVS[0:1, 0:512], AF.Ln, bias=1e-5))
        st = pipe.stage(f"{label}v4{tc}")
        st.op("scalar", lambda e: e.activation(sVS[0:1, 0:512], sVS[0:1, 0:512], AF.Exp, scale=-0.5))
        st = pipe.stage(f"{label}v5{tc}")
        st.op("vector", lambda e: e.tensor_tensor(
            sMR[0:1, 0:512], sMU[0:1, 0:512], sVS[0:1, 0:512], ALU.mult))
        st = pipe.stage(f"{label}v6{tc}")
        st.op("vector", lambda e, tc=tc: e.tensor_copy(
            bRS[0:1, tc * 512:(tc + 1) * 512], sVS[0:1, 0:512]))
        st.op("scalar", lambda e, tc=tc: e.copy(
            bMR2[0:1, tc * 512:(tc + 1) * 512], sMR[0:1, 0:512]))

    for tc in range(4):
        ldx = emit_xf_load(cx, seq, tc, shift=shift, name=label + "n", xsrc=xsrc)
        if gb_idx is None:
            bc = pipe.stage(f"{label}d{tc}")
            bc.op("tensor", lambda e, tc=tc: e.matmul(
                P4[:, 0:512], onesb[0:1, :], bRS[0:1, tc * 512:(tc + 1) * 512],
                start=True, stop=True))
            bc.op("tensor", lambda e, tc=tc: e.matmul(
                P5[:, 0:512], onesb[0:1, :], bMR2[0:1, tc * 512:(tc + 1) * 512],
                start=True, stop=True))
            for c in range(NCH):
                n1 = pipe.stage(f"{label}e{tc}_{c}")
                n1.op("vector", lambda e, c=c: e.tensor_tensor(
                    scA[:, 0:512], cx.xf(c), P4[:, 0:512], ALU.mult))
                n2 = pipe.stage(f"{label}f{tc}_{c}")
                n2.op("vector", lambda e, c=c, tc=tc: e.tensor_tensor(
                    dst_fn(c, tc), scA[:, 0:512], P5[:, 0:512], ALU.subtract))
                if out_f32_fn is not None:
                    n3 = pipe.stage(f"{label}g{tc}_{c}")
                    n3.op("vector", lambda e, c=c, tc=tc: e.tensor_tensor(
                        out_f32_fn(c, tc), scA[:, 0:512], P5[:, 0:512], ALU.subtract))
                    if post_f32 is not None:
                        post_f32(c, tc)
        else:
            for c in range(NCH):
                bc = pipe.stage(f"{label}d{tc}_{c}")
                bc.op("tensor", lambda e, c=c, tc=tc: e.matmul(
                    P4[:, 0:512], cx.sb["gb2b"][0:1, c * 128:(c + 1) * 128],
                    bRS[0:1, tc * 512:(tc + 1) * 512], start=True, stop=True))
                bc.op("tensor", lambda e, c=c, tc=tc: e.matmul(
                    P5[:, 0:512], cx.sb["gb2b"][0:2, c * 128:(c + 1) * 128],
                    bMR2[0:2, tc * 512:(tc + 1) * 512], start=True, stop=True))
                n1 = pipe.stage(f"{label}e{tc}_{c}")
                n1.op("vector", lambda e, c=c: e.tensor_tensor(
                    scA[:, 0:512], cx.xf(c), P4[:, 0:512], ALU.mult))
                n2 = pipe.stage(f"{label}f{tc}_{c}")
                n2.op("vector", lambda e, c=c, tc=tc: e.tensor_tensor(
                    dst_fn(c, tc), scA[:, 0:512], P5[:, 0:512], ALU.subtract))


def emit_l0a(cx, seq):
    pipe = cx.pipe
    P = cx.ps
    XB = cx.sb["XB"]
    cb = cx.sb["cstb"]
    cstf = cx.sb["cstf"]
    onesb = cb[:, 0:128]
    idb = cb[:, 128:256]
    invD = cb[:, 256:257]
    WS0, WS1 = cx.sb["WS0"], cx.sb["WS1"]
    scA, scB = cx.sb["scA"], cx.sb["scB"]
    sqt = cx.sb["sqt"]
    sVS, bRS = cx.sb["sVS"], cx.sb["bRS"]

    def xbsl(c, tc, w=512):
        return XB[:, c * 2049 + 1 + tc * 512: c * 2049 + 1 + tc * 512 + w]

    with contextlib.ExitStack() as ph:
        QB = cx.sbuf("QB", [128, NCH * T], BF16, ph)
        KB = cx.sbuf("KB", [128, NCH * T], BF16, ph)
        VB = cx.sbuf("VB", [128, NCH * T], BF16, ph)
        TB2 = cx.sbuf("TB2", [128, 2 * T], BF16, ph)
        vkT = cx.sbuf("vkT", [128, 8 * 66], BF16, ph)
        ttile = cx.sbuf("ttile", [128, 16 * 65 + NCH * 128], BF16, ph)
        vkb = cx.sbuf("vkb", [128, 64], BF16, ph)
        den = scA
        rb16 = cx.sb["xbt"]
        bcb = cx.sb["sqt"]
        PTg = ttile[:, 0:NCH * 128]
        KOFF = 16 * 65

        st = pipe.stage("l0a_tbl")
        st.dma("sync", lambda e: e.dma_start(
            out=cx.sb["XF"][:, 0:T], in_=cx.d["ropeI"][0]))
        st.dma("sync", lambda e: e.dma_start(
            out=cx.sb["XF"][:, T:2 * T], in_=cx.d["ropeI"][1]))
        st2 = pipe.stage("l0a_tbl2")
        st2.op("vector", lambda e: e.tensor_copy(TB2[:, 0:2 * T], cx.sb["XF"][:, 0:4096]))

        emit_ln(cx, seq, f"A{seq}ln", xbsl)

        cx.dbg(f"ln{seq}", lambda: XB[:, 0:NCH * 2049], seq == 0)
        if DEBUG and seq == 0:
            for _fb in (0, 4, 5):
                cx.dbg(f"wimgq{_fb}", (lambda _fb=_fb: cx.wimg_ap(
                    cx.wmeta["l0a_qkv"][0] + _fb * 131072, 131072).rearrange(
                        "(p x) -> p x", p=128)))
        pmm = {}
        prd = {}
        idx = 0
        pld = {}
        for half in range(2):
            for fbk in range(24):
                buf = WS0 if idx % 2 == 0 else WS1
                ldd = (None if idx == 0 else
                       [pld[0]] if idx == 1 else [pmm[idx - 2]])
                ld = pipe.stage(f"A{seq}qL{half}_{fbk}", deps=ldd)
                cx.load_wblock(ld, "l0a_qkv", fbk, buf)
                pld[idx] = ld
                mmd = None if idx == 0 else [ld] + ([prd[idx - 2]] if idx >= 2 else [])
                mm = pipe.stage(f"A{seq}qM{half}_{fbk}", deps=mmd)
                g = idx % 2
                for k in range(8):
                    for j in range(2):
                        tc = half * 2 + j
                        mm.op("tensor", lambda e, k=k, j=j, tc=tc, buf=buf, g=g: e.matmul(
                            P[f"P{g * 2 + j}"][:, 0:512], buf[:, k * 128:(k + 1) * 128],
                            xbsl(k, tc), start=(k == 0), stop=(k == 7)))
                ev = pipe.stage(f"A{seq}qE{half}_{fbk}", deps=[mm])
                sect, cc = divmod(fbk, 8)
                dst = (QB, KB, VB)[sect]
                for j in range(2):
                    tc = half * 2 + j
                    ev.op("scalar", lambda e, j=j, tc=tc, cc=cc, dst=dst, fbk=fbk, g=g: e.activation(
                        dst[:, cc * T + tc * 512: cc * T + (tc + 1) * 512],
                        P[f"P{g * 2 + j}"][:, 0:512], AF.Identity,
                        bias=cx.sb["bqkv"][:, fbk:fbk + 1]))
                pmm[idx] = mm
                prd[idx] = ev
                idx += 1

        cx.dbg(f"qraw{seq}", lambda: QB[:, :], seq == 0)
        cx.dbg(f"kraw{seq}", lambda: KB[:, :], seq == 0)
        cx.dbg(f"vraw{seq}", lambda: VB[:, :], seq == 0)
        for idx, SRC in ((0, QB), (1, KB)):
            for tc in range(4):
                for c in range(NCH):
                    s1 = pipe.stage(f"A{seq}s{idx}a{tc}_{c}")
                    s1.op("scalar", lambda e, SRC=SRC, c=c, tc=tc: e.activation(
                        sqt[:, 0:512],
                        SRC[:, c * T + tc * 512: c * T + (tc + 1) * 512], AF.Square))
                    s2 = pipe.stage(f"A{seq}s{idx}b{tc}_{c}")
                    s2.op("tensor", lambda e, c=c: e.matmul(
                        P["P4"][0:1, 0:512], invD, sqt[:, 0:512],
                        start=(c == 0), stop=(c == 7)))
                s3 = pipe.stage(f"A{seq}s{idx}c{tc}")
                s3.op("scalar", lambda e: e.activation(
                    sVS[0:1, 0:512], P["P4"][0:1, 0:512], AF.Ln, bias=RMS_EPS))
                s4 = pipe.stage(f"A{seq}s{idx}d{tc}")
                s4.op("scalar", lambda e: e.activation(
                    sVS[0:1, 0:512], sVS[0:1, 0:512], AF.Exp, scale=-0.5))
                s5 = pipe.stage(f"A{seq}s{idx}e{tc}")
                s5.op("vector", lambda e, tc=tc: e.tensor_copy(
                    bRS[0:1, tc * 512:(tc + 1) * 512], sVS[0:1, 0:512]))
            for c in range(NCH):
                stp = pipe.stage(f"A{seq}s{idx}g{c}")
                stp.op("vector", lambda e, idx=idx, c=c: e.tensor_scalar(
                    PTg[:, c * 128:(c + 1) * 128],
                    cstf[:, 0:128],
                    cx.sb["gqk"][:, idx * NCH + c:idx * NCH + c + 1], None, ALU.mult))
            for c in range(NCH):
                for tc in range(4):
                    sl = slice(c * T + tc * 512, c * T + (tc + 1) * 512)
                    m1 = pipe.stage(f"A{seq}r{idx}a{c}_{tc}")
                    m1.op("tensor", lambda e, SRC=SRC, sl=sl, c=c: e.matmul(
                        P["P4"][:, 0:512],
                        PTg[:, c * 128:(c + 1) * 128],
                        SRC[:, sl], start=True, stop=True))
                    m1.op("vector", lambda e, SRC=SRC, sl=sl, idx=idx, c=c, tc=tc: e.scalar_tensor_tensor(
                        scA[:, 0:512], SRC[:, sl],
                        cx.sb["gqk"][:, idx * NCH + c:idx * NCH + c + 1],
                        TB2[:, tc * 512:(tc + 1) * 512], ALU.mult, ALU.mult))
                    m2 = pipe.stage(f"A{seq}r{idx}b{c}_{tc}")
                    m2.op("vector", lambda e, tc=tc: e.tensor_tensor(
                        scB[:, 0:512], P["P4"][:, 0:512],
                        TB2[:, T + tc * 512: T + (tc + 1) * 512], ALU.mult))
                    m3 = pipe.stage(f"A{seq}r{idx}c{c}_{tc}")
                    m3.op("vector", lambda e: e.tensor_tensor(
                        scA[:, 0:512], scA[:, 0:512], scB[:, 0:512], ALU.add))
                    m4 = pipe.stage(f"A{seq}r{idx}d{c}_{tc}")
                    m4.op("tensor", lambda e, tc=tc: e.matmul(
                        P["P4"][:, 0:512], onesb[0:1, :],
                        bRS[0:1, tc * 512:(tc + 1) * 512], start=True, stop=True))
                    m5 = pipe.stage(f"A{seq}r{idx}e{c}_{tc}")
                    m5.op("vector", lambda e: e.tensor_tensor(
                        scA[:, 0:512], scA[:, 0:512], P["P4"][:, 0:512], ALU.mult))
                    m6 = pipe.stage(f"A{seq}r{idx}f{c}_{tc}")
                    m6.op("vector", lambda e, SRC=SRC, sl=sl: e.tensor_relu(
                        SRC[:, sl], scA[:, 0:512]))

        cx.dbg(f"qf{seq}", lambda: QB[:, :], seq == 0)
        cx.dbg(f"kf{seq}", lambda: KB[:, :], seq == 0)
        stm = pipe.stage(f"A{seq}ones")
        stm.op("gpsimd", lambda e: e.memset(
            ttile[:, 0:16 * 65].rearrange("p (h w) -> p h w", w=65)[:, :, 64:65],
            1.0))
        for tt in range(16):
            trs = pipe.stage(f"A{seq}vT{tt}")
            for c in range(NCH):
                trs.op("tensor", lambda e, c=c, tt=tt: e.transpose(
                    P["PT"][:, c * 128:(c + 1) * 128],
                    KB[:, c * T + tt * 128: c * T + (tt + 1) * 128], idb))
            tre = pipe.stage(f"A{seq}vU{tt}")
            tre.op("vector", lambda e: e.tensor_copy(
                ttile[:, KOFF:KOFF + NCH * 128], P["PT"][:, 0:NCH * 128]))
            trs2 = pipe.stage(f"A{seq}vV{tt}")
            for c in range(NCH):
                trs2.op("tensor", lambda e, c=c, tt=tt: e.transpose(
                    P["PT"][:, c * 128:(c + 1) * 128],
                    VB[:, c * T + tt * 128: c * T + (tt + 1) * 128], idb))
            tre2 = pipe.stage(f"A{seq}vW{tt}")
            for h in range(16):
                hc, hp = divmod(h, 2)
                tre2.op("vector", lambda e, hc=hc, hp=hp, h=h: e.tensor_copy(
                    ttile[:, h * 65:h * 65 + 64],
                    P["PT"][:, hc * 128 + hp * 64: hc * 128 + hp * 64 + 64]))
            mm = pipe.stage(f"A{seq}vM{tt}")
            for h in range(16):
                bank = P["P4"] if h < 8 else P["P5"]
                hh = h % 8
                mm.op("tensor", lambda e, h=h, hh=hh, bank=bank, tt=tt: e.matmul(
                    bank[0:65, hh * 64:(hh + 1) * 64],
                    ttile[:, h * 65:h * 65 + 65],
                    ttile[:, KOFF + h * 64: KOFF + (h + 1) * 64],
                    start=(tt == 0 and hh == 0), stop=(tt == 15 and hh == 7)))

        for h in range(16):
            bank = P["P4"] if h < 8 else P["P5"]
            hh = h % 8
            base = 64 * (h % 2)
            c = h // 2
            ev = pipe.stage(f"A{seq}kE{h}")
            ev.op("scalar", lambda e, bank=bank, hh=hh: e.activation(
                vkb[0:65, 0:64], bank[0:65, hh * 64:(hh + 1) * 64], AF.Identity))
            tr = pipe.stage(f"A{seq}kF{h}")
            tr.op("tensor", lambda e, base=base: e.transpose(
                P["PT"][base:base + 64, 0:65], vkb[0:65, 0:64], idb[0:65, 0:65]))
            ev2 = pipe.stage(f"A{seq}kG{h}")
            ev2.op("vector", lambda e, base=base, c=c: e.tensor_copy(
                vkT[base:base + 64, c * 66:c * 66 + 65],
                P["PT"][base:base + 64, 0:65]))

        cx.dbg(f"vkT{seq}", lambda: vkT[:, :], seq == 0)
        for tc in range(4):
            for h in range(16):
                base = 64 * (h % 2)
                c = h // 2
                qsl = slice(c * T + tc * 512, c * T + (tc + 1) * 512)
                mm = pipe.stage(f"A{seq}xM{tc}_{h}")
                if base == 0:
                    mm.op("tensor", lambda e, c=c, qsl=qsl: e.matmul(
                        P["P6"][0:65, 0:512], vkT[0:64, c * 66:c * 66 + 65],
                        QB[0:64, qsl], start=True, stop=True))
                else:
                    mm.op("tensor", lambda e, c=c, qsl=qsl: e.matmul(
                        P["P6"][64:128, 0:512], vkT[64:128, c * 66:c * 66 + 64],
                        QB[64:128, qsl], start=True, stop=True))
                    mm.op("tensor", lambda e, c=c, qsl=qsl: e.matmul(
                        P["P0"][0:1, 0:512], vkT[64:128, c * 66 + 64:c * 66 + 65],
                        QB[64:128, qsl], start=True, stop=True))
                dsrc = 64 if base == 0 else 0
                r1 = pipe.stage(f"A{seq}xR{tc}_{h}")
                if base == 0:
                    r1.op("vector", lambda e: e.tensor_scalar(
                        den[64:65, 0:512], P["P6"][64:65, 0:512], 1e-6, None, ALU.add))
                else:
                    r1.op("vector", lambda e: e.tensor_scalar(
                        den[0:1, 0:512], P["P0"][0:1, 0:512], 1e-6, None, ALU.add))
                r2 = pipe.stage(f"A{seq}xS{tc}_{h}")
                r2.op("vector", lambda e, dsrc=dsrc: e.reciprocal(
                    den[dsrc:dsrc + 1, 0:512], den[dsrc:dsrc + 1, 0:512]))
                r3 = pipe.stage(f"A{seq}xT{tc}_{h}")
                r3.op("vector", lambda e, dsrc=dsrc: e.tensor_copy(
                    rb16[dsrc:dsrc + 1, 0:512], den[dsrc:dsrc + 1, 0:512]))
                r4 = pipe.stage(f"A{seq}xU{tc}_{h}")
                r4.op("tensor", lambda e, base=base, dsrc=dsrc: e.matmul(
                    P["P5"][base:base + 64, 0:512], onesb[dsrc:dsrc + 1, 0:64],
                    rb16[dsrc:dsrc + 1, 0:512], start=True, stop=True))
                r5 = pipe.stage(f"A{seq}xV{tc}_{h}")
                r5.op("scalar", lambda e, base=base: e.activation(
                    bcb[base:base + 64, 0:512], P["P5"][base:base + 64, 0:512],
                    AF.Identity))
                r6 = pipe.stage(f"A{seq}xW{tc}_{h}")
                nsrc = 0 if base == 0 else 64
                r6.op("vector", lambda e, base=base, c=c, tc=tc, nsrc=nsrc: e.tensor_tensor(
                    XB[base:base + 64,
                       c * 2049 + 1 + tc * 512: c * 2049 + 1 + (tc + 1) * 512],
                    P["P6"][nsrc:nsrc + 64, 0:512], bcb[base:base + 64, 0:512],
                    ALU.mult))

        cx.dbg(f"attn{seq}", lambda: XB[:, 0:NCH * 2049], seq == 0)
        pld = {}
        pmm = {}
        pev = {}
        pmm2 = {}
        pb23 = {}
        pb4 = {}
        lastb23 = None
        for tc in range(4):
            for fbk in range(8):
                idx = tc * 8 + fbk
                buf = WS0 if idx % 2 == 0 else WS1
                bank = P[f"P{idx % 2}"]
                sq = VB[:, (8 + idx % 2) * 512:(9 + idx % 2) * 512]
                ldd = (None if idx == 0 else
                       [pld[0]] if idx == 1 else [pmm[idx - 2]])
                ld = pipe.stage(f"A{seq}wL{tc}_{fbk}", deps=ldd)
                cx.load_wblock(ld, "l0a_out", fbk, buf)
                pld[idx] = ld
                mmd = None if idx == 0 else [ld] + \
                    ([pev[idx - 2]] if idx >= 2 else [])
                if fbk < 2 and lastb23 is not None:
                    mmd = mmd + [lastb23]
                mm = pipe.stage(f"A{seq}wM{tc}_{fbk}", deps=mmd)
                for k in range(8):
                    mm.op("tensor", lambda e, k=k, tc=tc, buf=buf, bank=bank: e.matmul(
                        bank[:, 0:512], buf[:, k * 128:(k + 1) * 128],
                        XB[:, k * 2049 + 1 + tc * 512: k * 2049 + 1 + (tc + 1) * 512],
                        start=(k == 0), stop=(k == 7)))
                pmm[idx] = mm
                ev = pipe.stage(f"A{seq}wE{tc}_{fbk}", deps=[mm])
                ev.op("scalar", lambda e, fbk=fbk, bank=bank: e.activation(
                    VB[:, fbk * 512:(fbk + 1) * 512], bank[:, 0:512],
                    AF.Identity, bias=cx.sb["bout"][:, fbk:fbk + 1]))
                ev.op("scalar", lambda e, fbk=fbk, bank=bank, sq=sq: e.activation(
                    sq, bank[:, 0:512], AF.Square,
                    bias=cx.sb["bout"][:, fbk:fbk + 1]))
                pev[idx] = ev
                if fbk >= 1:
                    mm2 = pipe.stage(f"A{seq}wS{tc}_{fbk - 1}",
                                     deps=[pev[idx - 1]])
                    sqp = VB[:, (8 + (idx - 1) % 2) * 512:(9 + (idx - 1) % 2) * 512]
                    mm2.op("tensor", lambda e, fbk=fbk, sqp=sqp: e.matmul(
                        P["P4"][0:1, 0:512], invD, sqp,
                        start=(fbk - 1 == 0), stop=False))
            mm2 = pipe.stage(f"A{seq}wS{tc}_7", deps=[pev[tc * 8 + 7]])
            sqp = VB[:, (8 + 7 % 2) * 512:(9 + 7 % 2) * 512]
            mm2.op("tensor", lambda e, sqp=sqp: e.matmul(
                P["P4"][0:1, 0:512], invD, sqp, start=False, stop=True))
            s1 = pipe.stage(f"A{seq}o1{tc}")
            s1.op("scalar", lambda e: e.activation(
                sVS[0:1, 0:512], P["P4"][0:1, 0:512], AF.Ln, bias=RMS_EPS))
            s2 = pipe.stage(f"A{seq}o2{tc}")
            s2.op("scalar", lambda e: e.activation(
                sVS[0:1, 0:512], sVS[0:1, 0:512], AF.Exp, scale=-0.5))
            s3 = pipe.stage(f"A{seq}o3{tc}")
            s3.op("vector", lambda e: e.tensor_copy(bRS[0:1, 0:512], sVS[0:1, 0:512]))
            ldx = emit_xf_load(cx, seq, tc, name=f"A{seq}res")
            for c in range(NCH):
                gbank = P[f"P{4 + c % 2}"]
                scX = scA if c % 2 == 0 else scB
                b1d = None if c == 0 else ([pb23[c - 2]] if c >= 2 else [pb23[0] if 0 in pb23 else None])
                if c == 1:
                    b1d = None
                b1 = pipe.stage(f"A{seq}p{tc}_{c}", deps=b1d)
                b1.op("tensor", lambda e, c=c, gbank=gbank: e.matmul(
                    gbank[:, 0:512], cx.sb["goutb"][0:1, c * 128:(c + 1) * 128],
                    bRS[0:1, 0:512], start=True, stop=True))
                b23d = [b1] + ([pb4[c - 2]] if c >= 2 else [])
                b23 = pipe.stage(f"A{seq}q{tc}_{c}", deps=b23d)
                b23.op("vector", lambda e, c=c, gbank=gbank, scX=scX: e.tensor_tensor(
                    scX[:, 0:512], VB[:, c * 512:(c + 1) * 512], gbank[:, 0:512],
                    ALU.mult))
                b23.op("vector", lambda e, c=c, scX=scX: e.tensor_tensor(
                    scX[:, 0:512], scX[:, 0:512], cx.xf(c), ALU.add))
                pb23[c] = b23
                b4 = pipe.stage(f"A{seq}s{tc}_{c}")
                b4.dma("sync", lambda e, c=c, tc=tc, scX=scX: e.dma_start(
                    out=cx.x_ap(seq, c, tc * 512, 512), in_=scX[:, 0:512]))
                pb4[c] = b4
            lastb23 = pb23[7]




PARTS_ALL = ("l0a", "l0f", "l1a", "l1f")


def build_program(wmeta, n_wsrc, parts=PARTS_ALL, prepass=True):
    nc = bass.Bass("TRN2", target_bir_lowering=False)
    es = contextlib.ExitStack()
    pipe = Pipe(nc, es)
    cx = Ctx(nc, es, pipe, wmeta)

    cx.dram("wsrc", [n_wsrc], F32, kind="ExternalInput")
    wsplit = wmeta["l1a_in"][0]
    assert wsplit % 512 == 0, wsplit
    cx.wsplit = wsplit
    cx.dram("wimgA", [wsplit], BF16, kind="Internal")
    cx.dram("wimgB", [n_wsrc - wsplit], BF16, kind="Internal")
    cx.dram("xin", [2, NCH, 128, TPAD], F32, kind="ExternalInput")
    cx.dram("xst", [2, NCH, 128, TPAD], F32, kind="Internal")
    cx.dram("xst2", [NCH, 128, TPAD], F32, kind="Internal")
    cx.dram("gdram", [NCH, 128, T], BF16, kind="Internal")
    cx.dram("ddram", [NCH, 128, T], BF16, kind="Internal")
    cx.dram("ygdram", [NCH, 128, T], BF16, kind="Internal")
    cx.dram("bcrow", [2, 16, T], BF16, kind="Internal")
    cx.dram("mdram", [8, T], BF16, kind="Internal")
    cx.dram("xout", [2, NCH, 128, T], F32, kind="ExternalOutput")
    cx.dram("ropeI", [2, 128, T], F32, kind="ExternalInput")
    cx.dram("xpos", [4, 128, T], F32, kind="ExternalInput")
    for name, shape in [("cstf", [128, 648]), ("bqkv", [128, 24]),
                        ("bout", [128, 8]), ("gqk_col", [128, 16]),
                        ("gout_row", [1, 1024]),
                        ("l0f_qs", [128, 6]), ("l0f_ks", [128, 6]),
                        ("l0f_g", [6, 1024]), ("l0f_b", [6, 1024]),
                        ("l0f_b1", [6, 128, 44]), ("l0f_b2", [6, 128, 8]),
                        ("l1a_conv", [128, 24]), ("l1a_db", [128, 8]),
                        ("l1a_alog", [128, 128]), ("l1f_wr", [128, 64]),
                        ("l1f_br", [8, 1]), ("l1f_b1", [8, 128, 32]),
                        ("l1f_b2", [8, 1024])]:
        cx.dram(name, shape, F32, kind="ExternalInput")
    cx.dram("cstb", [128, 392], BF16, kind="ExternalInput")
    cx.dram("cstk", [128, 8], F32, kind="ExternalInput")

    cx.sbuf("XB", [128, NCH * 2049], BF16)
    cx.sbuf("XF", [128, 4096], F32)
    cx.sbuf("WS0", [128, 4096], BF16)
    cx.sbuf("WS1", [128, 4096], BF16)
    cx.sbuf("cstf", [128, 648], F32)
    cx.sbuf("cstk", [128, 8], F32)
    cx.sbuf("cstb", [128, 392], BF16)
    cx.sbuf("scA", [128, 512], F32)
    cx.sbuf("scB", [128, 512], F32)
    cx.sbuf("xbt", [128, 512], BF16)
    cx.sbuf("sqt", [128, 512], BF16)
    cx.sbuf("sMU", [1, 512], F32)
    cx.sbuf("sMSQ", [1, 512], F32)
    cx.sbuf("sVS", [1, 512], F32)
    cx.sbuf("sMR", [1, 512], F32)
    cx.sbuf("bRS", [1, T], BF16)
    cx.sbuf("bMR2", [2, T], BF16)
    cx.sbuf("gb2b", [2, 1024], BF16)
    cx.sbuf("bqkv", [128, 24], F32)
    cx.sbuf("bout", [128, 8], F32)
    cx.sbuf("gqk", [128, 16], F32)
    cx.sbuf("goutb", [1, 1024], BF16)
    cx.sbuf("qsf", [128, 6], F32)
    cx.sbuf("ksf", [128, 6], F32)
    cx.sbuf("taps", [128, 24], F32)
    cx.sbuf("dbase", [128, 8], F32)
    cx.sbuf("alog", [128, 128], F32)
    cx.sbuf("wrf", [128, 64], F32)
    cx.sbuf("brf", [8, 1], F32)
    for i in range(7):
        cx.psum(f"P{i}", [128, 512], F32)
    cx.psum("PT", [128, 1024], BF16)
    for j, v in ((0, 1e-5), (1, RMS_EPS), (2, 1.0), (3, -1e-6)):
        nc.const_aps.register(F32, v, cx.sb["cstk"][:, j:j + 1])

    emit_vec_loads(cx)
    emit_state_init(cx)
    if prepass:
        with contextlib.ExitStack() as pps:
            cx.sbuf("ppf0", [128, 4096], F32, pps)
            cx.sbuf("ppf1", [128, 4096], F32, pps)
            emit_prepass(cx, n_wsrc)

    for seq in range(2):
        if "l0a" in parts:
            emit_l0a(cx, seq)
        if "l0f" in parts:
            emit_l0f(cx, seq)
        if "l1a" in parts:
            emit_l1a(cx, seq)
        if "l1f" in parts:
            emit_l1f(cx, seq)

    fin = pipe.stage("xoutc")
    fin.dma("gpsimd", lambda e: e.dma_start(
        out=cx.d["xout"][:], in_=cx.d["xst"][:, :, :, 1:T + 1]))
    done = pipe.stage("done", deps=[fin])
    done.op("sync", lambda e: e.nop())

    pipe.finalize()
    es.close()
    return nc


_UNUSED = 0



def emit_l0f(cx, seq):
    pipe = cx.pipe
    P = cx.ps
    XB = cx.sb["XB"]
    cb = cx.sb["cstb"]
    cstf = cx.sb["cstf"]
    idb = cb[:, 128:256]
    EH = cb[:, 259:261]
    ETsel = cb[0:2, 264:392]
    WS0, WS1 = cx.sb["WS0"], cx.sb["WS1"]
    scA, scB = cx.sb["scA"], cx.sb["scB"]
    sqt = cx.sb["sqt"]
    XF = cx.sb["XF"]
    maskUT = cstf[:, 256:384]
    maskLT = cstf[:, 384:512]

    def xbsl(c, tc, w=512):
        return XB[:, c * 2049 + 1 + tc * 512: c * 2049 + 1 + tc * 512 + w]

    with contextlib.ExitStack() as ph:
        QB = cx.sbuf("QB", [128, NCH * T], BF16, ph)
        KB = cx.sbuf("KB", [128, NCH * T], BF16, ph)
        VB = cx.sbuf("VB", [128, NCH * T], BF16, ph)
        tblC = cx.sbuf("tblC", [128, T], BF16, ph)
        tblS = cx.sbuf("tblS", [128, T], BF16, ph)
        vtm = cx.sbuf("vtm", [128, 3 * 1024], BF16, ph)
        pnT = cx.sbuf("pnT", [128, 384], BF16, ph)
        vb1 = cx.sbuf("vb1", [128, 44], F32, ph)
        vb2 = cx.sbuf("vb2", [128, 8], F32, ph)
        xft = cx.sbuf("xft", [128, 512], F32, ph)
        scS = scA[:, 0:384]
        pn = sqt[:, 0:384]
        mxc = scB[:, 0:2]
        rinvb = cx.sb["xbt"][0:2, 0:512]
        PTg = vtm[:, 0:NCH * 128]
        atm = cx.sb["xbt"][:, 0:512]
        gtm = sqt[:, 0:512]
        gtm2 = scB[:, 0:256].bitcast(BF16)

        def out_dma(c, tc):
            st = pipe.stage(f"G{seq}od{c}_{tc}")
            st.dma("sync", lambda e, c=c, tc=tc: e.dma_start(
                out=cx.d["xst2"][c, :, 1 + tc * 512: 1 + (tc + 1) * 512],
                in_=scB[:, 0:512]))

        emit_ln(cx, seq, f"G{seq}ln", xbsl, out_f32_fn=lambda c, tc: scB[:, 0:512],
                post_f32=out_dma)
        stz = pipe.stage(f"G{seq}zp")
        stz.op("vector", lambda e: e.memset(scA[:, 0:8], 0.0))
        stz2 = pipe.stage(f"G{seq}zp2")
        for c in range(NCH):
            stz2.dma("sync", lambda e, c=c: e.dma_start(
                out=cx.d["xst2"][c, :, 0:1], in_=scA[:, c:c + 1],
                allow_slow_non_contiguous=True))

        cx.dbg(f"x2init{seq}", lambda: cx.d["xst2"][:, :, :], seq == 0)
        for li in range(6):
            for tc in range(4):
                emit_xf_load(cx, seq, tc, shift=True, name=f"G{seq}L{li}c{tc}",
                             xsrc="xst2")
                cp = pipe.stage(f"G{seq}L{li}cc{tc}")
                for c in range(NCH):
                    cp.op("vector", lambda e, c=c, tc=tc: e.tensor_copy(
                        xbsl(c, tc), cx.xf(c)))
            cx.dbg(f"xsh{li}", lambda: XB[:, 0:NCH * 2049], seq == 0 and li == 0)
            pmm = {}
            prd = {}
            pld = {}
            idx = 0
            for half in range(2):
                for fbk in range(24):
                    buf = WS0 if idx % 2 == 0 else WS1
                    ldd = (None if idx == 0 else
                           [pld[0]] if idx == 1 else [pmm[idx - 2]])
                    ld = pipe.stage(f"G{seq}L{li}qL{half}_{fbk}", deps=ldd)
                    cx.load_wblock(ld, f"l0f_qkv{li}", fbk, buf)
                    pld[idx] = ld
                    mmd = None if idx == 0 else [ld] + ([prd[idx - 2]] if idx >= 2 else [])
                    mm = pipe.stage(f"G{seq}L{li}qM{half}_{fbk}", deps=mmd)
                    g = idx % 2
                    for k in range(8):
                        for j in range(2):
                            tc = half * 2 + j
                            mm.op("tensor", lambda e, k=k, j=j, tc=tc, buf=buf, g=g: e.matmul(
                                P[f"P{g * 2 + j}"][:, 0:512], buf[:, k * 128:(k + 1) * 128],
                                xbsl(k, tc), start=(k == 0), stop=(k == 7)))
                    ev = pipe.stage(f"G{seq}L{li}qE{half}_{fbk}", deps=[mm])
                    sect, cc = divmod(fbk, 8)
                    dst = (QB, KB, VB)[sect]
                    for j in range(2):
                        tc = half * 2 + j
                        ev.op("scalar", lambda e, j=j, tc=tc, cc=cc, dst=dst, g=g: e.activation(
                            dst[:, cc * T + tc * 512: cc * T + (tc + 1) * 512],
                            P[f"P{g * 2 + j}"][:, 0:512], AF.Identity))
                    pmm[idx] = mm
                    prd[idx] = ev
                    idx += 1

            cx.dbg(f"qr{li}", lambda: QB[:, :], seq == 0 and li == 0)
            cx.dbg(f"vr{li}", lambda: VB[:, :], seq == 0 and li == 0)
            for idx, (SRC, scol) in enumerate(((QB, "qsf"), (KB, "ksf"))):
                tl = pipe.stage(f"G{seq}L{li}t{idx}")
                tl.dma("sync", lambda e, idx=idx: e.dma_start(
                    out=XF[:, 0:T], in_=cx.d["xpos"][2 * idx]))
                tl.dma("sync", lambda e, idx=idx: e.dma_start(
                    out=XF[:, T:2 * T], in_=cx.d["xpos"][2 * idx + 1]))
                tl2 = pipe.stage(f"G{seq}L{li}u{idx}")
                tl2.op("vector", lambda e: e.tensor_copy(tblC[:, 0:T], XF[:, 0:T]))
                tl2.op("scalar", lambda e: e.copy(tblS[:, 0:T], XF[:, T:2 * T]))
                pg = pipe.stage(f"G{seq}L{li}pg{idx}")
                for c in range(NCH):
                    pg.op("vector", lambda e, c=c, scol=scol: e.tensor_scalar(
                        PTg[:, c * 128:(c + 1) * 128], cstf[:, 128:256],
                        cx.sb[scol][:, li:li + 1], None, ALU.mult))
                for c in range(NCH):
                    for tc in range(4):
                        sl = slice(c * T + tc * 512, c * T + (tc + 1) * 512)
                        s1 = pipe.stage(f"G{seq}L{li}n{idx}a{c}_{tc}")
                        s1.op("scalar", lambda e, SRC=SRC, sl=sl: e.activation(
                            sqt[:, 0:512], SRC[:, sl], AF.Square))
                        s2 = pipe.stage(f"G{seq}L{li}n{idx}b{c}_{tc}")
                        s2.op("tensor", lambda e: e.matmul(
                            P["P4"][0:2, 0:512], EH, sqt[:, 0:512],
                            start=True, stop=True))
                        s3 = pipe.stage(f"G{seq}L{li}n{idx}c{c}_{tc}")
                        s3.op("vector", lambda e: e.tensor_scalar(
                            scA[0:2, 0:512], P["P4"][0:2, 0:512], 1e-24, None,
                            ALU.max))
                        s4 = pipe.stage(f"G{seq}L{li}n{idx}d{c}_{tc}")
                        s4.op("scalar", lambda e: e.activation(
                            scA[0:2, 0:512], scA[0:2, 0:512], AF.Ln))
                        s5 = pipe.stage(f"G{seq}L{li}n{idx}e{c}_{tc}")
                        s5.op("scalar", lambda e: e.activation(
                            scA[0:2, 0:512], scA[0:2, 0:512], AF.Exp, scale=-0.5))
                        s6 = pipe.stage(f"G{seq}L{li}n{idx}f{c}_{tc}")
                        s6.op("vector", lambda e: e.tensor_copy(
                            rinvb[0:2, 0:512], scA[0:2, 0:512]))
                        m1 = pipe.stage(f"G{seq}L{li}r{idx}a{c}_{tc}")
                        m1.op("tensor", lambda e, c=c, SRC=SRC, sl=sl: e.matmul(
                            P["P4"][:, 0:512], PTg[:, c * 128:(c + 1) * 128],
                            SRC[:, sl], start=True, stop=True))
                        m1.op("vector", lambda e, SRC=SRC, sl=sl, scol=scol, tc=tc: e.scalar_tensor_tensor(
                            scA[:, 0:512], SRC[:, sl],
                            cx.sb[scol][:, li:li + 1],
                            tblC[:, tc * 512:(tc + 1) * 512], ALU.mult, ALU.mult))
                        m2 = pipe.stage(f"G{seq}L{li}r{idx}b{c}_{tc}")
                        m2.op("vector", lambda e, tc=tc: e.tensor_tensor(
                            scB[:, 0:512], P["P4"][:, 0:512],
                            tblS[:, tc * 512:(tc + 1) * 512], ALU.mult))
                        m3 = pipe.stage(f"G{seq}L{li}r{idx}c{c}_{tc}")
                        m3.op("vector", lambda e: e.tensor_tensor(
                            scA[:, 0:512], scA[:, 0:512], scB[:, 0:512], ALU.add))
                        m4 = pipe.stage(f"G{seq}L{li}r{idx}d{c}_{tc}")
                        m4.op("tensor", lambda e: e.matmul(
                            P["P4"][:, 0:512], ETsel, rinvb[0:2, 0:512],
                            start=True, stop=True))
                        m5 = pipe.stage(f"G{seq}L{li}r{idx}e{c}_{tc}")
                        m5.op("vector", lambda e: e.tensor_tensor(
                            scA[:, 0:512], scA[:, 0:512], P["P4"][:, 0:512], ALU.mult))
                        m6 = pipe.stage(f"G{seq}L{li}r{idx}f{c}_{tc}")
                        m6.op("vector", lambda e, SRC=SRC, sl=sl: e.tensor_copy(
                            SRC[:, sl], scA[:, 0:512]))

            cx.dbg(f"qh{li}", lambda: QB[:, :], seq == 0 and li == 0)
            cx.dbg(f"kh{li}", lambda: KB[:, :], seq == 0 and li == 0)
            for i0 in range(16):
                slot = i0 % 3
                tv = pipe.stage(f"G{seq}L{li}aV{i0}")
                for c in range(NCH):
                    tv.op("tensor", lambda e, c=c, i0=i0: e.transpose(
                        P["PT"][:, c * 128:(c + 1) * 128],
                        VB[:, c * T + i0 * 128: c * T + (i0 + 1) * 128], idb))
                tv2 = pipe.stage(f"G{seq}L{li}aW{i0}")
                tv2.op("vector", lambda e, slot=slot: e.tensor_copy(
                    vtm[:, slot * 1024:(slot + 1) * 1024], P["PT"][:, 0:1024]))
                ncnk = min(3, i0 + 1)
                jlo = i0 - ncnk + 1
                jw = ncnk * 128
                for h in range(16):
                    base = 64 * (h % 2)
                    c = h // 2
                    mm = pipe.stage(f"G{seq}L{li}aM{i0}_{h}")
                    mm.op("tensor", lambda e, c=c, base=base, i0=i0, jlo=jlo, jw=jw: e.matmul(
                        P["P6"][:, 0:jw],
                        QB[base:base + 64, c * T + i0 * 128: c * T + (i0 + 1) * 128],
                        KB[base:base + 64, c * T + jlo * 128: c * T + jlo * 128 + jw],
                        start=True, stop=True))
                    ms = pipe.stage(f"G{seq}L{li}aS{i0}_{h}")
                    for jc in range(ncnk):
                        if jc == 0 and ncnk == 3:
                            msk = maskUT
                        elif jc == ncnk - 1:
                            msk = maskLT
                        else:
                            msk = None
                        if msk is None:
                            ms.op("vector", lambda e, jc=jc: e.tensor_copy(
                                scS[:, jc * 128:(jc + 1) * 128],
                                P["P6"][:, jc * 128:(jc + 1) * 128]))
                        else:
                            ms.op("vector", lambda e, jc=jc, msk=msk: e.tensor_tensor(
                                scS[:, jc * 128:(jc + 1) * 128],
                                P["P6"][:, jc * 128:(jc + 1) * 128], msk, ALU.add))
                    cx.dbg(f"scS{i0}_{h}", lambda: scA[:, 0:384],
                           seq == 0 and li == 0 and i0 <= 1 and h == 0)
                    mr = pipe.stage(f"G{seq}L{li}aR{i0}_{h}")
                    mr.op("vector", lambda e, jw=jw: e.reduce_max(
                        mxc[:, 0:1], scS[:, 0:jw], axis=AX.X, negate=True))
                    mr2 = pipe.stage(f"G{seq}L{li}aR2{i0}_{h}")
                    mr2.op("vector", lambda e: e.tensor_scalar(
                        mxc[:, 1:2], mxc[:, 0:1], 8.0, None, ALU.mult))
                    mex = pipe.stage(f"G{seq}L{li}aE{i0}_{h}")
                    mex.op("scalar", lambda e, jw=jw: e.activation(
                        scS[:, 0:jw], scS[:, 0:jw], AF.Exp, scale=8.0,
                        bias=mxc[:, 1:2], accum_out=mxc[:, 0:1]))
                    cx.dbg(f"pexp{i0}_{h}", lambda: scA[:, 0:384],
                           seq == 0 and li == 0 and i0 <= 1 and h == 0)
                    cx.dbg(f"mxc{i0}_{h}", lambda: scB[:, 0:2],
                           seq == 0 and li == 0 and i0 <= 1 and h == 0)
                    mrc = pipe.stage(f"G{seq}L{li}aC{i0}_{h}")
                    mrc.op("vector", lambda e: e.reciprocal(mxc[:, 0:1], mxc[:, 0:1]))
                    mpn = pipe.stage(f"G{seq}L{li}aP{i0}_{h}")
                    mpn.op("vector", lambda e, jw=jw: e.tensor_scalar(
                        pn[:, 0:jw], scS[:, 0:jw], mxc[:, 0:1], None, ALU.mult))
                    cx.dbg(f"pn{i0}_{h}", lambda: cx.sb["sqt"][:, 0:384],
                           seq == 0 and li == 0 and i0 <= 1 and h == 0)
                    mtr = pipe.stage(f"G{seq}L{li}aT{i0}_{h}")
                    for jc in range(ncnk):
                        mtr.op("tensor", lambda e, jc=jc: e.transpose(
                            P["PT"][:, jc * 128:(jc + 1) * 128],
                            pn[:, jc * 128:(jc + 1) * 128], idb))
                    mte = pipe.stage(f"G{seq}L{li}aU{i0}_{h}")
                    mte.op("vector", lambda e, jw=jw: e.tensor_copy(
                        pnT[:, 0:jw], P["PT"][:, 0:jw]))
                    mav = pipe.stage(f"G{seq}L{li}aX{i0}_{h}")
                    for jc in range(ncnk):
                        jt = jlo + jc
                        mav.op("tensor", lambda e, jc=jc, jt=jt, h=h, base=base, ncnk=ncnk: e.matmul(
                            P["P5"][base:base + 64, 0:128],
                            vtm[:, (jt % 3) * 1024 + h * 64: (jt % 3) * 1024 + h * 64 + 64],
                            pnT[:, jc * 128:(jc + 1) * 128],
                            start=(jc == 0), stop=(jc == ncnk - 1)))
                    mae = pipe.stage(f"G{seq}L{li}aY{i0}_{h}")
                    mae.op("scalar", lambda e, base=base, c=c, i0=i0: e.activation(
                        XB[base:base + 64,
                           c * 2049 + 1 + i0 * 128: c * 2049 + 1 + (i0 + 1) * 128],
                        P["P5"][base:base + 64, 0:128], AF.Identity))

            cx.dbg(f"ao{li}", lambda: XB[:, 0:NCH * 2049], seq == 0 and li == 0)
            pmm = {}
            prd = {}
            pld = {}
            pdm = {}
            idx = 0
            for tc in range(4):
                emit_xf_load(cx, seq, tc, shift=False, name=f"G{seq}L{li}w{tc}",
                             xsrc="xst2")
                xfst = pipe.stages[-1]
                for fbk in range(8):
                    buf = WS0 if idx % 2 == 0 else WS1
                    ldd = (None if idx == 0 else
                           [pld[0]] if idx == 1 else [pmm[idx - 2]])
                    ld = pipe.stage(f"G{seq}L{li}wL{tc}_{fbk}", deps=ldd)
                    cx.load_wblock(ld, f"l0f_wo{li}", fbk, buf)
                    pld[idx] = ld
                    mmd = None if idx == 0 else [ld] + ([prd[idx - 2]] if idx >= 2 else [])
                    mm = pipe.stage(f"G{seq}L{li}wM{tc}_{fbk}", deps=mmd)
                    g = idx % 2
                    for k in range(8):
                        mm.op("tensor", lambda e, k=k, tc=tc, buf=buf, g=g: e.matmul(
                            P[f"P{g * 2}"][:, 0:512], buf[:, k * 128:(k + 1) * 128],
                            xbsl(k, tc), start=(k == 0), stop=(k == 7)))
                    scX = scB if idx % 2 == 0 else scA
                    ev = pipe.stage(f"G{seq}L{li}wE{tc}_{fbk}",
                                    deps=[mm, xfst] + ([pdm[idx - 2]] if idx >= 2 else []))
                    ev.op("vector", lambda e, fbk=fbk, g=g, scX=scX: e.tensor_tensor(
                        scX[:, 0:512], P[f"P{g * 2}"][:, 0:512], cx.xf(fbk), ALU.add))
                    ev2 = pipe.stage(f"G{seq}L{li}wF{tc}_{fbk}")
                    ev2.dma("sync", lambda e, fbk=fbk, tc=tc, scX=scX: e.dma_start(
                        out=cx.d["xst2"][fbk, :, 1 + tc * 512: 1 + (tc + 1) * 512],
                        in_=scX[:, 0:512]))
                    pmm[idx] = mm
                    prd[idx] = ev
                    pdm[idx] = ev2
                    idx += 1

            cx.dbg(f"x2a{li}", lambda: cx.d["xst2"][:, :, :], seq == 0 and li < 2)
            emit_ln(cx, seq, f"G{seq}L{li}f", xbsl, shift=True, gb_idx=li,
                    xsrc="xst2")
            ldb = pipe.stage(f"G{seq}L{li}b1")
            ldb.dma("sync", lambda e: e.dma_start(out=vb1[:], in_=cx.d["l0f_b1"][li]))
            ldb.dma("sync", lambda e: e.dma_start(out=vb2[:], in_=cx.d["l0f_b2"][li]))
            w1s = lambda p, i: VB[:, (p * 4 + i) * 512:(p * 4 + i + 1) * 512]
            VBf = VB[:, 4096:12288].bitcast(F32)
            xres = lambda p, jj: VBf[:, (p * 2 + jj) * 512:(p * 2 + jj + 1) * 512]
            scXf = lambda p, jj: VBf[:, (4 + p * 2 + jj) * 512:(5 + p * 2 + jj) * 512]
            for half in range(2):
                pmmA = {}
                pmmB = {}
                pS1 = {}
                pS2 = {}
                pld = {}
                for j in range(22):
                    p = j % 2
                    buf = WS0 if p == 0 else WS1
                    ldd = (None if j == 0 else
                           [pld[0]] if j == 1 else [pmmB[j - 2]])
                    lda = pipe.stage(f"G{seq}L{li}1L{half}_{j}", deps=ldd)
                    cx.load_wblock(lda, f"l0f_w1{li}", j, buf, col=0)
                    cx.load_wblock(lda, f"l0f_w1{li}", j + 22, buf, col=1024)
                    pld[j] = lda
                    ab = 0 if p == 0 else 4
                    mmd = None if j == 0 else ([lda] +
                                               ([pS1[j - 2]] if j >= 2 else []))
                    mmA = pipe.stage(f"G{seq}L{li}1A{half}_{j}", deps=mmd)
                    for jj in range(2):
                        tc = half * 2 + jj
                        for k in range(8):
                            mmA.op("tensor", lambda e, k=k, tc=tc, jj=jj, ab=ab, buf=buf: e.matmul(
                                P[f"P{ab + jj}"][:, 0:512], buf[:, k * 128:(k + 1) * 128],
                                xbsl(k, tc), start=(k == 0), stop=(k == 7)))
                    mmB = pipe.stage(f"G{seq}L{li}1B{half}_{j}",
                                     deps=None if j == 0 else [pS1[j - 1]])
                    for jj in range(2):
                        tc = half * 2 + jj
                        for k in range(8):
                            mmB.op("tensor", lambda e, k=k, tc=tc, jj=jj, buf=buf: e.matmul(
                                P[f"P{2 + jj}"][:, 0:512],
                                buf[:, 1024 + k * 128:1024 + (k + 1) * 128],
                                xbsl(k, tc), start=(k == 0), stop=(k == 7)))
                    S1 = pipe.stage(f"G{seq}L{li}1S{half}_{j}",
                                    deps=[mmB] + ([pS2[j - 2]] if j >= 2 else []))
                    S1.op("scalar", lambda e, j=j, p=p: e.activation(
                        w1s(p, 2), P["P2"][:, 0:512], AF.Gelu,
                        bias=vb1[:, j + 22:j + 23]))
                    S1.op("scalar", lambda e, j=j, p=p: e.activation(
                        w1s(p, 3), P["P3"][:, 0:512], AF.Gelu,
                        bias=vb1[:, j + 22:j + 23]))
                    S1.op("scalar", lambda e, j=j, ab=ab, p=p: e.activation(
                        w1s(p, 0), P[f"P{ab}"][:, 0:512], AF.Identity,
                        bias=vb1[:, j:j + 1]))
                    S1.op("scalar", lambda e, j=j, ab=ab, p=p: e.activation(
                        w1s(p, 1), P[f"P{ab + 1}"][:, 0:512], AF.Identity,
                        bias=vb1[:, j:j + 1]))
                    S2 = pipe.stage(f"G{seq}L{li}1T{half}_{j}")
                    S2.op("vector", lambda e, j=j, p=p: e.tensor_tensor(
                        QB[:, j * 512:(j + 1) * 512], w1s(p, 0), w1s(p, 2), ALU.mult))
                    S2.op("vector", lambda e, j=j, p=p: e.tensor_tensor(
                        KB[:, j * 512:(j + 1) * 512], w1s(p, 1), w1s(p, 3), ALU.mult))
                    pmmA[j] = mmA
                    pmmB[j] = mmB
                    pS1[j] = S1
                    pS2[j] = S2
                cx.dbg(f"g{li}_{half}", lambda: QB[:, 0:22 * 512],
                       seq == 0 and li == 0 and half == 0)
                pmm = {}
                pld = {}
                pE = {}
                pF = {}
                pG = {}
                for fbk in range(8):
                    p = fbk % 2
                    buf = WS0 if p == 0 else WS1
                    ldd = (None if fbk == 0 else
                           [pld[0]] if fbk == 1 else [pmm[fbk - 2]])
                    ld = pipe.stage(f"G{seq}L{li}2L{half}_{fbk}", deps=ldd)
                    cx.load_wblock(ld, f"l0f_w2{li}", fbk, buf)
                    pld[fbk] = ld
                    mmd = None if fbk == 0 else [ld] + ([pE[fbk - 2]] if fbk >= 2 else [])
                    mm = pipe.stage(f"G{seq}L{li}2M{half}_{fbk}", deps=mmd)
                    for k in range(22):
                        for jj, GD in ((0, QB), (1, KB)):
                            mm.op("tensor", lambda e, k=k, jj=jj, GD=GD, buf=buf, p=p: e.matmul(
                                P[f"P{p * 2 + jj}"][:, 0:512], buf[:, k * 128:(k + 1) * 128],
                                GD[:, k * 512:(k + 1) * 512],
                                start=(k == 0), stop=(k == 21)))
                    pmm[fbk] = mm
                    L = pipe.stage(f"G{seq}L{li}2X{half}_{fbk}",
                                   deps=[mm] + ([pF[fbk - 2]] if fbk >= 2 else []))
                    for jj in range(2):
                        tc = half * 2 + jj
                        L.dma("sync", lambda e, fbk=fbk, tc=tc, jj=jj, p=p: e.dma_start(
                            out=xres(p, jj),
                            in_=cx.d["xst2"][fbk, :, 1 + tc * 512: 1 + (tc + 1) * 512]))
                    E = pipe.stage(f"G{seq}L{li}2E{half}_{fbk}",
                                   deps=[mm] + ([pG[fbk - 2]] if fbk >= 2 else []))
                    for jj in range(2):
                        E.op("scalar", lambda e, fbk=fbk, jj=jj, p=p: e.activation(
                            scXf(p, jj), P[f"P{p * 2 + jj}"][:, 0:512], AF.Identity,
                            bias=vb2[:, fbk:fbk + 1]))
                    F = pipe.stage(f"G{seq}L{li}2F{half}_{fbk}", deps=[E, L])
                    for jj in range(2):
                        F.op("vector", lambda e, jj=jj, p=p: e.tensor_tensor(
                            scXf(p, jj), scXf(p, jj), xres(p, jj), ALU.add))
                    G = pipe.stage(f"G{seq}L{li}2G{half}_{fbk}")
                    for jj in range(2):
                        tc = half * 2 + jj
                        G.dma("sync", lambda e, fbk=fbk, tc=tc, jj=jj, p=p: e.dma_start(
                            out=cx.d["xst2"][fbk, :, 1 + tc * 512: 1 + (tc + 1) * 512],
                            in_=scXf(p, jj)))
                    pE[fbk] = E
                    pF[fbk] = F
                    pG[fbk] = G

        cx.dbg(f"x2f{seq}", lambda: cx.d["xst2"][:, :, :], seq == 0)
        for tc in range(4):
            emit_xf_load(cx, seq, tc, shift=False, name=f"G{seq}fr{tc}", xsrc="xst")
            for c in range(NCH):
                l1 = pipe.stage(f"G{seq}fa{tc}_{c}")
                l1.dma("sync", lambda e, c=c, tc=tc: e.dma_start(
                    out=xft[:, 0:512],
                    in_=cx.d["xst2"][c, :, 1 + tc * 512: 1 + (tc + 1) * 512]))
                l2 = pipe.stage(f"G{seq}fb{tc}_{c}")
                l2.op("vector", lambda e, c=c: e.tensor_tensor(
                    scB[:, 0:512], cx.xf(c), xft[:, 0:512], ALU.add))
                l3 = pipe.stage(f"G{seq}fc{tc}_{c}")
                l3.dma("sync", lambda e, c=c, tc=tc: e.dma_start(
                    out=cx.x_ap(seq, c, tc * 512, 512), in_=scB[:, 0:512]))



_UNUSED2 = 0



def emit_l1a(cx, seq):
    pipe = cx.pipe
    P = cx.ps
    XB = cx.sb["XB"]
    cb = cx.sb["cstb"]
    onesb = cb[:, 0:128]
    WS0, WS1 = cx.sb["WS0"], cx.sb["WS1"]
    scA, scB = cx.sb["scA"], cx.sb["scB"]
    xbt, sqt = cx.sb["xbt"], cx.sb["sqt"]
    XF = cx.sb["XF"]

    def xbsl(c, tc, w=512):
        return XB[:, c * 2049 + 1 + tc * 512: c * 2049 + 1 + tc * 512 + w]

    def dlt(c, tc, w=512):
        return XB[:, c * 2049 + 1 + tc * 512: c * 2049 + 1 + tc * 512 + w]

    with contextlib.ExitStack() as ph:
        VR = cx.sbuf("VR", [128, NCH * 2050], BF16, ph)
        BCB = cx.sbuf("BCB", [128, 2 * 16 * 512], BF16, ph)
        YAC = cx.sbuf("YAC", [128, 512], F32, ph)
        dat = cx.sbuf("dat", [128, 4 * 512], F32, ph)
        rt = cx.sbuf("rt", [128, 4 * 512], F32, ph)
        wk = cx.sbuf("wk", [128, 4 * 512], F32, ph)
        et = cx.sbuf("et", [128, 4 * 512], BF16, ph)
        bu = cx.sbuf("bu", [128, 4 * 512], BF16, ph)
        u2 = cx.sbuf("u2", [128, 512], F32, ph)
        hst = cx.sbuf("hst", [128, 8 * 16], F32, ph)
        AT = cx.sbuf("AT", [128, 8 * 16], F32, ph)
        brow = cx.sbuf("brow", [64, 1024], BF16, ph)
        gtb = cx.sbuf("gtb", [128, 512], BF16, ph)
        bRS2 = cx.sbuf("bRS2", [1, T], BF16, ph)
        bMR22 = cx.sbuf("bMR22", [1, T], BF16, ph)

        def vr(c, tc, off=0, w=512):
            return VR[:, c * 2050 + 1 + off + tc * 512: c * 2050 + 1 + off + tc * 512 + w]

        emit_ln(cx, seq, f"M{seq}ln", xbsl)
        sv = pipe.stage(f"M{seq}sv")
        sv.op("vector", lambda e: e.tensor_copy(bRS2[0:1, 0:T], cx.sb["bRS"][0:1, 0:T]))
        sv.op("scalar", lambda e: e.copy(bMR22[0:1, 0:T], cx.sb["bMR2"][0:1, 0:T]))
        zp = pipe.stage(f"M{seq}zp")
        zp.op("gpsimd", lambda e: e.memset(
            VR[:].rearrange("p (c t) -> p c t", t=2050)[:, :, 0:1], 0))
        zp.op("vector", lambda e: e.memset(
            VR[:].rearrange("p (c t) -> p c t", t=2050)[:, :, 2049:2050], 0))

        gtbs = et
        pmm = {}
        prd = {}
        pld = {}
        pdm = {}
        idx = 0
        for half in range(2):
            for fbk in range(16):
                buf = WS0 if idx % 2 == 0 else WS1
                ldd = (None if idx == 0 else
                       [pld[0]] if idx == 1 else [pmm[idx - 2]])
                ld = pipe.stage(f"M{seq}iL{half}_{fbk}", deps=ldd)
                cx.load_wblock(ld, "l1a_in", fbk, buf)
                pld[idx] = ld
                mmd = None if idx == 0 else [ld] + ([prd[idx - 2]] if idx >= 2 else [])
                mm = pipe.stage(f"M{seq}iM{half}_{fbk}", deps=mmd)
                g = idx % 2
                for k in range(8):
                    for j in range(2):
                        tc = half * 2 + j
                        mm.op("tensor", lambda e, k=k, j=j, tc=tc, buf=buf, g=g: e.matmul(
                            P[f"P{g * 2 + j}"][:, 0:512], buf[:, k * 128:(k + 1) * 128],
                            xbsl(k, tc), start=(k == 0), stop=(k == 7)))
                pmm[idx] = mm
                if fbk < 8:
                    ev = pipe.stage(f"M{seq}iE{half}_{fbk}", deps=[mm])
                    for j in range(2):
                        tc = half * 2 + j
                        ev.op("scalar", lambda e, j=j, fbk=fbk, tc=tc, g=g: e.activation(
                            vr(fbk, tc), P[f"P{g * 2 + j}"][:, 0:512], AF.Identity))
                    prd[idx] = ev
                else:
                    ev = pipe.stage(f"M{seq}iE{half}_{fbk}",
                                    deps=[mm] + ([pdm[idx - 2]] if idx - 2 in pdm else []))
                    for j in range(2):
                        ev.op("scalar", lambda e, j=j, g=g: e.activation(
                            gtbs[:, (g * 2 + j) * 512:(g * 2 + j + 1) * 512],
                            P[f"P{g * 2 + j}"][:, 0:512], AF.Identity))
                    ev2 = pipe.stage(f"M{seq}iF{half}_{fbk}")
                    for j in range(2):
                        tc = half * 2 + j
                        ev2.dma("sync", lambda e, j=j, g=g, fbk=fbk, tc=tc: e.dma_start(
                            out=cx.d["gdram"][fbk - 8, :, tc * 512:(tc + 1) * 512],
                            in_=gtbs[:, (g * 2 + j) * 512:(g * 2 + j + 1) * 512]))
                    prd[idx] = ev
                    pdm[idx] = ev2
                idx += 1

        pcv = {}
        psl = {}
        for c in range(NCH):
            cbuf = dat if c % 2 == 0 else rt
            cvd = [psl[c - 2]] if c >= 2 else None
            cv = pipe.stage(f"M{seq}ca{c}", deps=cvd)
            for tc in range(4):
                cv.op("vector", lambda e, c=c, tc=tc, cbuf=cbuf: e.tensor_scalar(
                    cbuf[:, tc * 512:(tc + 1) * 512], vr(c, tc, off=-1),
                    cx.sb["taps"][:, c * 3:c * 3 + 1], None, ALU.mult))
                cv.op("vector", lambda e, c=c, tc=tc, cbuf=cbuf: e.scalar_tensor_tensor(
                    cbuf[:, tc * 512:(tc + 1) * 512], vr(c, tc),
                    cx.sb["taps"][:, c * 3 + 1:c * 3 + 2],
                    cbuf[:, tc * 512:(tc + 1) * 512], ALU.mult, ALU.add))
                cv.op("vector", lambda e, c=c, tc=tc, cbuf=cbuf: e.scalar_tensor_tensor(
                    cbuf[:, tc * 512:(tc + 1) * 512], vr(c, tc, off=1),
                    cx.sb["taps"][:, c * 3 + 2:c * 3 + 3],
                    cbuf[:, tc * 512:(tc + 1) * 512], ALU.mult, ALU.add))
            sl = pipe.stage(f"M{seq}ce{c}")
            for tc in range(4):
                sl.op("scalar", lambda e, c=c, tc=tc, cbuf=cbuf: e.activation(
                    vr(c, tc), cbuf[:, tc * 512:(tc + 1) * 512], AF.Silu))
            pcv[c] = cv
            psl[c] = sl

        ldb = pipe.stage(f"M{seq}bL")
        cx.load_wblock(ldb, "l1a_B", 0, WS0)
        cx.load_wblock(ldb, "l1a_C", 0, WS1)
        pbe = {}
        pbd = {}
        for tc in range(4):
            po = (tc % 2) * 32
            mmd = [pbe[tc - 2]] if tc >= 2 else None
            mm = pipe.stage(f"M{seq}bM{tc}", deps=mmd)
            for k in range(8):
                mm.op("tensor", lambda e, k=k, tc=tc, po=po: e.matmul(
                    P["P4"][po:po + 16, 0:512], WS0[:, k * 16:(k + 1) * 16],
                    vr(k, tc), start=(k == 0), stop=(k == 7)))
                mm.op("tensor", lambda e, k=k, tc=tc, po=po: e.matmul(
                    P["P5"][po:po + 16, 0:512], WS1[:, k * 16:(k + 1) * 16],
                    vr(k, tc), start=(k == 0), stop=(k == 7)))
            ev = pipe.stage(f"M{seq}bE{tc}",
                            deps=[mm] + ([pbd[tc - 2]] if tc >= 2 else []))
            ev.op("scalar", lambda e, tc=tc, po=po: e.activation(
                brow[0:16, (tc % 2) * 512:(tc % 2) * 512 + 512],
                P["P4"][po:po + 16, 0:512], AF.Identity))
            ev.op("scalar", lambda e, tc=tc, po=po: e.activation(
                brow[32:48, (tc % 2) * 512:(tc % 2) * 512 + 512],
                P["P5"][po:po + 16, 0:512], AF.Identity))
            dm = pipe.stage(f"M{seq}bF{tc}")
            dm.dma("sync", lambda e, tc=tc: e.dma_start(
                out=cx.d["bcrow"][0, :, tc * 512:(tc + 1) * 512],
                in_=brow[0:16, (tc % 2) * 512:(tc % 2) * 512 + 512]))
            dm.dma("sync", lambda e, tc=tc: e.dma_start(
                out=cx.d["bcrow"][1, :, tc * 512:(tc + 1) * 512],
                in_=brow[32:48, (tc % 2) * 512:(tc % 2) * 512 + 512]))
            pbe[tc] = ev
            pbd[tc] = dm

        pmm = {}
        prd = {}
        pld = {}
        idx = 0
        for half in range(2):
            for fbk in range(8):
                buf = WS0 if idx % 2 == 0 else WS1
                ldd = (None if idx == 0 else
                       [pld[0]] if idx == 1 else [pmm[idx - 2]])
                ld = pipe.stage(f"M{seq}dL{half}_{fbk}", deps=ldd)
                cx.load_wblock(ld, "l1a_delta", fbk, buf)
                pld[idx] = ld
                mmd = None if idx == 0 else [ld] + ([prd[idx - 2]] if idx >= 2 else [])
                mm = pipe.stage(f"M{seq}dM{half}_{fbk}", deps=mmd)
                g = idx % 2
                for k in range(8):
                    for j in range(2):
                        tc = half * 2 + j
                        mm.op("tensor", lambda e, k=k, j=j, tc=tc, buf=buf, g=g: e.matmul(
                            P[f"P{g * 2 + j}"][:, 0:512], buf[:, k * 128:(k + 1) * 128],
                            vr(k, tc), start=(k == 0), stop=(k == 7)))
                ev = pipe.stage(f"M{seq}dE{half}_{fbk}", deps=[mm])
                for j in range(2):
                    tc = half * 2 + j
                    ev.op("scalar", lambda e, j=j, fbk=fbk, tc=tc, g=g: e.activation(
                        xbsl(fbk, tc), P[f"P{g * 2 + j}"][:, 0:512], AF.Identity,
                        bias=cx.sb["dbase"][:, fbk:fbk + 1]))
                pmm[idx] = mm
                prd[idx] = ev
                idx += 1

        at1 = pipe.stage(f"M{seq}at1")
        at1.op("scalar", lambda e: e.activation(AT[:, 0:128], cx.sb["alog"][:, 0:128],
                                                AF.Exp))
        at2 = pipe.stage(f"M{seq}at2")
        at2.op("vector", lambda e: e.tensor_scalar(AT[:, 0:128], AT[:, 0:128], -1.0,
                                                   None, ALU.mult))

        pmm = {}
        prd = {}
        pld = {}
        pV2 = {}
        pD = {}
        idx = 0
        for half in range(2):
            for fbk in range(8):
                buf = WS0 if idx % 2 == 0 else WS1
                ldd = (None if idx == 0 else
                       [pld[0]] if idx == 1 else [pmm[idx - 2]])
                ld = pipe.stage(f"M{seq}tL{half}_{fbk}", deps=ldd)
                cx.load_wblock(ld, "l1a_tau", fbk, buf)
                pld[idx] = ld
                mmd = None if idx == 0 else [ld] + ([prd[idx - 2]] if idx >= 2 else [])
                mm = pipe.stage(f"M{seq}tM{half}_{fbk}", deps=mmd)
                g = idx % 2
                for k in range(8):
                    for j in range(2):
                        tc = half * 2 + j
                        mm.op("tensor", lambda e, k=k, j=j, tc=tc, buf=buf, g=g: e.matmul(
                            P[f"P{g * 2 + j}"][:, 0:512], buf[:, k * 128:(k + 1) * 128],
                            xbsl(k, tc), start=(k == 0), stop=(k == 7)))
                da = lambda j, g=g: dat[:, (g * 2 + j) * 512:(g * 2 + j + 1) * 512]
                rta = lambda j, g=g: rt[:, (g * 2 + j) * 512:(g * 2 + j + 1) * 512]
                V1 = pipe.stage(f"M{seq}tA{half}_{fbk}",
                                deps=[mm] + ([pV2[idx - 2]] if idx >= 2 else []))
                for j in range(2):
                    bank = P[f"P{g * 2 + j}"]
                    V1.op("vector", lambda e, bank=bank, j=j, da=da: e.tensor_scalar(
                        da(j), bank[:, 0:512], -1.0, None, ALU.mult))
                    V1.op("vector", lambda e, bank=bank, j=j, da=da: e.tensor_tensor(
                        da(j), da(j), bank[:, 0:512], ALU.max))
                S1 = pipe.stage(f"M{seq}tC{half}_{fbk}")
                for j in range(2):
                    S1.op("scalar", lambda e, j=j, da=da, rta=rta: e.activation(
                        rta(j), da(j), AF.Exp, scale=-1.0))
                    S1.op("scalar", lambda e, j=j, rta=rta: e.activation(
                        rta(j), rta(j), AF.Ln, bias=1.0))
                V2 = pipe.stage(f"M{seq}tE{half}_{fbk}",
                                deps=[S1] + ([pD[idx - 2]] if idx >= 2 else []))
                for j in range(2):
                    bank = P[f"P{g * 2 + j}"]
                    V2.op("vector", lambda e, bank=bank, j=j, da=da: e.tensor_scalar(
                        da(j), bank[:, 0:512], 0.0, None, ALU.max))
                    V2.op("vector", lambda e, j=j, g=g, da=da, rta=rta: e.tensor_tensor(
                        gtbs[:, (g * 2 + j) * 512:(g * 2 + j + 1) * 512],
                        da(j), rta(j), ALU.add))
                D = pipe.stage(f"M{seq}tG{half}_{fbk}")
                for j in range(2):
                    tc = half * 2 + j
                    D.dma("sync", lambda e, fbk=fbk, tc=tc, j=j, g=g: e.dma_start(
                        out=cx.d["ddram"][fbk, :, tc * 512:(tc + 1) * 512],
                        in_=gtbs[:, (g * 2 + j) * 512:(g * 2 + j + 1) * 512]))
                pmm[idx] = mm
                prd[idx] = V2
                pV2[idx] = V2
                pD[idx] = D
                idx += 1

        for c in range(NCH):
            ldd = pipe.stage(f"M{seq}dl{c}")
            ldd.dma("sync", lambda e, c=c: e.dma_start(
                out=XB[:, c * 2049 + 1: c * 2049 + 1 + T], in_=cx.d["ddram"][c]))

        for tt in range(4):
            bb = pipe.stage(f"M{seq}sB{tt}")
            for s in range(16):
                for bc in range(2):
                    bb.dma("gpsimd", lambda e, s=s, bc=bc, tt=tt: e.dma_start(
                        out=BCB[:, (bc * 16 + s) * 512:(bc * 16 + s + 1) * 512],
                        in_=cx.d["bcrow"][bc, s, tt * 512:(tt + 1) * 512]
                            .unsqueeze(0).broadcast_to((128, 512))))
            for c in range(NCH):
                uu = pipe.stage(f"M{seq}sU{tt}_{c}")
                uu.op("vector", lambda e, c=c, tt=tt: e.tensor_tensor(
                    u2[:, 0:512], dlt(c, tt), vr(c, tt), ALU.mult))
                ym = pipe.stage(f"M{seq}sY{tt}_{c}")
                ym.op("vector", lambda e: e.memset(YAC[:, 0:512], 0.0))
                for g in range(4):
                    for si in range(4):
                        s = g * 4 + si
                        d1 = pipe.stage(f"M{seq}sa{tt}_{c}_{g}_{si}")
                        d1.op("vector", lambda e, c=c, tt=tt, s=s, si=si: e.tensor_scalar(
                            dat[:, si * 512:(si + 1) * 512], dlt(c, tt),
                            AT[:, c * 16 + s:c * 16 + s + 1], -2e-6,
                            ALU.mult, ALU.min))
                    d2 = pipe.stage(f"M{seq}sb{tt}_{c}_{g}")
                    d2.op("scalar", lambda e: e.activation(
                        et[:, 0:2048], dat[:, 0:2048], AF.Exp))
                    d3 = pipe.stage(f"M{seq}sc{tt}_{c}_{g}")
                    d3.op("scalar", lambda e: e.activation(
                        rt[:, 0:2048], dat[:, 0:2048], AF.Ln, scale=-1.0, bias=-1e-6))
                    d4 = pipe.stage(f"M{seq}sd{tt}_{c}_{g}")
                    d4.op("scalar", lambda e: e.activation(
                        rt[:, 0:2048], rt[:, 0:2048], AF.Exp, scale=-1.0))
                    d5 = pipe.stage(f"M{seq}se{tt}_{c}_{g}")
                    d5.op("vector", lambda e: e.tensor_scalar(
                        wk[:, 0:2048], et[:, 0:2048], -1.0, 1.0, ALU.mult, ALU.add))
                    d6 = pipe.stage(f"M{seq}sf{tt}_{c}_{g}")
                    d6.op("vector", lambda e: e.tensor_tensor(
                        wk[:, 0:2048], wk[:, 0:2048], rt[:, 0:2048], ALU.mult))
                    d7 = pipe.stage(f"M{seq}sg{tt}_{c}_{g}")
                    d7.op("vector", lambda e, g=g: e.tensor_tensor(
                        dat[:, 0:2048].rearrange("p (s t) -> p s t", t=512),
                        u2[:, 0:512].unsqueeze(1).broadcast_to((128, 4, 512)),
                        BCB[:, g * 4 * 512:(g + 1) * 4 * 512].rearrange(
                            "p (s t) -> p s t", t=512),
                        ALU.mult))
                    d8 = pipe.stage(f"M{seq}sh{tt}_{c}_{g}")
                    d8.op("vector", lambda e: e.tensor_tensor(
                        bu[:, 0:2048], wk[:, 0:2048], dat[:, 0:2048], ALU.mult))
                    for si in range(4):
                        s = g * 4 + si
                        d9 = pipe.stage(f"M{seq}si{tt}_{c}_{g}_{si}")
                        if tt == 0:
                            d9.op("vector", lambda e, si=si: e.tensor_tensor_scan(
                                rt[:, si * 512:(si + 1) * 512],
                                et[:, si * 512:(si + 1) * 512],
                                bu[:, si * 512:(si + 1) * 512],
                                0.0, ALU.mult, ALU.add))
                        else:
                            d9.op("vector", lambda e, si=si, c=c, s=s: e.tensor_tensor_scan(
                                rt[:, si * 512:(si + 1) * 512],
                                et[:, si * 512:(si + 1) * 512],
                                bu[:, si * 512:(si + 1) * 512],
                                hst[:, c * 16 + s:c * 16 + s + 1], ALU.mult, ALU.add))
                        da = pipe.stage(f"M{seq}sj{tt}_{c}_{g}_{si}")
                        da.op("vector", lambda e, si=si, c=c, s=s: e.tensor_copy(
                            hst[:, c * 16 + s:c * 16 + s + 1],
                            rt[:, si * 512 + 511: si * 512 + 512]))
                    db = pipe.stage(f"M{seq}sk{tt}_{c}_{g}")
                    db.op("vector", lambda e, g=g: e.tensor_tensor(
                        rt[:, 0:2048], rt[:, 0:2048],
                        BCB[:, (16 + g * 4) * 512:(16 + (g + 1) * 4) * 512],
                        ALU.mult))
                    for si in range(4):
                        dc = pipe.stage(f"M{seq}sl{tt}_{c}_{g}_{si}")
                        dc.op("vector", lambda e, si=si: e.tensor_tensor(
                            YAC[:, 0:512], YAC[:, 0:512],
                            rt[:, si * 512:(si + 1) * 512], ALU.add))
                g1 = pipe.stage(f"M{seq}sg1{tt}_{c}")
                g1.dma("sync", lambda e, c=c, tt=tt: e.dma_start(
                    out=xbt[:, 0:512],
                    in_=cx.d["gdram"][c, :, tt * 512:(tt + 1) * 512]))
                g2 = pipe.stage(f"M{seq}sg2{tt}_{c}")
                g2.op("scalar", lambda e: e.activation(
                    scA[:, 0:512], xbt[:, 0:512], AF.Silu))
                g3 = pipe.stage(f"M{seq}sg3{tt}_{c}")
                g3.op("vector", lambda e: e.tensor_tensor(
                    gtb[:, 0:512], YAC[:, 0:512], scA[:, 0:512], ALU.mult))
                g4s = pipe.stage(f"M{seq}sg4{tt}_{c}")
                g4s.dma("sync", lambda e, c=c, tt=tt: e.dma_start(
                    out=cx.d["ygdram"][c, :, tt * 512:(tt + 1) * 512],
                    in_=gtb[:, 0:512]))

        for c in range(NCH):
            ldy = pipe.stage(f"M{seq}oy{c}")
            ldy.dma("sync", lambda e, c=c: e.dma_start(
                out=VR[:, c * 2050 + 1: c * 2050 + 1 + T], in_=cx.d["ygdram"][c]))
        bRS2p1 = bu[0:1, 0:T]
        bp = pipe.stage(f"M{seq}op")
        bp.op("vector", lambda e: e.tensor_scalar(
            bRS2p1[0:1, 0:T], bRS2[0:1, 0:T], 1.0, None, ALU.add))
        pmm = {}
        pld = {}
        pE = {}
        pD = {}
        idx = 0
        for tc in range(4):
            emit_xf_load(cx, seq, tc, shift=False, name=f"M{seq}or{tc}", xsrc="xst")
            xfst = pipe.stages[-1]
            if tc > 0:
                xfst.deps = [pE[idx - 1]]
            bc = pipe.stage(f"M{seq}oB{tc}",
                            deps=[pE[idx - 1]] if tc > 0 else None)
            bc.op("tensor", lambda e, tc=tc: e.matmul(
                P["P4"][:, 0:512], onesb[0:1, :],
                bRS2p1[0:1, tc * 512:(tc + 1) * 512], start=True, stop=True))
            bc.op("tensor", lambda e, tc=tc: e.matmul(
                P["P5"][:, 0:512], onesb[0:1, :],
                bMR22[0:1, tc * 512:(tc + 1) * 512], start=True, stop=True))
            for fbk in range(8):
                buf = WS0 if idx % 2 == 0 else WS1
                g = idx % 2
                w0 = wk[:, (g * 2) * 512:(g * 2 + 1) * 512]
                w1 = wk[:, (g * 2 + 1) * 512:(g * 2 + 2) * 512]
                ldd = (None if idx == 0 else
                       [pld[0]] if idx == 1 else [pmm[idx - 2]])
                ld = pipe.stage(f"M{seq}oL{tc}_{fbk}", deps=ldd)
                cx.load_wblock(ld, "l1a_out", fbk, buf)
                pld[idx] = ld
                mmd = None if idx == 0 else [ld] + ([pE[idx - 2]] if idx >= 2 else [])
                mm = pipe.stage(f"M{seq}oM{tc}_{fbk}", deps=mmd)
                for k in range(8):
                    mm.op("tensor", lambda e, k=k, tc=tc, buf=buf, g=g: e.matmul(
                        P[f"P{g * 2}"][:, 0:512], buf[:, k * 128:(k + 1) * 128],
                        vr(k, tc), start=(k == 0), stop=(k == 7)))
                E = pipe.stage(f"M{seq}oE{tc}_{fbk}",
                               deps=[mm, xfst, bc] +
                                    ([pD[idx - 2]] if idx >= 2 else []))
                E.op("vector", lambda e, fbk=fbk, w0=w0: e.tensor_tensor(
                    w0, cx.xf(fbk), P["P4"][:, 0:512], ALU.mult))
                E.op("vector", lambda e, w0=w0: e.tensor_tensor(
                    w0, w0, P["P5"][:, 0:512], ALU.subtract))
                E.op("vector", lambda e, g=g, w0=w0, w1=w1: e.tensor_tensor(
                    w1, w0, P[f"P{g * 2}"][:, 0:512], ALU.add))
                D = pipe.stage(f"M{seq}oI{tc}_{fbk}")
                D.dma("sync", lambda e, fbk=fbk, tc=tc, w1=w1: e.dma_start(
                    out=cx.x_ap(seq, fbk, tc * 512, 512), in_=w1))
                pmm[idx] = mm
                pE[idx] = E
                pD[idx] = D
                idx += 1



_UNUSED3 = 0



def emit_l1f(cx, seq):
    pipe = cx.pipe
    P = cx.ps
    XB = cx.sb["XB"]
    cb = cx.sb["cstb"]
    cstf = cx.sb["cstf"]
    onesb = cb[:, 0:128]
    idf = cstf[:, 520:648]
    WS0, WS1 = cx.sb["WS0"], cx.sb["WS1"]
    scA, scB = cx.sb["scA"], cx.sb["scB"]
    xbt = cx.sb["xbt"]
    XF = cx.sb["XF"]

    def xbsl(c, tc, w=512):
        return XB[:, c * 2049 + 1 + tc * 512: c * 2049 + 1 + tc * 512 + w]

    with contextlib.ExitStack() as ph:
        mrowb = cx.sbuf("mrowb", [8, T], BF16, ph)
        lgt = cx.sbuf("lgt", [8, 512], F32, ph)
        whib = cx.sbuf("whib", [128, 64], BF16, ph)
        wlob = cx.sbuf("wlob", [128, 64], BF16, ph)
        b2b = cx.sbuf("b2b", [8, 1024], BF16, ph)
        b2f = cx.sbuf("b2f", [8, 1024], F32, ph)
        vb1e = cx.sbuf("vb1e", [128, 32], F32, ph)
        mb2 = cx.sbuf("mb2", [128, 1024], BF16, ph)
        yacc = cx.sbuf("yacc", [128, 8 * 1024], F32, ph)

        r0 = pipe.stage(f"R{seq}w0")
        r0.op("vector", lambda e: e.tensor_copy(whib[:, 0:64], cx.sb["wrf"][:, 0:64]))
        r1 = pipe.stage(f"R{seq}w1")
        r1.op("vector", lambda e: e.tensor_tensor(
            wlob[:, 0:64], cx.sb["wrf"][:, 0:64], whib[:, 0:64], ALU.subtract))
        r2 = pipe.stage(f"R{seq}w2")
        r2.dma("sync", lambda e: e.dma_start(out=b2f[:], in_=cx.d["l1f_b2"][:]))
        r3 = pipe.stage(f"R{seq}w3")
        r3.op("vector", lambda e: e.tensor_copy(b2b[:], b2f[:]))

        with contextlib.ExitStack() as ph2:
            LO = cx.sbuf("LO", [128, NCH * T], BF16, ph2)

            def lo_sl(c, tc):
                return LO[:, c * T + tc * 512: c * T + (tc + 1) * 512]

            def f32out(c, tc):
                return scB[:, 0:512]

            def post(c, tc):
                st = pipe.stage(f"R{seq}lo{c}_{tc}")
                st.op("vector", lambda e, c=c, tc=tc: e.tensor_tensor(
                    lo_sl(c, tc), scB[:, 0:512], xbsl(c, tc), ALU.subtract))

            emit_ln(cx, seq, f"R{seq}ln", xbsl, out_f32_fn=f32out, post_f32=post)

            for tc in range(4):
                mm = pipe.stage(f"R{seq}rM{tc}")
                for k in range(8):
                    mm.op("tensor", lambda e, k=k, tc=tc: e.matmul(
                        P["P4"][0:8, 0:512], whib[:, k * 8:(k + 1) * 8],
                        xbsl(k, tc), start=(k == 0), stop=False))
                for k in range(8):
                    mm.op("tensor", lambda e, k=k, tc=tc: e.matmul(
                        P["P4"][0:8, 0:512], wlob[:, k * 8:(k + 1) * 8],
                        xbsl(k, tc), start=False, stop=False))
                for k in range(8):
                    mm.op("tensor", lambda e, k=k, tc=tc: e.matmul(
                        P["P4"][0:8, 0:512], whib[:, k * 8:(k + 1) * 8],
                        lo_sl(k, tc), start=False, stop=(k == 7)))
                ev = pipe.stage(f"R{seq}rE{tc}")
                ev.op("scalar", lambda e: e.activation(
                    lgt[0:8, 0:512], P["P4"][0:8, 0:512], AF.Identity,
                    bias=cx.sb["brf"][0:8, 0:1]))
                tr = pipe.stage(f"R{seq}rT{tc}")
                for st4 in range(4):
                    tr.op("tensor", lambda e, st4=st4: e.transpose(
                        P["P5"][:, st4 * 8:(st4 + 1) * 8],
                        lgt[0:8, st4 * 128:(st4 + 1) * 128], idf[0:8, 0:8]))
                mx = pipe.stage(f"R{seq}rX{tc}")
                mx.op("vector", lambda e: e.reduce_max(
                    scB[:, 0:4], P["P5"][:, 0:32].rearrange("p (a b) -> p a b", b=8),
                    axis=AX.X))
                eq = pipe.stage(f"R{seq}rQ{tc}")
                eq.op("vector", lambda e: e.tensor_tensor(
                    scA[:, 0:32].rearrange("p (a b) -> p a b", b=8),
                    P["P5"][:, 0:32].rearrange("p (a b) -> p a b", b=8),
                    scB[:, 0:4].unsqueeze(2).broadcast_to((128, 4, 8)),
                    ALU.is_equal))
                tb = pipe.stage(f"R{seq}rB{tc}")
                for st4 in range(4):
                    tb.op("tensor", lambda e, st4=st4: e.transpose(
                        P["P6"][0:8, st4 * 128:(st4 + 1) * 128],
                        scA[:, st4 * 8:(st4 + 1) * 8], idf[:, 0:128]))
                me = pipe.stage(f"R{seq}rC{tc}")
                me.op("scalar", lambda e, tc=tc: e.activation(
                    mrowb[0:8, tc * 512:(tc + 1) * 512], P["P6"][0:8, 0:512],
                    AF.Identity))
                md = pipe.stage(f"R{seq}rD{tc}")
                md.dma("sync", lambda e, tc=tc: e.dma_start(
                    out=cx.d["mdram"][:, tc * 512:(tc + 1) * 512],
                    in_=mrowb[0:8, tc * 512:(tc + 1) * 512]))

        with contextlib.ExitStack() as ph3:
            HB0 = cx.sbuf("HB0", [128, 32 * 512], BF16, ph3)
            HB1 = cx.sbuf("HB1", [128, 32 * 512], BF16, ph3)
            ht = cx.sbuf("ht", [128, 2 * 512], BF16, ph3)
            ypairs = ((P["P4"], P["P5"]), (P["P6"], P["P0"]))
            for half in range(2):
                pib = {}
                pie = {}
                for fbk in range(8):
                    bkA, bkB = ypairs[fbk % 2]
                    ib = pipe.stage(f"R{seq}y0{half}_{fbk}",
                                    deps=[pie[fbk - 2]] if fbk >= 2 else None)
                    for jj in range(2):
                        tc = half * 2 + jj
                        ib.op("tensor", lambda e, fbk=fbk, jj=jj, tc=tc,
                              bkA=bkA, bkB=bkB: e.matmul(
                            bkA[:, 0:512] if jj == 0 else bkB[:, 0:512],
                            b2b[0:8, fbk * 128:(fbk + 1) * 128],
                            mrowb[0:8, tc * 512:(tc + 1) * 512],
                            start=True, stop=True))
                    ie = pipe.stage(f"R{seq}y1{half}_{fbk}", deps=[ib])
                    ie.op("vector", lambda e, fbk=fbk, bkA=bkA: e.tensor_copy(
                        yacc[:, fbk * 1024: fbk * 1024 + 512], bkA[:, 0:512]))
                    ie.op("scalar", lambda e, fbk=fbk, bkB=bkB: e.copy(
                        yacc[:, fbk * 1024 + 512: fbk * 1024 + 1024], bkB[:, 0:512]))
                    pib[fbk] = ib
                    pie[fbk] = ie
                for ex in range(8):
                    lb = pipe.stage(f"R{seq}e{half}_{ex}b")
                    lb.dma("sync", lambda e, ex=ex: e.dma_start(
                        out=vb1e[:], in_=cx.d["l1f_b1"][ex]))
                    for jj in range(2):
                        tc = half * 2 + jj
                        lb.dma("gpsimd", lambda e, ex=ex, jj=jj, tc=tc: e.dma_start(
                            out=mb2[:, jj * 512:(jj + 1) * 512],
                            in_=cx.d["mdram"][ex, tc * 512:(tc + 1) * 512]
                                .unsqueeze(0).broadcast_to((128, 512))))
                    pmm = {}
                    pld = {}
                    pe2 = {}
                    for fbk in range(32):
                        p = fbk % 2
                        buf = WS0 if p == 0 else WS1
                        ldd = (None if fbk == 0 else
                               [pld[0]] if fbk == 1 else [pmm[fbk - 2]])
                        ld = pipe.stage(f"R{seq}e{half}_{ex}L{fbk}", deps=ldd)
                        cx.load_wblock(ld, f"l1f_w1_{ex}", fbk, buf)
                        pld[fbk] = ld
                        mmd = None if fbk == 0 else [ld] + \
                            ([pe2[fbk - 2]] if fbk >= 2 else [])
                        mm = pipe.stage(f"R{seq}e{half}_{ex}M{fbk}", deps=mmd)
                        for k in range(8):
                            for jj in range(2):
                                tc = half * 2 + jj
                                mm.op("tensor", lambda e, k=k, jj=jj, tc=tc, buf=buf, g=p: e.matmul(
                                    P[f"P{g * 2 + jj}"][:, 0:512], buf[:, k * 128:(k + 1) * 128],
                                    xbsl(k, tc), start=(k == 0), stop=(k == 7)))
                        e1 = pipe.stage(f"R{seq}e{half}_{ex}E{fbk}",
                                        deps=[mm] + ([pe2[fbk - 1]] if fbk >= 1 else []))
                        for jj in range(2):
                            e1.op("scalar", lambda e, jj=jj, fbk=fbk, g=p: e.activation(
                                ht[:, jj * 512:(jj + 1) * 512],
                                P[f"P{g * 2 + jj}"][:, 0:512], AF.Relu,
                                bias=vb1e[:, fbk:fbk + 1]))
                        e2 = pipe.stage(f"R{seq}e{half}_{ex}F{fbk}")
                        for jj, HB in ((0, HB0), (1, HB1)):
                            e2.op("vector", lambda e, jj=jj, fbk=fbk, HB=HB: e.tensor_tensor(
                                HB[:, fbk * 512:(fbk + 1) * 512],
                                ht[:, jj * 512:(jj + 1) * 512],
                                mb2[:, jj * 512:(jj + 1) * 512], ALU.mult))
                        pmm[fbk] = mm
                        pe2[fbk] = e2
                    pmm = {}
                    pld = {}
                    pav = {}
                    for fbk in range(8):
                        p = fbk % 2
                        buf = WS0 if p == 0 else WS1
                        ldd = (None if fbk == 0 else
                               [pld[0]] if fbk == 1 else [pmm[fbk - 2]])
                        ld = pipe.stage(f"R{seq}e{half}_{ex}W{fbk}", deps=ldd)
                        cx.load_wblock(ld, f"l1f_w2_{ex}", fbk, buf)
                        pld[fbk] = ld
                        mmd = None if fbk == 0 else [ld] + \
                            ([pav[fbk - 2]] if fbk >= 2 else [])
                        mm = pipe.stage(f"R{seq}e{half}_{ex}N{fbk}", deps=mmd)
                        for k in range(32):
                            for jj, HB in ((0, HB0), (1, HB1)):
                                mm.op("tensor", lambda e, k=k, jj=jj, HB=HB, buf=buf, g=p: e.matmul(
                                    P[f"P{g * 2 + jj}"][:, 0:512], buf[:, k * 128:(k + 1) * 128],
                                    HB[:, k * 512:(k + 1) * 512],
                                    start=(k == 0), stop=(k == 31)))
                        av = pipe.stage(f"R{seq}e{half}_{ex}A{fbk}", deps=[mm])
                        for jj in range(2):
                            av.op("vector", lambda e, jj=jj, fbk=fbk, g=p: e.tensor_tensor(
                                yacc[:, fbk * 1024 + jj * 512: fbk * 1024 + (jj + 1) * 512],
                                yacc[:, fbk * 1024 + jj * 512: fbk * 1024 + (jj + 1) * 512],
                                P[f"P{g * 2 + jj}"][:, 0:512], ALU.add))
                        pmm[fbk] = mm
                        pav[fbk] = av
                for jj in range(2):
                    tc = half * 2 + jj
                    emit_xf_load(cx, seq, tc, shift=False, name=f"R{seq}res{half}_{jj}",
                                 xsrc="xst")
                    xfld = pipe.stages[-1]
                    pEw = {}
                    pDw = {}
                    for fbk in range(8):
                        scX = scB if fbk % 2 == 0 else scA
                        Ew = pipe.stage(f"R{seq}z{half}_{jj}_{fbk}",
                                        deps=[xfld] +
                                             ([pDw[fbk - 2]] if fbk >= 2 else []))
                        Ew.op("vector", lambda e, fbk=fbk, jj=jj, scX=scX: e.tensor_tensor(
                            scX[:, 0:512],
                            yacc[:, fbk * 1024 + jj * 512: fbk * 1024 + (jj + 1) * 512],
                            cx.xf(fbk), ALU.add))
                        Dw = pipe.stage(f"R{seq}zz{half}_{jj}_{fbk}")
                        Dw.dma("sync", lambda e, fbk=fbk, tc=tc, scX=scX: e.dma_start(
                            out=cx.x_ap(seq, fbk, tc * 512, 512), in_=scX[:, 0:512]))
                        pEw[fbk] = Ew
                        pDw[fbk] = Dw



_CACHE = {}


def _prep_cache(inputs):
    key = id(inputs.get("L0_attn_Wqkv"))
    if key not in _CACHE:
        _CACHE.clear()
        _CACHE[key] = host_prep(inputs)
    return _CACHE[key]


def make_in_map(inputs, core):
    w, wsrc, vec, consts = _prep_cache(inputs)
    x = np.asarray(inputs["L0_attn_x"], np.float32)
    xin = np.stack([x_to_fm(x[core * 2 + s]) for s in range(2)])
    m = {"wsrc": wsrc, "xin": xin,
         "ropeI": consts["tabs_ropeI"], "xpos": consts["tabs_xpos"],
         "cstf": consts["cstf"], "cstb": consts["cstb"], "cstk": consts["cstk"]}
    for k, v in vec.items():
        m[k.replace("gqk_col", "gqk_col")] = v
    return m


VEC_DRAM_NAMES = ["bqkv", "bout", "gqk_col", "gout_row", "l0f_qs", "l0f_ks",
                  "l0f_g", "l0f_b", "l0f_b1", "l0f_b2", "l1a_conv", "l1a_db",
                  "l1a_alog", "l1f_wr", "l1f_br", "l1f_b1", "l1f_b2"]


def kernel(**inputs):
    w, wsrc, vec, consts = _prep_cache(inputs)
    parts = getattr(kernel, "_parts", PARTS_ALL)
    ncores = getattr(kernel, "_ncores", 8)
    nc = build_program(w.meta, wsrc.size, parts=parts)
    in_maps = [make_in_map(inputs, c) for c in range(ncores)]
    res = run_bass_kernel_spmd(nc, in_maps, core_ids=list(range(ncores)))
    outs = []
    for c in range(ncores):
        fm = np.asarray(res.results[c]["xout"], np.float32)
        for s in range(2):
            outs.append(fm[s].reshape(1024, T).T)
    return np.stack(outs).astype(np.float32)
